# version 04
# baseline (speedup 1.0000x reference)
from contextlib import ExitStack

import numpy as np
import concourse.bass as bass
import concourse.mybir as mybir
from concourse.bass_utils import run_bass_kernel_spmd

F32 = mybir.dt.float32
BF16 = mybir.dt.bfloat16
AF = mybir.ActivationFunctionType
ALU = mybir.AluOpType

NCORES = 8
B, L, D = 2, 4096, 4096
TT = B * L
TOK = TT // NCORES
DC = D // NCORES
N = 16
R = 160
DTR = 128
CH = 256
NCH = TT // CH
CHB = L // CH
LN_EPS = 1e-5
RMS_EPS = 1.1920929e-07
TAU = 0.1


class Sched:

    ENGS = ("sync", "gpsimd", "tensor", "scalar", "vector")

    def __init__(self):
        self.ops = {e: [] for e in self.ENGS}
        self.cnt = {}

    def sems(self, names):
        for n in names:
            self.cnt[n] = 0

    def wait(self, eng, name, val):
        if val > 0:
            self.ops[eng].append(("w", name, val))

    def run(self, eng, fn, inc=None, k=1):
        self.ops[eng].append(("r", fn, inc, k))
        if inc is not None:
            self.cnt[inc] += k
            return self.cnt[inc]
        return None

    def emit(self, block, sems):
        def mk(eng_name):
            def body(e):
                for op in self.ops[eng_name]:
                    if op[0] == "w":
                        e.wait_ge(sems[op[1]], op[2])
                    else:
                        inst = op[1](e)
                        if op[2] is not None:
                            inst.then_inc(sems[op[2]], op[3])
            return body
        block.sync(mk("sync"))
        block.gpsimd(mk("gpsimd"))
        block.tensor(mk("tensor"))
        block.scalar(mk("scalar"))
        block.vector(mk("vector"))


SEMNAMES = ["LD", "GD", "TS", "SS", "VS", "ST", "PS"]
CONSTS = [0.0, 1.0, LN_EPS, RMS_EPS]


def build_l1():
    nc = bass.Bass("TRN2", target_bir_lowering=False)
    xtok = nc.dram_tensor("xtok", [TOK, D], F32, kind="ExternalInput").ap()
    xtokT = nc.dram_tensor("xtokT", [D, TOK], F32, kind="ExternalInput").ap()
    xpwT = nc.dram_tensor("xpwT", [D, R], F32, kind="ExternalInput").ap()
    wcatT = nc.dram_tensor("wcatT", [D, 24], F32, kind="ExternalInput").ap()
    ones_in = nc.dram_tensor("ones_in", [1, 128], F32, kind="ExternalInput").ap()

    o_xp = nc.dram_tensor("xpraw", [R, TOK], F32, kind="ExternalOutput").ap()
    o_lx = nc.dram_tensor("lx", [24, TOK], F32, kind="ExternalOutput").ap()
    o_m = nc.dram_tensor("mcol", [128, 8], F32, kind="ExternalOutput").ap()
    o_r = nc.dram_tensor("rcol", [128, 8], F32, kind="ExternalOutput").ap()
    o_mr = nc.dram_tensor("mrcol", [128, 8], F32, kind="ExternalOutput").ap()
    o_ws = nc.dram_tensor("wsums", [1, R + 24], F32, kind="ExternalOutput").ap()

    NTT = TOK // 128
    ND = D // 128

    with ExitStack() as ctx:
        def sb(name, shape, dt=F32):
            return ctx.enter_context(nc.sbuf_tensor(name, shape, dt))

        def ps(name, shape):
            return ctx.enter_context(nc.psum_tensor(name, shape, F32))

        _xk = sb("xk", [128, 2 * D])
        _xb = sb("xb", [128, 4 * TOK])
        _wpb = sb("wpb", [128, ND * R])
        _wcb = sb("wcb", [128, ND * 24])
        onec = sb("onec", [128, 1])
        scr = sb("scr", [128, D], BF16)
        sumc = sb("sumc", [128, 8])
        sqc = sb("sqc", [128, 8])
        meanc = sb("meanc", [128, 8])
        varc = sb("varc", [128, 8])
        sdc = sb("sdc", [128, 8])
        rstdc = sb("rstdc", [128, 8])
        t8 = sb("t8", [128, 8])
        mrc = sb("mrc", [128, 8])
        xpev = sb("xpev", [128, TOK])
        xpev2 = sb("xpev2", [32, TOK])
        lxev = sb("lxev", [24, TOK])
        wsev = sb("wsev", [1, R + 24])
        csts = sb("csts", [128, 4])
        p1 = ps("p1", [128, TOK])
        p2 = ps("p2", [32, TOK])
        plx = ps("plx", [24, TOK])
        pws = ps("pws", [1, R])
        pwf = ps("pwf", [1, 24])
        sems = {n: ctx.enter_context(nc.semaphore(n)) for n in SEMNAMES}
        block = ctx.enter_context(nc.Block())

        S = Sched()
        S.sems(SEMNAMES)
        for j, v in enumerate(CONSTS):
            nc.const_aps.register(F32, v, csts[:, j:j + 1])
        for j, v in enumerate(CONSTS):
            S.run("vector", lambda e, j=j, v=v: e.memset(csts[:, j:j + 1], v), "VS")
        vs_consts = S.cnt["VS"]
        S.wait("scalar", "VS", vs_consts)

        xk = [_xk[:, (i % 2) * D:((i % 2) + 1) * D] for i in range(NTT)]
        xb = [_xb[:, (i % 4) * TOK:((i % 4) + 1) * TOK] for i in range(ND)]
        wpb = [_wpb[:, i * R:(i + 1) * R] for i in range(ND)]
        wcb = [_wcb[:, i * 24:(i + 1) * 24] for i in range(ND)]

        for i in range(ND):
            S.run("gpsimd", lambda e, i=i: e.dma_start(
                out=wpb[i], in_=xpwT[i * 128:(i + 1) * 128, :]), "GD", 16)
        for i in range(ND):
            S.run("gpsimd", lambda e, i=i: e.dma_start(
                out=wcb[i], in_=wcatT[i * 128:(i + 1) * 128, :]), "GD", 16)
        S.run("gpsimd", lambda e: e.dma_start(
            out=onec[:, :],
            in_=ones_in.rearrange("o (p q) -> (o p) q", q=1)), "GD", 16)
        gd_all = S.cnt["GD"]

        ld_tile = []
        ss_tile = [0] * NTT
        for i in range(2):
            S.run("sync", lambda e, i=i: e.dma_start(
                out=xk[i], in_=xtok[i * 128:(i + 1) * 128, :]), "LD", 16)
            ld_tile.append(S.cnt["LD"])
        for i in range(NTT):
            S.wait("scalar", "LD", ld_tile[i])
            S.run("scalar", lambda e, i=i: e.activation(
                out=scr[:, :], in_=xk[i], func=AF.Copy, bias=0.0,
                accum_out=sumc[:, i:i + 1]), "SS")
            S.wait("scalar", "SS", S.cnt["SS"])
            S.run("scalar", lambda e, i=i: e.activation(
                out=scr[:, :], in_=xk[i], func=AF.Square,
                accum_out=sqc[:, i:i + 1]), "SS")
            ss_tile[i] = S.cnt["SS"]
            S.wait("scalar", "SS", ss_tile[i])
            if i + 2 < NTT:
                S.wait("sync", "SS", ss_tile[i])
                S.run("sync", lambda e, i=i: e.dma_start(
                    out=xk[i + 2], in_=xtok[(i + 2) * 128:(i + 3) * 128, :]),
                    "LD", 16)
                ld_tile.append(S.cnt["LD"])
        ss_stats = S.cnt["SS"]

        S.wait("vector", "SS", ss_stats)
        S.run("vector", lambda e: e.tensor_scalar(
            out=meanc[:, :], in0=sumc[:, :], scalar1=1.0 / D, scalar2=None,
            op0=ALU.mult), "VS")
        S.wait("vector", "VS", S.cnt["VS"])
        S.run("vector", lambda e: e.tensor_scalar(
            out=varc[:, :], in0=sqc[:, :], scalar1=1.0 / D, scalar2=None,
            op0=ALU.mult), "VS")
        S.wait("vector", "VS", S.cnt["VS"])
        S.run("vector", lambda e: e.tensor_tensor(
            out=t8[:, :], in0=meanc[:, :], in1=meanc[:, :], op=ALU.mult), "VS")
        S.wait("vector", "VS", S.cnt["VS"])
        S.run("vector", lambda e: e.tensor_tensor(
            out=varc[:, :], in0=varc[:, :], in1=t8[:, :], op=ALU.subtract), "VS")
        vs_var = S.cnt["VS"]
        S.wait("scalar", "VS", vs_var)
        S.run("scalar", lambda e: e.activation(
            out=sdc[:, :], in_=varc[:, :], func=AF.Sqrt, bias=LN_EPS), "SS")
        ss_sd = S.cnt["SS"]
        S.wait("vector", "SS", ss_sd)
        S.run("vector", lambda e: e.reciprocal(out=rstdc[:, :], in_=sdc[:, :]), "VS")
        S.wait("vector", "VS", S.cnt["VS"])
        S.run("vector", lambda e: e.tensor_tensor(
            out=mrc[:, :], in0=meanc[:, :], in1=rstdc[:, :], op=ALU.mult), "VS")
        vs_rstd = S.cnt["VS"]

        S.wait("tensor", "GD", gd_all)
        ldx = [0] * ND
        ts_used = [0] * ND
        for i in range(4):
            S.run("sync", lambda e, i=i: e.dma_start(
                out=xb[i], in_=xtokT[i * 128:(i + 1) * 128, :]), "LD", 16)
            ldx[i] = S.cnt["LD"]
        for i in range(ND):
            S.wait("tensor", "LD", ldx[i])
            for h in range(2):
                lo, hi = h * 512, (h + 1) * 512
                S.run("tensor", lambda e, i=i, lo=lo, hi=hi: e.matmul(
                    p1[:, lo:hi], wpb[i][:, 0:128], xb[i][:, lo:hi],
                    start=(i == 0), stop=(i == ND - 1)), None)
                S.run("tensor", lambda e, i=i, lo=lo, hi=hi: e.matmul(
                    p2[:, lo:hi], wpb[i][:, 128:160], xb[i][:, lo:hi],
                    start=(i == 0), stop=(i == ND - 1)), None)
                S.run("tensor", lambda e, i=i, lo=lo, hi=hi: e.matmul(
                    plx[:, lo:hi], wcb[i], xb[i][:, lo:hi],
                    start=(i == 0), stop=(i == ND - 1)), None)
            S.run("tensor", lambda e, i=i: e.matmul(
                pws[:, :], onec[:, :], wpb[i],
                start=(i == 0), stop=(i == ND - 1)), None)
            S.run("tensor", lambda e, i=i: e.matmul(
                pwf[:, :], onec[:, :], wcb[i],
                start=(i == 0), stop=(i == ND - 1)), "TS")
            ts_used[i] = S.cnt["TS"]
            if i + 4 < ND:
                S.wait("sync", "TS", ts_used[i])
                S.run("sync", lambda e, i=i: e.dma_start(
                    out=xb[i + 4], in_=xtokT[(i + 4) * 128:(i + 5) * 128, :]),
                    "LD", 16)
                ldx[i + 4] = S.cnt["LD"]
        ts_all = S.cnt["TS"]

        S.wait("scalar", "TS", ts_all)
        for fn in (
            lambda e: e.activation(out=xpev[:, :], in_=p1[:, :], func=AF.Copy, bias=0.0),
            lambda e: e.activation(out=xpev2[:, :], in_=p2[:, :], func=AF.Copy, bias=0.0),
            lambda e: e.activation(out=lxev[:, :], in_=plx[:, :], func=AF.Copy, bias=0.0),
            lambda e: e.activation(out=wsev[:, 0:R], in_=pws[:, :], func=AF.Copy, bias=0.0),
            lambda e: e.activation(out=wsev[:, R:R + 24], in_=pwf[:, :], func=AF.Copy, bias=0.0),
        ):
            S.run("scalar", fn, "SS")
        ss_evac = S.cnt["SS"]

        S.wait("sync", "SS", ss_evac)
        S.wait("sync", "VS", vs_rstd)
        S.run("sync", lambda e: e.dma_start(out=o_xp[0:128, :], in_=xpev[:, :]), "ST", 16)
        S.run("sync", lambda e: e.dma_start(out=o_xp[128:160, :], in_=xpev2[:, :]), "ST", 16)
        S.run("sync", lambda e: e.dma_start(out=o_lx, in_=lxev[:, :]), "ST", 16)
        S.run("sync", lambda e: e.dma_start(out=o_m, in_=meanc[:, :]), "ST", 16)
        S.run("sync", lambda e: e.dma_start(out=o_r, in_=rstdc[:, :]), "ST", 16)
        S.run("sync", lambda e: e.dma_start(out=o_mr, in_=mrc[:, :]), "ST", 16)
        S.run("sync", lambda e: e.dma_start(out=o_ws, in_=wsev[:, :]), "ST", 16)
        S.wait("sync", "ST", S.cnt["ST"])

        S.emit(block, sems)
    return nc


_L1 = None


def _get_l1():
    global _L1
    if _L1 is None:
        _L1 = build_l1()
    return _L1


def run_l1(x2d, x_proj_w, wcat):
    nc = _get_l1()
    xpwT = np.ascontiguousarray(x_proj_w.T)
    wcatT = np.ascontiguousarray(wcat.T)
    ones = np.ones((1, 128), np.float32)
    in_maps = []
    for c in range(NCORES):
        xt = np.ascontiguousarray(x2d[c * TOK:(c + 1) * TOK])
        in_maps.append(dict(
            xtok=xt, xtokT=np.ascontiguousarray(xt.T),
            xpwT=xpwT, wcatT=wcatT, ones_in=ones))
    res = run_bass_kernel_spmd(nc, in_maps, core_ids=list(range(NCORES)))
    return res.results




def build_l2():
    nc = bass.Bass("TRN2", target_bir_lowering=False)
    xdT = nc.dram_tensor("xdT", [DC, TT], F32, kind="ExternalInput").ap()
    xp = nc.dram_tensor("xp", [R, TT], F32, kind="ExternalInput").ap()
    st1 = nc.dram_tensor("st1", [3, TT], F32, kind="ExternalInput").ap()
    alog = nc.dram_tensor("alog", [128, 4 * N], F32, kind="ExternalInput").ap()
    dcol = nc.dram_tensor("dcol", [128, 4], F32, kind="ExternalInput").ap()
    dtwT = nc.dram_tensor("dtwT", [128, DC], F32, kind="ExternalInput").ap()
    dtbc = nc.dram_tensor("dtbc", [128, 4], F32, kind="ExternalInput").ap()
    wctT = nc.dram_tensor("wctT", [128, 4 * 24], F32, kind="ExternalInput").ap()
    wsc = nc.dram_tensor("wsc", [128, 1], F32, kind="ExternalInput").ap()
    wsb = nc.dram_tensor("wsb", [32, 1], F32, kind="ExternalInput").ap()

    o_y2 = nc.dram_tensor("y2", [DC, TT], F32, kind="ExternalOutput").ap()
    o_lnp = nc.dram_tensor("lnp", [2, TT], F32, kind="ExternalOutput").ap()
    o_ly = nc.dram_tensor("ly", [24, TT], F32, kind="ExternalOutput").ap()

    bcd = [nc.dram_tensor(f"bcd{i}", [32, CH], F32, kind="Internal").ap()
           for i in range(2)]

    with ExitStack() as ctx:
        def sb(name, shape, dt=F32):
            return ctx.enter_context(nc.sbuf_tensor(name, shape, dt))

        def ps(name, shape):
            return ctx.enter_context(nc.psum_tensor(name, shape, F32))

        _xd = sb("xd", [128, 2 * 4 * CH])
        _xln = sb("xln", [128, 4 * CH])
        _dt4 = sb("dt4", [128, 4 * CH])
        _w4 = sb("w4", [128, 4 * CH])
        _dti = sb("dti", [128, 2 * CH])
        _dta = sb("dta", [128, 2 * CH])
        _bc = sb("bc", [32, 2 * CH])
        _bca = sb("bca", [32, 2 * CH])
        _mrep = sb("mrep", [128, 2 * CH])
        _rrep = sb("rrep", [128, 2 * CH])
        _mrrep = sb("mrrep", [128, 2 * CH])
        _ball = sb("ball", [128, N * CH])
        _call = sb("call", [128, N * CH])
        _u1 = sb("u1", [128, 2 * N * CH])
        _u2 = sb("u2", [128, 2 * N * CH])
        _u3 = sb("u3", [128, 2 * N * CH])
        _y2t = sb("y2t", [128, 4 * CH])
        ysq = sb("ysq", [128, CH])
        sptmp = sb("sptmp", [128, CH])
        carry = sb("carry", [128, 4 * N])
        inj = sb("inj", [128, N])
        t256a = sb("t256a", [128, CH])
        t256b = sb("t256b", [32, CH])
        alog_s = sb("alog_s", [128, 4 * N])
        aneg = sb("aneg", [128, 4 * N])
        dcol_s = sb("dcol_s", [128, 4])
        dtw_s = sb("dtw_s", [128, DC])
        dtb_s = sb("dtb_s", [128, 4])
        wct_s = sb("wct_s", [128, 4 * 24])
        wsc_s = sb("wsc_s", [128, 1])
        wsb_s = sb("wsb_s", [32, 1])
        nwsc = sb("nwsc", [128, 1])
        nwsb = sb("nwsb", [32, 1])
        lyev = sb("lyev", [24, 2 * CH])
        lnev = sb("lnev", [1, 4 * CH])
        csts = sb("csts", [128, 4])
        pdtz = [ps(f"pdtz{i}", [128, CH]) for i in range(2)]
        pln1 = ps("pln1", [1, CH])
        pln2 = ps("pln2", [1, CH])
        ply = ps("ply", [24, CH])
        sems = {n: ctx.enter_context(nc.semaphore(n)) for n in SEMNAMES}
        block = ctx.enter_context(nc.Block())

        S = Sched()
        S.sems(SEMNAMES)
        for j, v in enumerate(CONSTS):
            nc.const_aps.register(F32, v, csts[:, j:j + 1])
        for j, v in enumerate(CONSTS):
            S.run("vector", lambda e, j=j, v=v: e.memset(csts[:, j:j + 1], v), "VS")
        vs_consts = S.cnt["VS"]
        S.wait("scalar", "VS", vs_consts)
        S.wait("tensor", "VS", vs_consts)

        xd = [[_xd[:, (p * 4 + t) * CH:(p * 4 + t + 1) * CH] for t in range(4)]
              for p in range(2)]
        xln = [_xln[:, t * CH:(t + 1) * CH] for t in range(4)]
        dt4 = [_dt4[:, t * CH:(t + 1) * CH] for t in range(4)]
        w4 = [_w4[:, t * CH:(t + 1) * CH] for t in range(4)]
        dti = [_dti[:, p * CH:(p + 1) * CH] for p in range(2)]
        dta = [_dta[:, p * CH:(p + 1) * CH] for p in range(2)]
        bc = [_bc[:, p * CH:(p + 1) * CH] for p in range(2)]
        bca = [_bca[:, p * CH:(p + 1) * CH] for p in range(2)]
        mrep = [_mrep[:, p * CH:(p + 1) * CH] for p in range(2)]
        rrep = [_rrep[:, p * CH:(p + 1) * CH] for p in range(2)]
        mrrep = [_mrrep[:, p * CH:(p + 1) * CH] for p in range(2)]
        ball = [_ball[:, :], _ball[:, :]]
        call = [_call[:, :], _call[:, :]]
        u1 = [_u1[:, p * N * CH:(p + 1) * N * CH] for p in range(2)]
        u2b = [_u2[:, p * N * CH:(p + 1) * N * CH] for p in range(2)]
        u3b = [_u3[:, p * N * CH:(p + 1) * N * CH] for p in range(2)]
        y2t = [_y2t[:, t * CH:(t + 1) * CH] for t in range(4)]
        lyv = [lyev[:, p * CH:(p + 1) * CH] for p in range(2)]
        lnv = [[lnev[:, (p * 2 + j) * CH:(p * 2 + j + 1) * CH] for j in range(2)]
               for p in range(2)]

        for fn in (
            lambda e: e.dma_start(out=alog_s[:, :], in_=alog),
            lambda e: e.dma_start(out=dcol_s[:, :], in_=dcol),
            lambda e: e.dma_start(out=dtw_s[:, :], in_=dtwT),
            lambda e: e.dma_start(out=dtb_s[:, :], in_=dtbc),
            lambda e: e.dma_start(out=wct_s[:, :], in_=wctT),
            lambda e: e.dma_start(out=wsc_s[:, :], in_=wsc),
            lambda e: e.dma_start(out=wsb_s[:, :], in_=wsb),
        ):
            S.run("sync", fn, "LD", 16)
        ld_once = S.cnt["LD"]
        S.wait("scalar", "LD", ld_once)
        S.run("scalar", lambda e: e.activation(
            out=aneg[:, :], in_=alog_s[:, :], func=AF.Exp), "SS")
        ss_aexp = S.cnt["SS"]
        S.wait("vector", "SS", ss_aexp)
        S.run("vector", lambda e: e.tensor_scalar(
            out=aneg[:, :], in0=aneg[:, :], scalar1=-1.0, scalar2=None,
            op0=ALU.mult), "VS")
        S.wait("vector", "LD", ld_once)
        S.run("vector", lambda e: e.tensor_scalar(
            out=nwsc[:, :], in0=wsc_s[:, :], scalar1=-1.0, scalar2=None,
            op0=ALU.mult), "VS")
        S.run("vector", lambda e: e.tensor_scalar(
            out=nwsb[:, :], in0=wsb_s[:, :], scalar1=-1.0, scalar2=None,
            op0=ALU.mult), "VS")
        vs_pre = S.cnt["VS"]
        S.wait("scalar", "VS", vs_pre)
        S.wait("tensor", "LD", ld_once)

        cons_vs = [0, 0]
        cons_ball = [0, 0]
        cons_u1 = [0, 0]
        ss_dtz_sp = [0, 0]
        ts_sq_use = 0
        vs_evac_prev = 0
        st_y2 = [0] * 4
        st_misc = 0
        pool_self = 0

        ld_chunk_p = [0, 0]
        ball_ps_gate = 0
        ball_vs_gate = 0

        def emit_loads(ci2):
            par2 = ci2 % 2
            c02 = ci2 * CH
            cols2 = slice(c02, c02 + CH)
            S.wait("sync", "VS", cons_vs[par2])
            for t in range(4):
                S.run("sync", lambda e, t=t, par2=par2, cols2=cols2: e.dma_start(
                    out=xd[par2][t], in_=xdT[t * 128:(t + 1) * 128, cols2]), "LD", 16)
            S.run("sync", lambda e, par2=par2, cols2=cols2: e.dma_start(
                out=dti[par2], in_=xp[0:128, cols2]), "LD", 16)
            S.run("sync", lambda e, par2=par2, cols2=cols2: e.dma_start(
                out=bc[par2], in_=xp[128:160, cols2]), "LD", 16)
            S.run("sync", lambda e, par2=par2, cols2=cols2: e.dma_start(
                out=mrep[par2][:, :], in_=st1[0:1, cols2].broadcast_to([128, CH])),
                "LD", 16)
            S.run("sync", lambda e, par2=par2, cols2=cols2: e.dma_start(
                out=rrep[par2][:, :], in_=st1[1:2, cols2].broadcast_to([128, CH])),
                "LD", 16)
            S.run("sync", lambda e, par2=par2, cols2=cols2: e.dma_start(
                out=mrrep[par2][:, :], in_=st1[2:3, cols2].broadcast_to([128, CH])),
                "LD", 16)
            ld_chunk_p[par2] = S.cnt["LD"]

        emit_loads(0)
        for ci in range(NCH):
            par = ci % 2
            cb = ci % CHB
            c0 = ci * CH
            cols = slice(c0, c0 + CH)
            ld_chunk = ld_chunk_p[par]
            S.wait("vector", "LD", ld_chunk)
            S.run("vector", lambda e, par=par: e.tensor_tensor(
                out=t256a[:, :], in0=dti[par], in1=rrep[par], op=ALU.mult), "VS")
            S.wait("vector", "VS", S.cnt["VS"])
            S.run("vector", lambda e, par=par: e.scalar_tensor_tensor(
                out=dta[par], in0=mrrep[par], scalar=nwsc[:, 0:1], in1=t256a[:, :],
                op0=ALU.mult, op1=ALU.add), "VS")
            vs_dta = S.cnt["VS"]
            S.run("vector", lambda e, par=par: e.tensor_tensor(
                out=t256b[:, :], in0=bc[par], in1=rrep[par][0:32, :], op=ALU.mult), "VS")
            S.wait("vector", "VS", S.cnt["VS"])
            S.run("vector", lambda e, par=par: e.scalar_tensor_tensor(
                out=bca[par], in0=mrrep[par][0:32, :], scalar=nwsb[:, 0:1],
                in1=t256b[:, :], op0=ALU.mult, op1=ALU.add), "VS")
            vs_bca = S.cnt["VS"]
            vs_xln = [0] * 4
            for t in range(4):
                S.run("vector", lambda e, t=t, par=par: e.tensor_tensor(
                    out=xln[t], in0=xd[par][t], in1=rrep[par], op=ALU.mult), "VS")
                S.wait("vector", "VS", S.cnt["VS"])
                S.run("vector", lambda e, t=t, par=par: e.tensor_tensor(
                    out=xln[t], in0=xln[t], in1=mrrep[par], op=ALU.subtract), "VS")
                vs_xln[t] = S.cnt["VS"]

            S.wait("sync", "VS", vs_bca)
            S.run("sync", lambda e, par=par: e.dma_start(
                out=bcd[par], in_=bca[par]), "LD", 16)
            S.wait("sync", "LD", S.cnt["LD"])
            S.wait("sync", "PS", ball_ps_gate)
            S.wait("sync", "VS", ball_vs_gate)
            S.run("sync", lambda e, par=par: e.dma_start(
                out=ball[par].rearrange("p (n t) -> p n t", n=N),
                in_=bcd[par][0:N, :].rearrange("n (o t) -> o n t", o=1)
                    .broadcast_to([128, N, CH])), "LD", 16)
            S.run("sync", lambda e, par=par: e.dma_start(
                out=call[par].rearrange("p (n t) -> p n t", n=N),
                in_=bcd[par][N:2 * N, :].rearrange("n (o t) -> o n t", o=1)
                    .broadcast_to([128, N, CH])), "LD", 16)
            ld_bcast = S.cnt["LD"]

            vs_tts = [0] * 4
            vs_red = [0] * 4
            z_done = [0] * 4
            carry_done = [0] * 4
            ss_dA = [0] * 4
            for t in range(4):
                pz = pdtz[t % 2]
                u2 = u2b[t % 2]
                u3 = u3b[t % 2]
                S.wait("tensor", "SS", ss_dtz_sp[t % 2])
                S.wait("tensor", "VS", vs_dta)
                S.run("tensor", lambda e, t=t, pz=pz, par=par: e.matmul(
                    pz[:, :], dtw_s[:, t * 128:(t + 1) * 128], dta[par],
                    start=True, stop=True), "TS")
                ts_dtz = S.cnt["TS"]
                S.wait("scalar", "TS", ts_dtz)
                S.run("scalar", lambda e, t=t, pz=pz: e.activation(
                    out=sptmp[:, :], in_=pz[:, :], func=AF.Exp,
                    bias=dtb_s[:, t:t + 1]), "SS")
                S.wait("scalar", "SS", S.cnt["SS"])
                S.run("scalar", lambda e, t=t: e.activation(
                    out=dt4[t], in_=sptmp[:, :], func=AF.Ln, bias=1.0), "SS")
                ss_dtz_sp[t % 2] = S.cnt["SS"]
                ss_sp = S.cnt["SS"]
                S.wait("vector", "SS", ss_sp)
                S.wait("vector", "VS", vs_xln[t])
                S.run("vector", lambda e, t=t: e.tensor_tensor(
                    out=w4[t], in0=dt4[t], in1=xln[t], op=ALU.mult), "VS")
                vs_w = S.cnt["VS"]

                up = u1[t % 2]
                S.wait("scalar", "VS", cons_u1[t % 2])
                for n in range(N):
                    S.run("scalar", lambda e, t=t, n=n, up=up: e.activation(
                        out=up[:, n * CH:(n + 1) * CH], in_=dt4[t], func=AF.Exp,
                        scale=aneg[:, t * N + n:t * N + n + 1]), "SS")
                ss_dA[t] = S.cnt["SS"]

                uv = up.rearrange("p (n t) -> p n t", n=N)
                u2v = u2.rearrange("p (n t) -> p n t", n=N)
                S.wait("gpsimd", "VS", vs_red[t - 2] if t >= 2 else
                       (vs_red_last2[t] if ci > 0 else 0))
                S.wait("gpsimd", "LD", ld_bcast)
                S.wait("gpsimd", "VS", vs_w)
                S.run("gpsimd", lambda e, t=t, par=par, u2v=u2v: e.tensor_tensor(
                    out=u2v,
                    in0=w4[t].rearrange("p (o t) -> p o t", o=1)
                        .broadcast_to([128, N, CH]),
                    in1=ball[par].rearrange("p (n t) -> p n t", n=N),
                    op=ALU.mult), "PS")
                S.wait("gpsimd", "PS", S.cnt["PS"])
                S.wait("gpsimd", "SS", ss_dA[t])
                if cb > 0:
                    S.run("gpsimd", lambda e, t=t, uv=uv: e.tensor_tensor(
                        out=inj[:, :], in0=uv[:, :, 0],
                        in1=carry[:, t * N:(t + 1) * N], op=ALU.mult), "PS")
                    S.wait("gpsimd", "PS", S.cnt["PS"])
                S.run("gpsimd", lambda e, uv=uv: e.memset(uv[:, :, 0:1], 0.0), "PS")
                S.wait("gpsimd", "PS", S.cnt["PS"])
                if cb > 0:
                    S.run("gpsimd", lambda e, u2v=u2v: e.tensor_tensor(
                        out=u2v[:, :, 0], in0=u2v[:, :, 0],
                        in1=inj[:, :], op=ALU.add), "PS")
                    S.wait("gpsimd", "PS", S.cnt["PS"])
                ps_dbx = S.cnt["PS"]
                if t >= 1:
                    tc = t - 1
                    S.wait("gpsimd", "VS", vs_tts[tc])
                    S.run("gpsimd", lambda e, tc=tc: e.tensor_copy(
                        out=carry[:, tc * N:(tc + 1) * N],
                        in_=u3b[tc % 2].rearrange("p (n t) -> p n t", n=N)[:, :, CH - 1]),
                        "PS")
                    carry_done[tc] = S.cnt["PS"]
                    S.wait("gpsimd", "PS", S.cnt["PS"])

                S.wait("vector", "SS", ss_dA[t])
                S.wait("vector", "PS", ps_dbx)
                if t >= 2:
                    S.wait("vector", "VS", z_done[t - 2])
                    S.wait("vector", "PS", carry_done[t - 2])
                elif ci > 0:
                    S.wait("vector", "VS", z_last2[t])
                    S.wait("vector", "PS", carry_last2[t])
                S.run("vector", lambda e, up=up, u2=u2, u3=u3: e.tensor_tensor_scan(
                    out=u3[:, :], data0=up[:, :], data1=u2[:, :], initial=0.0,
                    op0=ALU.mult, op1=ALU.add), "VS")
                vs_tts[t] = S.cnt["VS"]
                cons_u1[t % 2] = vs_tts[t]
                S.wait("vector", "VS", vs_tts[t])
                S.run("vector", lambda e, t=t, par=par, u2=u2, u3=u3:
                      e.tensor_tensor(
                          out=u2.rearrange("p (c n) -> p n c", n=N),
                          in0=u3.rearrange("p (n c) -> p n c", n=N),
                          in1=call[par].rearrange("p (n c) -> p n c", n=N),
                          op=ALU.mult), "VS")
                z_done[t] = S.cnt["VS"]
                S.wait("vector", "VS", z_done[t])
                S.wait("vector", "LD", st_y2[t])
                S.run("vector", lambda e, t=t: e.tensor_reduce(
                    out=y2t[t], in_=u2b[t % 2].rearrange("p (c n) -> p c n", n=N),
                    axis=mybir.AxisListType.X, op=ALU.add), "VS")
                S.wait("vector", "VS", S.cnt["VS"])
                S.run("vector", lambda e, t=t: e.scalar_tensor_tensor(
                    out=y2t[t], in0=xln[t], scalar=dcol_s[:, t:t + 1],
                    in1=y2t[t], op0=ALU.mult, op1=ALU.add), "VS")
                vs_red[t] = S.cnt["VS"]
                S.wait("vector", "VS", vs_red[t])
            S.wait("gpsimd", "VS", vs_tts[3])
            S.run("gpsimd", lambda e: e.tensor_copy(
                out=carry[:, 3 * N:4 * N],
                in_=u3b[1].rearrange("p (n t) -> p n t", n=N)[:, :, CH - 1]), "PS")
            carry_done[3] = S.cnt["PS"]
            S.wait("gpsimd", "PS", S.cnt["PS"])
            vs_red_last2 = [vs_red[2], vs_red[3], vs_red[2], vs_red[3]]
            z_last2 = [z_done[2], z_done[3], z_done[2], z_done[3]]
            carry_last2 = [carry_done[2], carry_done[3], carry_done[2], carry_done[3]]
            cons_ball_ps = S.cnt["PS"]
            cons_ball_vs = S.cnt["VS"]

            cons_vs[par] = S.cnt["VS"]
            ball_ps_gate = cons_ball_ps
            ball_vs_gate = cons_ball_vs
            if ci + 1 < NCH:
                emit_loads(ci + 1)

            S.wait("tensor", "VS", vs_evac_prev)
            for t in range(4):
                S.wait("scalar", "VS", vs_red[t])
                S.wait("scalar", "TS", ts_sq_use)
                S.run("scalar", lambda e, t=t: e.activation(
                    out=ysq[:, :], in_=y2t[t], func=AF.Square), "SS")
                ss_sq = S.cnt["SS"]
                S.wait("tensor", "VS", vs_red[t])
                S.run("tensor", lambda e, t=t: e.matmul(
                    pln1[:, :], csts[:, 1:2], y2t[t],
                    start=(t == 0), stop=(t == 3)), "TS" if t == 3 else None)
                S.wait("tensor", "SS", ss_sq)
                S.run("tensor", lambda e, t=t: e.matmul(
                    pln2[:, :], csts[:, 1:2], ysq[:, :],
                    start=(t == 0), stop=(t == 3)), "TS")
                ts_sq_use = S.cnt["TS"]
                S.run("tensor", lambda e, t=t: e.matmul(
                    ply[:, :], wct_s[:, t * 24:(t + 1) * 24], y2t[t],
                    start=(t == 0), stop=(t == 3)), "TS" if t == 3 else None)
            ts_stats = S.cnt["TS"]
            S.wait("vector", "TS", ts_stats)
            S.wait("vector", "LD", st_misc)
            S.run("vector", lambda e, par=par: e.tensor_copy(
                out=lnv[par][0], in_=pln1[:, :]), "VS")
            S.run("vector", lambda e, par=par: e.tensor_copy(
                out=lnv[par][1], in_=pln2[:, :]), "VS")
            S.run("vector", lambda e, par=par: e.tensor_copy(
                out=lyv[par], in_=ply[:, :]), "VS")
            vs_evac = S.cnt["VS"]
            vs_evac_prev = vs_evac
            for t in range(4):
                S.wait("sync", "VS", vs_red[t])
                S.run("sync", lambda e, t=t, cols=cols: e.dma_start(
                    out=o_y2[t * 128:(t + 1) * 128, cols], in_=y2t[t]), "LD", 16)
                st_y2[t] = S.cnt["LD"]
            S.wait("sync", "VS", vs_evac)
            S.run("sync", lambda e, par=par, cols=cols: e.dma_start(
                out=o_lnp[0:1, cols], in_=lnv[par][0]), "LD", 16)
            S.run("sync", lambda e, par=par, cols=cols: e.dma_start(
                out=o_lnp[1:2, cols], in_=lnv[par][1]), "LD", 16)
            S.run("sync", lambda e, par=par, cols=cols: e.dma_start(
                out=o_ly[:, cols], in_=lyv[par]), "LD", 16)
            st_misc = S.cnt["LD"]

        S.wait("sync", "LD", S.cnt["LD"])
        S.wait("sync", "GD", S.cnt["GD"])

        S.emit(block, sems)
    return nc


_L2 = None


def _get_l2():
    global _L2
    if _L2 is None:
        _L2 = build_l2()
    return _L2


def run_l2(x2d, xp_full, st1, A_log, Dv, dt_proj_w, dt_proj_b, wcat, wsum):
    nc = _get_l2()
    in_maps = []
    for c in range(NCORES):
        sl = slice(c * DC, (c + 1) * DC)
        in_maps.append(dict(
            xdT=np.ascontiguousarray(x2d[:, sl].T),
            xp=xp_full, st1=st1,
            alog=np.ascontiguousarray(
                A_log[sl].reshape(4, 128, N).transpose(1, 0, 2).reshape(128, 4 * N)),
            dcol=np.ascontiguousarray(Dv[sl].reshape(4, 128).T),
            dtwT=np.ascontiguousarray(dt_proj_w[sl].T),
            dtbc=np.ascontiguousarray(dt_proj_b[sl].reshape(4, 128).T),
            wctT=np.ascontiguousarray(
                wcat[:, sl].T.reshape(4, 128, 24).transpose(1, 0, 2).reshape(128, 96)),
            wsc=np.ascontiguousarray(wsum[0:128].reshape(128, 1)),
            wsb=np.ascontiguousarray(wsum[128:160].reshape(32, 1)),
        ))
    res = run_bass_kernel_spmd(nc, in_maps, core_ids=list(range(NCORES)))
    return res.results


def build_l3():
    nc = bass.Bass("TRN2", target_bir_lowering=False)
    xtok = nc.dram_tensor("xtok", [TOK, D], F32, kind="ExternalInput").ap()
    y2tok = nc.dram_tensor("y2tok", [TOK, D], F32, kind="ExternalInput").ap()
    lnps = nc.dram_tensor("lnps", [128, 16 * 8], F32, kind="ExternalInput").ap()
    lyt = nc.dram_tensor("lyt", [128, 192 * 8], F32, kind="ExternalInput").ap()
    lxt = nc.dram_tensor("lxt", [128, 192], F32, kind="ExternalInput").ap()
    wsumf = nc.dram_tensor("wsumf", [1, 24], F32, kind="ExternalInput").ap()
    hrb = nc.dram_tensor("hrb", [1, 16], F32, kind="ExternalInput").ap()
    hpb = nc.dram_tensor("hpb", [1, 4], F32, kind="ExternalInput").ap()
    hob = nc.dram_tensor("hob", [1, 4], F32, kind="ExternalInput").ap()
    gatei = nc.dram_tensor("gatei", [1, 1], F32, kind="ExternalInput").ap()
    eye16 = nc.dram_tensor("eye16", [1, 16], F32, kind="ExternalInput").ap()
    ident = nc.dram_tensor("ident", [128, 128], F32, kind="ExternalInput").ap()
    wb = nc.dram_tensor("wb", [1024, 1024], F32, kind="ExternalInput").ap()

    o_x3 = nc.dram_tensor("x3", [TOK, D], F32, kind="ExternalOutput").ap()
    o_dbg = nc.dram_tensor("dbg", [128, 192 + 128 + 32 + 32 + 48], F32,
                           kind="ExternalOutput").ap()

    NTT = TOK // 128

    with ExitStack() as ctx:
        def sb(name, shape, dt=F32):
            return ctx.enter_context(nc.sbuf_tensor(name, shape, dt))

        _xt = sb("xt", [128, 2 * D])
        _yt = sb("yt", [128, 2 * D])
        x2t = sb("x2t", [128, D])
        xint = sb("xint", [128, D])
        xmix = sb("xmix", [128, D])
        xpre32 = sb("xpre32", [128, 1024])
        xpreb = sb("xpreb", [128, 1024], BF16)
        xpT = sb("xpT", [128, 1024], BF16)
        yb = sb("yb", [128, 1024])
        _wbb = sb("wbb", [128, 8 * 1024], BF16)
        idn = sb("idn", [128, 128], BF16)
        lnps_s = sb("lnps_s", [128, 16 * 8])
        sc = sb("sc", [128, 16])
        lyt_s = sb("lyt_s", [128, 192 * 8])
        lyc = sb("lyc", [128, 192])
        lxt_s = sb("lxt_s", [128, 192])
        lgt = sb("lgt", [128, 192])
        zt = sb("zt", [128, 128])
        at = sb("at", [128, 128])
        pe_ = sb("pe_", [128, 128])
        ht = sb("ht", [128, 128])
        spre = sb("spre", [128, 32])
        spost = sb("spost", [128, 32])
        hpre = sb("hpre", [128, 32])
        hpost = sb("hpost", [128, 32])
        ev = sb("ev", [128, 32])
        eu = sb("eu", [128, 32])
        rs = sb("rs", [128, 32])
        zm = sb("zm", [128, 8])
        mu2c = sb("mu2c", [128, 8])
        r2c = sb("r2c", [128, 8])
        nmu2r2c = sb("nmu2r2c", [128, 8])
        spc = sb("spc", [128, 8])
        sqc = sb("sqc", [128, 8])
        mupc = sb("mupc", [128, 8])
        nmupc = sb("nmupc", [128, 8])
        varpc = sb("varpc", [128, 8])
        rpc = sb("rpc", [128, 8])
        alphac = sb("alphac", [128, 8])
        alphatc = sb("alphatc", [128, 8])
        t8a = sb("t8a", [128, 8])
        t8b = sb("t8b", [128, 8])
        s3c = sb("s3c", [128, 1])
        q3c = sb("q3c", [128, 1])
        mu3c = sb("mu3c", [128, 1])
        r3c = sb("r3c", [128, 1])
        nmu3r3c = sb("nmu3r3c", [128, 1])
        t1c = sb("t1c", [128, 1])
        scrb = sb("scrb", [128, D], BF16)
        wsfrep = sb("wsfrep", [128, 24])
        hrbrep = sb("hrbrep", [128, 16])
        hpbrep = sb("hpbrep", [128, 4])
        hobrep = sb("hobrep", [128, 4])
        eyerep = sb("eyerep", [128, 16])
        igrep = sb("igrep", [128, 16])
        gcol = sb("gcol", [128, 1])
        omgc = sb("omgc", [128, 1])
        csts = sb("csts", [128, 4])
        ptb = ctx.enter_context(nc.psum_tensor("ptb", [128, 128], BF16))
        py = [ctx.enter_context(nc.psum_tensor(f"py{i}", [128, 512], F32))
              for i in range(2)]
        sems = {n: ctx.enter_context(nc.semaphore(n)) for n in SEMNAMES}
        block = ctx.enter_context(nc.Block())

        S = Sched()
        S.sems(SEMNAMES)
        for j, v in enumerate(CONSTS):
            nc.const_aps.register(F32, v, csts[:, j:j + 1])
        for j, v in enumerate(CONSTS):
            S.run("vector", lambda e, j=j, v=v: e.memset(csts[:, j:j + 1], v), "VS")
        S.wait("scalar", "VS", S.cnt["VS"])

        xt = [_xt[:, (i % 2) * D:((i % 2) + 1) * D] for i in range(NTT)]
        yt = [_yt[:, (i % 2) * D:((i % 2) + 1) * D] for i in range(NTT)]
        wbb = [_wbb[:, i * 1024:(i + 1) * 1024] for i in range(8)]

        for fn in (
            lambda e: e.dma_start(out=lnps_s[:, :], in_=lnps),
            lambda e: e.dma_start(out=lyt_s[:, :], in_=lyt),
            lambda e: e.dma_start(out=lxt_s[:, :], in_=lxt),
        ):
            S.run("sync", fn, "LD", 16)
        for fn in (
            lambda e: e.dma_start(out=wsfrep[:, :], in_=wsumf.broadcast_to([128, 24])),
            lambda e: e.dma_start(out=hrbrep[:, :], in_=hrb.broadcast_to([128, 16])),
            lambda e: e.dma_start(out=hpbrep[:, :], in_=hpb.broadcast_to([128, 4])),
            lambda e: e.dma_start(out=hobrep[:, :], in_=hob.broadcast_to([128, 4])),
            lambda e: e.dma_start(out=eyerep[:, :], in_=eye16.broadcast_to([128, 16])),
            lambda e: e.dma_start(out=gcol[:, :], in_=gatei.broadcast_to([128, 1])),
            lambda e: e.dma_start(out=idn[:, :], in_=ident),
        ):
            S.run("gpsimd", fn, "GD", 16)
        for i in range(8):
            S.run("gpsimd", lambda e, i=i: e.dma_start(
                out=wbb[i], in_=wb[i * 128:(i + 1) * 128, :]), "GD", 16)
        ld_once = S.cnt["LD"]
        gd_once = S.cnt["GD"]
        S.wait("vector", "GD", gd_once)
        S.run("vector", lambda e: e.tensor_scalar(
            out=hrbrep[:, :], in0=hrbrep[:, :], scalar1=1.0 / TAU, scalar2=None,
            op0=ALU.mult), "VS")
        vs_hrb = S.cnt["VS"]

        S.wait("vector", "LD", ld_once)
        S.run("vector", lambda e: e.tensor_reduce(
            out=sc[:, :], in_=lnps_s.rearrange("p (r c) -> p r c", c=8),
            axis=mybir.AxisListType.X, op=ALU.add), "VS")
        S.run("vector", lambda e: e.tensor_reduce(
            out=lyc[:, :], in_=lyt_s.rearrange("p (r c) -> p r c", c=8),
            axis=mybir.AxisListType.X, op=ALU.add), "VS")
        S.wait("vector", "VS", S.cnt["VS"])
        sview = sc.rearrange("p (t r) -> p t r", r=2)
        S.run("vector", lambda e: e.tensor_scalar(
            out=mu2c[:, :], in0=sview[:, :, 0], scalar1=1.0 / D, scalar2=None,
            op0=ALU.mult), "VS")
        S.run("vector", lambda e: e.tensor_scalar(
            out=t8a[:, :], in0=sview[:, :, 1], scalar1=1.0 / D, scalar2=None,
            op0=ALU.mult), "VS")
        S.wait("vector", "VS", S.cnt["VS"])
        S.run("vector", lambda e: e.tensor_tensor(
            out=t8b[:, :], in0=mu2c[:, :], in1=mu2c[:, :], op=ALU.mult), "VS")
        S.wait("vector", "VS", S.cnt["VS"])
        S.run("vector", lambda e: e.tensor_tensor(
            out=t8a[:, :], in0=t8a[:, :], in1=t8b[:, :], op=ALU.subtract), "VS")
        vs_var2 = S.cnt["VS"]
        S.wait("vector", "VS", vs_var2)
        S.wait("scalar", "VS", vs_var2)
        S.run("scalar", lambda e: e.activation(
            out=t8b[:, :], in_=t8a[:, :], func=AF.Sqrt, bias=LN_EPS), "SS")
        ss_sd2 = S.cnt["SS"]
        S.wait("vector", "SS", ss_sd2)
        S.run("vector", lambda e: e.reciprocal(out=r2c[:, :], in_=t8b[:, :]), "VS")
        S.wait("vector", "VS", S.cnt["VS"])
        S.run("vector", lambda e: e.tensor_tensor(
            out=nmu2r2c[:, :], in0=mu2c[:, :], in1=r2c[:, :], op=ALU.mult), "VS")
        S.wait("vector", "VS", S.cnt["VS"])
        S.run("vector", lambda e: e.tensor_scalar(
            out=nmu2r2c[:, :], in0=nmu2r2c[:, :], scalar1=-1.0, scalar2=None,
            op0=ALU.mult), "VS")
        vs_ln2 = S.cnt["VS"]
        S.wait("vector", "VS", vs_ln2)

        ldA = [0] * NTT
        ssA = [0] * NTT
        vsA = [0] * NTT

        def loadA(tl):
            S.run("sync", lambda e, tl=tl: e.dma_start(
                out=xt[tl], in_=xtok[tl * 128:(tl + 1) * 128, :]), "LD", 16)
            S.run("sync", lambda e, tl=tl: e.dma_start(
                out=yt[tl], in_=y2tok[tl * 128:(tl + 1) * 128, :]), "LD", 16)
            ldA[tl] = S.cnt["LD"]

        loadA(0)
        loadA(1)
        for tl in range(NTT):
            S.wait("vector", "LD", ldA[tl])
            if tl >= 1:
                S.wait("vector", "SS", ssA[tl - 1])
            S.run("vector", lambda e, tl=tl: e.tensor_scalar(
                out=x2t[:, :], in0=yt[tl], scalar1=r2c[:, tl:tl + 1],
                scalar2=nmu2r2c[:, tl:tl + 1], op0=ALU.mult, op1=ALU.add), "VS")
            S.wait("vector", "VS", S.cnt["VS"])
            S.run("vector", lambda e, tl=tl: e.tensor_tensor(
                out=x2t[:, :], in0=x2t[:, :], in1=xt[tl], op=ALU.add), "VS")
            vsA[tl] = S.cnt["VS"]
            S.wait("scalar", "VS", vsA[tl])
            S.run("scalar", lambda e, tl=tl: e.activation(
                out=scrb[:, :], in_=x2t[:, :], func=AF.Copy, bias=0.0,
                accum_out=spc[:, tl:tl + 1]), "SS")
            S.wait("scalar", "SS", S.cnt["SS"])
            S.run("scalar", lambda e, tl=tl: e.activation(
                out=scrb[:, :], in_=x2t[:, :], func=AF.Square,
                accum_out=sqc[:, tl:tl + 1]), "SS")
            ssA[tl] = S.cnt["SS"]
            S.wait("scalar", "SS", ssA[tl])
            if tl + 2 < NTT:
                S.wait("sync", "VS", vsA[tl])
                loadA(tl + 2)
        ss_passa = S.cnt["SS"]

        S.wait("vector", "SS", ss_passa)
        S.run("vector", lambda e: e.tensor_scalar(
            out=mupc[:, :], in0=spc[:, :], scalar1=1.0 / D, scalar2=None,
            op0=ALU.mult), "VS")
        S.run("vector", lambda e: e.tensor_scalar(
            out=varpc[:, :], in0=sqc[:, :], scalar1=1.0 / D, scalar2=None,
            op0=ALU.mult), "VS")
        S.wait("vector", "VS", S.cnt["VS"])
        S.run("vector", lambda e: e.tensor_scalar(
            out=nmupc[:, :], in0=mupc[:, :], scalar1=-1.0, scalar2=None,
            op0=ALU.mult), "VS")
        S.run("vector", lambda e: e.tensor_tensor(
            out=t8a[:, :], in0=mupc[:, :], in1=mupc[:, :], op=ALU.mult), "VS")
        S.wait("vector", "VS", S.cnt["VS"])
        S.run("vector", lambda e: e.tensor_tensor(
            out=varpc[:, :], in0=varpc[:, :], in1=t8a[:, :], op=ALU.subtract), "VS")
        S.wait("vector", "VS", S.cnt["VS"])
        S.run("vector", lambda e: e.tensor_scalar(
            out=t8a[:, :], in0=varpc[:, :], scalar1=LN_EPS, scalar2=None,
            op0=ALU.add), "VS")
        vs_q = S.cnt["VS"]
        S.wait("scalar", "VS", vs_q)
        S.run("scalar", lambda e: e.activation(
            out=t8b[:, :], in_=t8a[:, :], func=AF.Sqrt), "SS")
        S.wait("vector", "SS", S.cnt["SS"])
        S.run("vector", lambda e: e.reciprocal(out=rpc[:, :], in_=t8b[:, :]), "VS")
        S.wait("vector", "VS", S.cnt["VS"])
        S.run("vector", lambda e: e.reciprocal(out=t8b[:, :], in_=t8a[:, :]), "VS")
        S.wait("vector", "VS", S.cnt["VS"])
        S.run("vector", lambda e: e.tensor_tensor(
            out=t8b[:, :], in0=varpc[:, :], in1=t8b[:, :], op=ALU.mult), "VS")
        S.wait("vector", "VS", S.cnt["VS"])
        S.run("vector", lambda e: e.tensor_scalar(
            out=t8b[:, :], in0=t8b[:, :], scalar1=RMS_EPS, scalar2=None,
            op0=ALU.add), "VS")
        vs_fr = S.cnt["VS"]
        S.wait("scalar", "VS", vs_fr)
        S.run("scalar", lambda e: e.activation(
            out=t8b[:, :], in_=t8b[:, :], func=AF.Sqrt), "SS")
        S.wait("vector", "SS", S.cnt["SS"])
        S.run("vector", lambda e: e.reciprocal(out=t8a[:, :], in_=t8b[:, :]), "VS")
        S.wait("vector", "VS", S.cnt["VS"])
        S.run("vector", lambda e: e.tensor_tensor(
            out=alphac[:, :], in0=t8a[:, :], in1=rpc[:, :], op=ALU.mult), "VS")
        S.wait("vector", "VS", S.cnt["VS"])
        S.run("vector", lambda e: e.tensor_scalar(
            out=alphatc[:, :], in0=alphac[:, :], scalar1=1.0 / TAU, scalar2=None,
            op0=ALU.mult), "VS")
        vs_alpha = S.cnt["VS"]
        S.wait("vector", "VS", vs_alpha)
        S.wait("vector", "VS", vs_hrb)

        for tl in range(NTT):
            lysl = lyc[:, tl * 24:(tl + 1) * 24]
            lxsl = lxt_s[:, tl * 24:(tl + 1) * 24]
            lgsl = lgt[:, tl * 24:(tl + 1) * 24]
            S.run("vector", lambda e, lysl=lysl, lxsl=lxsl, lgsl=lgsl, tl=tl:
                  e.scalar_tensor_tensor(
                      out=lgsl, in0=lysl, scalar=r2c[:, tl:tl + 1], in1=lxsl,
                      op0=ALU.mult, op1=ALU.add), "VS")
            S.wait("vector", "VS", S.cnt["VS"])
            S.run("vector", lambda e, lgsl=lgsl, tl=tl: e.scalar_tensor_tensor(
                out=lgsl, in0=wsfrep[:, :], scalar=nmu2r2c[:, tl:tl + 1],
                in1=lgsl, op0=ALU.mult, op1=ALU.add), "VS")
            S.wait("vector", "VS", S.cnt["VS"])
            S.run("vector", lambda e, lgsl=lgsl, tl=tl: e.scalar_tensor_tensor(
                out=lgsl, in0=wsfrep[:, :], scalar=nmupc[:, tl:tl + 1],
                in1=lgsl, op0=ALU.mult, op1=ALU.add), "VS")
            S.wait("vector", "VS", S.cnt["VS"])
            S.run("vector", lambda e, lgsl=lgsl, tl=tl: e.scalar_tensor_tensor(
                out=zt[:, tl * 16:(tl + 1) * 16], in0=lgsl[:, 0:16],
                scalar=alphatc[:, tl:tl + 1], in1=hrbrep[:, :],
                op0=ALU.mult, op1=ALU.add), "VS")
            S.run("vector", lambda e, lgsl=lgsl, tl=tl: e.scalar_tensor_tensor(
                out=spre[:, tl * 4:(tl + 1) * 4], in0=lgsl[:, 16:20],
                scalar=alphac[:, tl:tl + 1], in1=hpbrep[:, :],
                op0=ALU.mult, op1=ALU.add), "VS")
            S.run("vector", lambda e, lgsl=lgsl, tl=tl: e.scalar_tensor_tensor(
                out=spost[:, tl * 4:(tl + 1) * 4], in0=lgsl[:, 20:24],
                scalar=alphac[:, tl:tl + 1], in1=hobrep[:, :],
                op0=ALU.mult, op1=ALU.add), "VS")
            S.wait("vector", "VS", S.cnt["VS"])
        vs_zbuild = S.cnt["VS"]

        S.wait("vector", "VS", vs_zbuild)
        S.run("vector", lambda e: e.tensor_reduce(
            out=zm[:, :], in_=zt.rearrange("p (t k) -> p t k", k=16),
            axis=mybir.AxisListType.X, op=ALU.max), "VS")
        S.wait("vector", "VS", S.cnt["VS"])
        S.run("vector", lambda e: e.tensor_tensor(
            out=zt.rearrange("p (t k) -> p t k", k=16),
            in0=zt.rearrange("p (t k) -> p t k", k=16),
            in1=zm.rearrange("p (t o) -> p t o", o=1).broadcast_to([128, 8, 16]),
            op=ALU.subtract), "VS")
        vs_zshift = S.cnt["VS"]
        S.wait("vector", "VS", vs_zshift)

        zt_ij = zt.rearrange("p (t i j) -> p t i j", i=4, j=4)
        pe_ij = pe_.rearrange("p (t i j) -> p t i j", i=4, j=4)
        nu = eu
        nv = ev
        mx = rs
        sm_ = sb("sm_", [128, 32])
        S.run("vector", lambda e: e.memset(nv[:, :], 0.0), "VS")
        S.wait("vector", "VS", S.cnt["VS"])

        def half_step(bc_in, bc_dims, red_view, out_col):
            S.run("vector", lambda e: e.tensor_tensor(
                out=pe_ij, in0=zt_ij, in1=bc_in, op=ALU.subtract), "VS")
            S.wait("vector", "VS", S.cnt["VS"])
            S.run("vector", lambda e: e.tensor_reduce(
                out=mx[:, :], in_=red_view(pe_), axis=mybir.AxisListType.X,
                op=ALU.max), "VS")
            S.wait("vector", "VS", S.cnt["VS"])
            S.run("vector", lambda e: e.tensor_tensor(
                out=pe_ij, in0=pe_ij, in1=bc_dims(mx), op=ALU.subtract), "VS")
            vs_sub = S.cnt["VS"]
            S.wait("vector", "VS", vs_sub)
            S.wait("scalar", "VS", vs_sub)
            S.run("scalar", lambda e: e.activation(
                out=pe_[:, :], in_=pe_[:, :], func=AF.Exp), "SS")
            ss_e = S.cnt["SS"]
            S.wait("scalar", "SS", ss_e)
            S.wait("vector", "SS", ss_e)
            S.run("vector", lambda e: e.tensor_reduce(
                out=sm_[:, :], in_=red_view(pe_), axis=mybir.AxisListType.X,
                op=ALU.add), "VS")
            vs_s = S.cnt["VS"]
            S.wait("vector", "VS", vs_s)
            S.wait("scalar", "VS", vs_s)
            S.run("scalar", lambda e: e.activation(
                out=sm_[:, :], in_=sm_[:, :], func=AF.Ln), "SS")
            ss_l = S.cnt["SS"]
            S.wait("vector", "SS", ss_l)
            S.run("vector", lambda e: e.tensor_tensor(
                out=out_col[:, :], in0=mx[:, :], in1=sm_[:, :], op=ALU.add), "VS")
            S.wait("vector", "VS", S.cnt["VS"])

        bc_j_of = lambda col: col.rearrange(
            "p (t o j) -> p t o j", o=1, j=4).broadcast_to([128, 8, 4, 4])
        bc_i_of = lambda col: col.rearrange(
            "p (t i o) -> p t i o", o=1, i=4).broadcast_to([128, 8, 4, 4])
        bc_ti = lambda col: col.rearrange(
            "p (t i o) -> p t i o", o=1, i=4).broadcast_to([128, 8, 4, 4])
        def bc_tj(col):
            return col.rearrange("p (t j) -> p t j", j=4) \
                .rearrange("p (t o) j -> p t o j", o=1) \
                .broadcast_to([128, 8, 4, 4])
        red_j = lambda buf: buf.rearrange("p (ti j) -> p ti j", j=4)
        red_i = lambda buf: buf.rearrange("p (t i j) -> p t j i", i=4, j=4)

        for it in range(50):
            half_step(bc_tj(nv), bc_ti, red_j, nu)
            half_step(bc_ti(nu), bc_tj, red_i, nv)
        S.run("vector", lambda e: e.tensor_tensor(
            out=pe_ij, in0=zt_ij, in1=bc_ti(nu), op=ALU.subtract), "VS")
        S.wait("vector", "VS", S.cnt["VS"])
        S.run("vector", lambda e: e.tensor_tensor(
            out=pe_ij, in0=pe_ij, in1=bc_tj(nv), op=ALU.subtract), "VS")
        vs_parg = S.cnt["VS"]
        S.wait("scalar", "VS", vs_parg)
        S.run("scalar", lambda e: e.activation(
            out=pe_[:, :], in_=pe_[:, :], func=AF.Exp), "SS")
        ss_P = S.cnt["SS"]
        S.wait("vector", "SS", ss_P)
        vs_P = S.cnt["VS"]
        S.wait("scalar", "GD", gd_once)
        S.run("scalar", lambda e: e.activation(
            out=gcol[:, :], in_=gcol[:, :], func=AF.Sigmoid), "SS")
        ss_g = S.cnt["SS"]
        S.wait("vector", "SS", ss_g)
        S.run("vector", lambda e: e.tensor_scalar(
            out=omgc[:, :], in0=gcol[:, :], scalar1=-1.0, scalar2=1.0,
            op0=ALU.mult, op1=ALU.add), "VS")
        S.wait("vector", "VS", S.cnt["VS"])
        S.run("vector", lambda e: e.tensor_scalar(
            out=igrep[:, :], in0=eyerep[:, :], scalar1=omgc[:, 0:1], scalar2=None,
            op0=ALU.mult), "VS")
        S.wait("vector", "VS", S.cnt["VS"])
        S.run("vector", lambda e: e.scalar_tensor_tensor(
            out=ht.rearrange("p (t k) -> p t k", k=16),
            in0=pe_.rearrange("p (t k) -> p t k", k=16),
            scalar=gcol[:, 0:1],
            in1=igrep.rearrange("p (o k) -> p o k", o=1).broadcast_to([128, 8, 16]),
            op0=ALU.mult, op1=ALU.add), "VS")
        vs_H = S.cnt["VS"]
        S.wait("vector", "VS", vs_H)

        for (src, dst) in ((spre, hpre), (spost, hpost)):
            S.run("vector", lambda e, src=src: e.tensor_reduce(
                out=zm[:, :], in_=src.rearrange("p (t k) -> p t k", k=4),
                axis=mybir.AxisListType.X, op=ALU.max), "VS")
            S.wait("vector", "VS", S.cnt["VS"])
            S.run("vector", lambda e, src=src: e.tensor_tensor(
                out=src.rearrange("p (t k) -> p t k", k=4),
                in0=src.rearrange("p (t k) -> p t k", k=4),
                in1=zm.rearrange("p (t o) -> p t o", o=1).broadcast_to([128, 8, 4]),
                op=ALU.subtract), "VS")
            vs_sm = S.cnt["VS"]
            S.wait("scalar", "VS", vs_sm)
            S.run("scalar", lambda e, src=src: e.activation(
                out=src[:, :], in_=src[:, :], func=AF.Exp), "SS")
            ss_sm = S.cnt["SS"]
            S.wait("vector", "SS", ss_sm)
            S.run("vector", lambda e, src=src: e.tensor_reduce(
                out=zm[:, :], in_=src.rearrange("p (t k) -> p t k", k=4),
                axis=mybir.AxisListType.X, op=ALU.add), "VS")
            S.wait("vector", "VS", S.cnt["VS"])
            S.run("vector", lambda e: e.reciprocal(out=zm[:, :], in_=zm[:, :]), "VS")
            S.wait("vector", "VS", S.cnt["VS"])
            S.run("vector", lambda e, src=src, dst=dst: e.tensor_tensor(
                out=dst.rearrange("p (t k) -> p t k", k=4),
                in0=src.rearrange("p (t k) -> p t k", k=4),
                in1=zm.rearrange("p (t o) -> p t o", o=1).broadcast_to([128, 8, 4]),
                op=ALU.mult), "VS")
            S.wait("vector", "VS", S.cnt["VS"])
        vs_hs = S.cnt["VS"]

        S.wait("gpsimd", "VS", vs_hs)
        S.run("gpsimd", lambda e: e.dma_start(out=o_dbg[:, 0:192], in_=lgt[:, :]), "GD", 16)
        S.run("gpsimd", lambda e: e.dma_start(out=o_dbg[:, 192:320], in_=ht[:, :]), "GD", 16)
        S.run("gpsimd", lambda e: e.dma_start(out=o_dbg[:, 320:352], in_=hpre[:, :]), "GD", 16)
        S.run("gpsimd", lambda e: e.dma_start(out=o_dbg[:, 352:384], in_=hpost[:, :]), "GD", 16)
        S.run("gpsimd", lambda e: e.dma_start(out=o_dbg[:, 384:392], in_=mupc[:, :]), "GD", 16)
        S.run("gpsimd", lambda e: e.dma_start(out=o_dbg[:, 392:400], in_=varpc[:, :]), "GD", 16)
        S.run("gpsimd", lambda e: e.dma_start(out=o_dbg[:, 400:408], in_=alphac[:, :]), "GD", 16)
        S.run("gpsimd", lambda e: e.dma_start(out=o_dbg[:, 408:416], in_=r2c[:, :]), "GD", 16)
        S.run("gpsimd", lambda e: e.dma_start(out=o_dbg[:, 416:424], in_=mu2c[:, :]), "GD", 16)

        ldB = [0] * NTT
        vsB = [0] * NTT

        def loadB(tl, wait_vs):
            S.wait("sync", "VS", wait_vs)
            S.run("sync", lambda e, tl=tl: e.dma_start(
                out=xt[tl], in_=xtok[tl * 128:(tl + 1) * 128, :]), "LD", 16)
            S.run("sync", lambda e, tl=tl: e.dma_start(
                out=yt[tl], in_=y2tok[tl * 128:(tl + 1) * 128, :]), "LD", 16)
            ldB[tl] = S.cnt["LD"]

        loadB(0, vsA[NTT - 1])
        loadB(1, vsA[NTT - 1])
        ts_tr_prev = 0
        vs_xpT_evac = 0
        vs_yb_evac = 0
        gd_x3 = 0
        for tl in range(NTT):
            S.wait("vector", "LD", ldB[tl])
            S.wait("vector", "VS", vs_hs)
            S.wait("vector", "GD", gd_x3)
            S.run("vector", lambda e, tl=tl: e.tensor_scalar(
                out=x2t[:, :], in0=yt[tl], scalar1=r2c[:, tl:tl + 1],
                scalar2=nmu2r2c[:, tl:tl + 1], op0=ALU.mult, op1=ALU.add), "VS")
            S.wait("vector", "VS", S.cnt["VS"])
            S.run("vector", lambda e, tl=tl: e.tensor_tensor(
                out=x2t[:, :], in0=x2t[:, :], in1=xt[tl], op=ALU.add), "VS")
            vsB[tl] = S.cnt["VS"]
            S.wait("vector", "VS", vsB[tl])
            if tl + 2 < NTT:
                loadB(tl + 2, vsB[tl])
            S.run("vector", lambda e, tl=tl: e.tensor_scalar(
                out=xint[:, :], in0=x2t[:, :], scalar1=nmupc[:, tl:tl + 1],
                scalar2=rpc[:, tl:tl + 1], op0=ALU.add, op1=ALU.mult), "VS")
            S.wait("vector", "VS", S.cnt["VS"])
            for i in range(4):
                S.run("vector", lambda e, tl=tl, i=i: e.tensor_scalar(
                    out=xmix[:, i * 1024:(i + 1) * 1024],
                    in0=xint[:, 0:1024],
                    scalar1=ht[:, tl * 16 + i * 4:tl * 16 + i * 4 + 1],
                    scalar2=None, op0=ALU.mult), "VS")
            S.wait("vector", "VS", S.cnt["VS"])
            for j in range(1, 4):
                for i in range(4):
                    S.run("vector", lambda e, tl=tl, i=i, j=j: e.scalar_tensor_tensor(
                        out=xmix[:, i * 1024:(i + 1) * 1024],
                        in0=xint[:, j * 1024:(j + 1) * 1024],
                        scalar=ht[:, tl * 16 + i * 4 + j:tl * 16 + i * 4 + j + 1],
                        in1=xmix[:, i * 1024:(i + 1) * 1024],
                        op0=ALU.mult, op1=ALU.add), "VS")
                S.wait("vector", "VS", S.cnt["VS"])
            S.run("vector", lambda e, tl=tl: e.tensor_scalar(
                out=xpre32[:, :], in0=xint[:, 0:1024],
                scalar1=hpre[:, tl * 4:tl * 4 + 1], scalar2=None,
                op0=ALU.mult), "VS")
            S.wait("vector", "VS", S.cnt["VS"])
            for n in range(1, 4):
                S.run("vector", lambda e, tl=tl, n=n: e.scalar_tensor_tensor(
                    out=xpre32[:, :], in0=xint[:, n * 1024:(n + 1) * 1024],
                    scalar=hpre[:, tl * 4 + n:tl * 4 + n + 1], in1=xpre32[:, :],
                    op0=ALU.mult, op1=ALU.add), "VS")
                S.wait("vector", "VS", S.cnt["VS"])
            S.wait("vector", "TS", ts_tr_prev)
            S.run("vector", lambda e: e.tensor_copy(
                out=xpreb[:, :], in_=xpre32[:, :]), "VS")
            vs_xpreb = S.cnt["VS"]
            S.wait("vector", "VS", vs_xpreb)
            vs_ev = vs_xpT_evac
            for k in range(8):
                S.wait("tensor", "VS", vs_xpreb if k == 0 else vs_ev)
                S.run("tensor", lambda e, k=k: e.transpose(
                    ptb[:, :], xpreb[:, k * 128:(k + 1) * 128], idn[:, :]), "TS")
                ts_tr = S.cnt["TS"]
                S.wait("vector", "TS", ts_tr)
                S.run("vector", lambda e, k=k: e.tensor_copy(
                    out=xpT[:, k * 128:(k + 1) * 128], in_=ptb[:, :]), "VS")
                vs_ev = S.cnt["VS"]
                S.wait("vector", "VS", vs_ev)
            ts_tr_prev = S.cnt["TS"]
            vs_xpT_evac = vs_ev
            S.wait("tensor", "VS", vs_xpT_evac)
            S.wait("tensor", "GD", gd_once)
            S.wait("tensor", "VS", vs_yb_evac)
            for h in range(2):
                for k in range(8):
                    S.run("tensor", lambda e, k=k, h=h: e.matmul(
                        py[h][:, :], xpT[:, k * 128:(k + 1) * 128],
                        wbb[k][:, h * 512:(h + 1) * 512],
                        start=(k == 0), stop=(k == 7)),
                        "TS" if k == 7 else None)
            ts_mm = S.cnt["TS"]
            S.wait("vector", "TS", ts_mm)
            for h in range(2):
                S.run("vector", lambda e, h=h: e.tensor_copy(
                    out=yb[:, h * 512:(h + 1) * 512], in_=py[h][:, :]), "VS")
            S.wait("vector", "VS", S.cnt["VS"])
            vs_yb_evac = S.cnt["VS"]
            for n in range(4):
                S.run("vector", lambda e, tl=tl, n=n: e.scalar_tensor_tensor(
                    out=xmix[:, n * 1024:(n + 1) * 1024], in0=yb[:, :],
                    scalar=hpost[:, tl * 4 + n:tl * 4 + n + 1],
                    in1=xmix[:, n * 1024:(n + 1) * 1024],
                    op0=ALU.mult, op1=ALU.add), "VS")
            vs_ffn = S.cnt["VS"]
            S.wait("vector", "VS", vs_ffn)
            S.wait("scalar", "VS", vs_ffn)
            S.run("scalar", lambda e: e.activation(
                out=scrb[:, :], in_=xmix[:, :], func=AF.Copy, bias=0.0,
                accum_out=s3c[:, 0:1]), "SS")
            S.wait("scalar", "SS", S.cnt["SS"])
            S.run("scalar", lambda e: e.activation(
                out=scrb[:, :], in_=xmix[:, :], func=AF.Square,
                accum_out=q3c[:, 0:1]), "SS")
            ss_ln3 = S.cnt["SS"]
            S.wait("vector", "SS", ss_ln3)
            S.run("vector", lambda e: e.tensor_scalar(
                out=mu3c[:, :], in0=s3c[:, :], scalar1=1.0 / D, scalar2=None,
                op0=ALU.mult), "VS")
            S.run("vector", lambda e: e.tensor_scalar(
                out=t1c[:, :], in0=q3c[:, :], scalar1=1.0 / D, scalar2=None,
                op0=ALU.mult), "VS")
            S.wait("vector", "VS", S.cnt["VS"])
            S.run("vector", lambda e: e.tensor_tensor(
                out=r3c[:, :], in0=mu3c[:, :], in1=mu3c[:, :], op=ALU.mult), "VS")
            S.wait("vector", "VS", S.cnt["VS"])
            S.run("vector", lambda e: e.tensor_tensor(
                out=t1c[:, :], in0=t1c[:, :], in1=r3c[:, :], op=ALU.subtract), "VS")
            vs_v3 = S.cnt["VS"]
            S.wait("scalar", "VS", vs_v3)
            S.run("scalar", lambda e: e.activation(
                out=t1c[:, :], in_=t1c[:, :], func=AF.Sqrt, bias=LN_EPS), "SS")
            ss_sd3 = S.cnt["SS"]
            S.wait("vector", "SS", ss_sd3)
            S.run("vector", lambda e: e.reciprocal(out=r3c[:, :], in_=t1c[:, :]), "VS")
            S.wait("vector", "VS", S.cnt["VS"])
            S.run("vector", lambda e: e.tensor_tensor(
                out=nmu3r3c[:, :], in0=mu3c[:, :], in1=r3c[:, :], op=ALU.mult), "VS")
            S.wait("vector", "VS", S.cnt["VS"])
            S.run("vector", lambda e: e.tensor_scalar(
                out=nmu3r3c[:, :], in0=nmu3r3c[:, :], scalar1=-1.0, scalar2=None,
                op0=ALU.mult), "VS")
            S.wait("vector", "VS", S.cnt["VS"])
            S.wait("vector", "GD", gd_x3)
            S.run("vector", lambda e: e.tensor_scalar(
                out=xmix[:, :], in0=xmix[:, :], scalar1=r3c[:, 0:1],
                scalar2=nmu3r3c[:, 0:1], op0=ALU.mult, op1=ALU.add), "VS")
            S.wait("vector", "VS", S.cnt["VS"])
            S.run("vector", lambda e: e.tensor_tensor(
                out=x2t[:, :], in0=x2t[:, :], in1=xmix[:, :], op=ALU.add), "VS")
            vs_x3 = S.cnt["VS"]
            S.wait("gpsimd", "VS", vs_x3)
            S.run("gpsimd", lambda e, tl=tl: e.dma_start(
                out=o_x3[tl * 128:(tl + 1) * 128, :], in_=x2t[:, :]), "GD", 16)
            gd_x3 = S.cnt["GD"]
        S.wait("sync", "GD", S.cnt["GD"])

        S.emit(block, sems)
    return nc


_L3 = None


def _get_l3():
    global _L3
    if _L3 is None:
        _L3 = build_l3()
    return _L3


def run_l3(x2d, y2_full, lnp_parts, ly_parts, lx_parts, wsumf, hres_base,
           hpre_base, hpost_base, gate, wbr):
    nc = _get_l3()
    eye = np.eye(4, dtype=np.float32).reshape(1, 16)
    idn = np.eye(128, dtype=np.float32)
    lnp_st = np.stack(lnp_parts, axis=-1)
    ly_st = np.stack(ly_parts, axis=-1)
    in_maps = []
    for c in range(NCORES):
        tsl = slice(c * TOK, (c + 1) * TOK)
        lp = lnp_st[:, tsl, :].reshape(2, 8, 128, 8).transpose(2, 1, 0, 3) \
            .reshape(128, 16 * 8)
        lyp = ly_st[:, tsl, :].reshape(24, 8, 128, 8).transpose(2, 1, 0, 3) \
            .reshape(128, 192 * 8)
        lxp = lx_parts[c].reshape(24, 8, 128).transpose(2, 1, 0).reshape(128, 192)
        in_maps.append(dict(
            xtok=np.ascontiguousarray(x2d[tsl]),
            y2tok=np.ascontiguousarray(y2_full[:, tsl].T),
            lnps=np.ascontiguousarray(lp),
            lyt=np.ascontiguousarray(lyp),
            lxt=np.ascontiguousarray(lxp),
            wsumf=wsumf.reshape(1, 24),
            hrb=hres_base.reshape(1, 16),
            hpb=hpre_base.reshape(1, 4),
            hob=hpost_base.reshape(1, 4),
            gatei=gate.reshape(1, 1),
            eye16=eye,
            ident=idn,
            wb=wbr,
        ))
    res = run_bass_kernel_spmd(nc, in_maps, core_ids=list(range(NCORES)))
    return res.results


def kernel(L0_attn_x, L0_attn_A_log, L0_attn_D, L0_attn_x_proj_w,
           L0_attn_dt_proj_w, L0_attn_dt_proj_b, L0_ffn_W_res, L0_ffn_W_pre,
           L0_ffn_W_post, L0_ffn_H_res_base, L0_ffn_H_pre_base,
           L0_ffn_H_post_base, L0_ffn_gate, L0_ffn_W_branch):
    f32 = np.float32
    x2d = np.ascontiguousarray(np.asarray(L0_attn_x, f32).reshape(TT, D))
    xpw = np.asarray(L0_attn_x_proj_w, f32)
    wcat = np.ascontiguousarray(np.concatenate(
        [np.asarray(L0_ffn_W_res, f32), np.asarray(L0_ffn_W_pre, f32),
         np.asarray(L0_ffn_W_post, f32)], axis=0))

    r1 = run_l1(x2d, xpw, wcat)
    xp_full = np.ascontiguousarray(
        np.concatenate([r["xpraw"] for r in r1], axis=1))
    mean = np.concatenate([r["mcol"].T.reshape(-1) for r in r1])
    rstd = np.concatenate([r["rcol"].T.reshape(-1) for r in r1])
    mur = np.concatenate([r["mrcol"].T.reshape(-1) for r in r1])
    st1 = np.ascontiguousarray(np.stack([mean, rstd, mur]).astype(f32))
    wsum = r1[0]["wsums"][0, :R].astype(f32)
    wsumf = r1[0]["wsums"][0, R:].astype(f32)
    lx_parts = [r["lx"] for r in r1]

    r2 = run_l2(x2d, xp_full, st1, np.asarray(L0_attn_A_log, f32),
                np.asarray(L0_attn_D, f32), np.asarray(L0_attn_dt_proj_w, f32),
                np.asarray(L0_attn_dt_proj_b, f32), wcat, wsum)
    y2_full = np.concatenate([r["y2"] for r in r2], axis=0)
    lnp_parts = [r["lnp"] for r in r2]
    ly_parts = [r["ly"] for r in r2]

    r3 = run_l3(x2d, y2_full, lnp_parts, ly_parts, lx_parts, wsumf,
                np.asarray(L0_ffn_H_res_base, f32),
                np.asarray(L0_ffn_H_pre_base, f32),
                np.asarray(L0_ffn_H_post_base, f32),
                np.asarray(L0_ffn_gate, f32),
                np.ascontiguousarray(np.asarray(L0_ffn_W_branch, f32)))
    out = np.concatenate([r["x3"] for r in r3], axis=0)
    return out.reshape(B, L, D)


# version 05
# speedup vs baseline: 1.2226x; 1.2226x over previous
from contextlib import ExitStack

import numpy as np
import concourse.bass as bass
import concourse.mybir as mybir
from concourse.bass_utils import run_bass_kernel_spmd

F32 = mybir.dt.float32
BF16 = mybir.dt.bfloat16
AF = mybir.ActivationFunctionType
ALU = mybir.AluOpType

NCORES = 8
B, L, D = 2, 4096, 4096
TT = B * L
TOK = TT // NCORES
DC = D // NCORES
N = 16
R = 160
DTR = 128
CH = 256
NCH = TT // CH
CHB = L // CH
LN_EPS = 1e-5
RMS_EPS = 1.1920929e-07
TAU = 0.1


class Sched:

    ENGS = ("sync", "gpsimd", "tensor", "scalar", "vector")

    def __init__(self):
        self.ops = {e: [] for e in self.ENGS}
        self.cnt = {}

    def sems(self, names):
        for n in names:
            self.cnt[n] = 0

    def wait(self, eng, name, val):
        if val > 0:
            self.ops[eng].append(("w", name, val))

    def run(self, eng, fn, inc=None, k=1):
        self.ops[eng].append(("r", fn, inc, k))
        if inc is not None:
            self.cnt[inc] += k
            return self.cnt[inc]
        return None

    def emit(self, block, sems):
        def mk(eng_name):
            def body(e):
                for op in self.ops[eng_name]:
                    if op[0] == "w":
                        e.wait_ge(sems[op[1]], op[2])
                    else:
                        inst = op[1](e)
                        if op[2] is not None:
                            inst.then_inc(sems[op[2]], op[3])
            return body
        block.sync(mk("sync"))
        block.gpsimd(mk("gpsimd"))
        block.tensor(mk("tensor"))
        block.scalar(mk("scalar"))
        block.vector(mk("vector"))


SEMNAMES = ["LD", "GD", "TS", "SS", "VS", "ST", "PS"]
CONSTS = [0.0, 1.0, LN_EPS, RMS_EPS]


def build_l1():
    nc = bass.Bass("TRN2", target_bir_lowering=False)
    xtok = nc.dram_tensor("xtok", [TOK, D], F32, kind="ExternalInput").ap()
    xtokT = nc.dram_tensor("xtokT", [D, TOK], F32, kind="ExternalInput").ap()
    xpwT = nc.dram_tensor("xpwT", [D, R], F32, kind="ExternalInput").ap()
    wcatT = nc.dram_tensor("wcatT", [D, 24], F32, kind="ExternalInput").ap()
    ones_in = nc.dram_tensor("ones_in", [1, 128], F32, kind="ExternalInput").ap()

    o_xp = nc.dram_tensor("xpraw", [R, TOK], F32, kind="ExternalOutput").ap()
    o_lx = nc.dram_tensor("lx", [24, TOK], F32, kind="ExternalOutput").ap()
    o_m = nc.dram_tensor("mcol", [128, 8], F32, kind="ExternalOutput").ap()
    o_r = nc.dram_tensor("rcol", [128, 8], F32, kind="ExternalOutput").ap()
    o_mr = nc.dram_tensor("mrcol", [128, 8], F32, kind="ExternalOutput").ap()
    o_ws = nc.dram_tensor("wsums", [1, R + 24], F32, kind="ExternalOutput").ap()

    NTT = TOK // 128
    ND = D // 128

    with ExitStack() as ctx:
        def sb(name, shape, dt=F32):
            return ctx.enter_context(nc.sbuf_tensor(name, shape, dt))

        def ps(name, shape):
            return ctx.enter_context(nc.psum_tensor(name, shape, F32))

        _xk = sb("xk", [128, 2 * D])
        _xb = sb("xb", [128, 4 * TOK])
        _wpb = sb("wpb", [128, ND * R])
        _wcb = sb("wcb", [128, ND * 24])
        onec = sb("onec", [128, 1])
        scr = sb("scr", [128, D], BF16)
        sumc = sb("sumc", [128, 8])
        sqc = sb("sqc", [128, 8])
        meanc = sb("meanc", [128, 8])
        varc = sb("varc", [128, 8])
        sdc = sb("sdc", [128, 8])
        rstdc = sb("rstdc", [128, 8])
        t8 = sb("t8", [128, 8])
        mrc = sb("mrc", [128, 8])
        xpev = sb("xpev", [128, TOK])
        xpev2 = sb("xpev2", [32, TOK])
        lxev = sb("lxev", [24, TOK])
        wsev = sb("wsev", [1, R + 24])
        csts = sb("csts", [128, 4])
        p1 = ps("p1", [128, TOK])
        p2 = ps("p2", [32, TOK])
        plx = ps("plx", [24, TOK])
        pws = ps("pws", [1, R])
        pwf = ps("pwf", [1, 24])
        sems = {n: ctx.enter_context(nc.semaphore(n)) for n in SEMNAMES}
        block = ctx.enter_context(nc.Block())

        S = Sched()
        S.sems(SEMNAMES)
        for j, v in enumerate(CONSTS):
            nc.const_aps.register(F32, v, csts[:, j:j + 1])
        for j, v in enumerate(CONSTS):
            S.run("vector", lambda e, j=j, v=v: e.memset(csts[:, j:j + 1], v), "VS")
        vs_consts = S.cnt["VS"]
        S.wait("scalar", "VS", vs_consts)

        xk = [_xk[:, (i % 2) * D:((i % 2) + 1) * D] for i in range(NTT)]
        xb = [_xb[:, (i % 4) * TOK:((i % 4) + 1) * TOK] for i in range(ND)]
        wpb = [_wpb[:, i * R:(i + 1) * R] for i in range(ND)]
        wcb = [_wcb[:, i * 24:(i + 1) * 24] for i in range(ND)]

        for i in range(ND):
            S.run("gpsimd", lambda e, i=i: e.dma_start(
                out=wpb[i], in_=xpwT[i * 128:(i + 1) * 128, :]), "GD", 16)
        for i in range(ND):
            S.run("gpsimd", lambda e, i=i: e.dma_start(
                out=wcb[i], in_=wcatT[i * 128:(i + 1) * 128, :]), "GD", 16)
        S.run("gpsimd", lambda e: e.dma_start(
            out=onec[:, :],
            in_=ones_in.rearrange("o (p q) -> (o p) q", q=1)), "GD", 16)
        gd_all = S.cnt["GD"]

        ld_tile = []
        ss_tile = [0] * NTT
        for i in range(2):
            S.run("sync", lambda e, i=i: e.dma_start(
                out=xk[i], in_=xtok[i * 128:(i + 1) * 128, :]), "LD", 16)
            ld_tile.append(S.cnt["LD"])
        for i in range(NTT):
            S.wait("scalar", "LD", ld_tile[i])
            S.run("scalar", lambda e, i=i: e.activation(
                out=scr[:, :], in_=xk[i], func=AF.Copy, bias=0.0,
                accum_out=sumc[:, i:i + 1]), "SS")
            S.wait("scalar", "SS", S.cnt["SS"])
            S.run("scalar", lambda e, i=i: e.activation(
                out=scr[:, :], in_=xk[i], func=AF.Square,
                accum_out=sqc[:, i:i + 1]), "SS")
            ss_tile[i] = S.cnt["SS"]
            S.wait("scalar", "SS", ss_tile[i])
            if i + 2 < NTT:
                S.wait("sync", "SS", ss_tile[i])
                S.run("sync", lambda e, i=i: e.dma_start(
                    out=xk[i + 2], in_=xtok[(i + 2) * 128:(i + 3) * 128, :]),
                    "LD", 16)
                ld_tile.append(S.cnt["LD"])
        ss_stats = S.cnt["SS"]

        S.wait("vector", "SS", ss_stats)
        S.run("vector", lambda e: e.tensor_scalar(
            out=meanc[:, :], in0=sumc[:, :], scalar1=1.0 / D, scalar2=None,
            op0=ALU.mult), "VS")
        S.wait("vector", "VS", S.cnt["VS"])
        S.run("vector", lambda e: e.tensor_scalar(
            out=varc[:, :], in0=sqc[:, :], scalar1=1.0 / D, scalar2=None,
            op0=ALU.mult), "VS")
        S.wait("vector", "VS", S.cnt["VS"])
        S.run("vector", lambda e: e.tensor_tensor(
            out=t8[:, :], in0=meanc[:, :], in1=meanc[:, :], op=ALU.mult), "VS")
        S.wait("vector", "VS", S.cnt["VS"])
        S.run("vector", lambda e: e.tensor_tensor(
            out=varc[:, :], in0=varc[:, :], in1=t8[:, :], op=ALU.subtract), "VS")
        vs_var = S.cnt["VS"]
        S.wait("scalar", "VS", vs_var)
        S.run("scalar", lambda e: e.activation(
            out=sdc[:, :], in_=varc[:, :], func=AF.Sqrt, bias=LN_EPS), "SS")
        ss_sd = S.cnt["SS"]
        S.wait("vector", "SS", ss_sd)
        S.run("vector", lambda e: e.reciprocal(out=rstdc[:, :], in_=sdc[:, :]), "VS")
        S.wait("vector", "VS", S.cnt["VS"])
        S.run("vector", lambda e: e.tensor_tensor(
            out=mrc[:, :], in0=meanc[:, :], in1=rstdc[:, :], op=ALU.mult), "VS")
        vs_rstd = S.cnt["VS"]

        S.wait("tensor", "GD", gd_all)
        ldx = [0] * ND
        ts_used = [0] * ND
        for i in range(4):
            S.run("sync", lambda e, i=i: e.dma_start(
                out=xb[i], in_=xtokT[i * 128:(i + 1) * 128, :]), "LD", 16)
            ldx[i] = S.cnt["LD"]
        for i in range(ND):
            S.wait("tensor", "LD", ldx[i])
            for h in range(2):
                lo, hi = h * 512, (h + 1) * 512
                S.run("tensor", lambda e, i=i, lo=lo, hi=hi: e.matmul(
                    p1[:, lo:hi], wpb[i][:, 0:128], xb[i][:, lo:hi],
                    start=(i == 0), stop=(i == ND - 1)), None)
                S.run("tensor", lambda e, i=i, lo=lo, hi=hi: e.matmul(
                    p2[:, lo:hi], wpb[i][:, 128:160], xb[i][:, lo:hi],
                    start=(i == 0), stop=(i == ND - 1)), None)
                S.run("tensor", lambda e, i=i, lo=lo, hi=hi: e.matmul(
                    plx[:, lo:hi], wcb[i], xb[i][:, lo:hi],
                    start=(i == 0), stop=(i == ND - 1)), None)
            S.run("tensor", lambda e, i=i: e.matmul(
                pws[:, :], onec[:, :], wpb[i],
                start=(i == 0), stop=(i == ND - 1)), None)
            S.run("tensor", lambda e, i=i: e.matmul(
                pwf[:, :], onec[:, :], wcb[i],
                start=(i == 0), stop=(i == ND - 1)), "TS")
            ts_used[i] = S.cnt["TS"]
            if i + 4 < ND:
                S.wait("sync", "TS", ts_used[i])
                S.run("sync", lambda e, i=i: e.dma_start(
                    out=xb[i + 4], in_=xtokT[(i + 4) * 128:(i + 5) * 128, :]),
                    "LD", 16)
                ldx[i + 4] = S.cnt["LD"]
        ts_all = S.cnt["TS"]

        S.wait("scalar", "TS", ts_all)
        for fn in (
            lambda e: e.activation(out=xpev[:, :], in_=p1[:, :], func=AF.Copy, bias=0.0),
            lambda e: e.activation(out=xpev2[:, :], in_=p2[:, :], func=AF.Copy, bias=0.0),
            lambda e: e.activation(out=lxev[:, :], in_=plx[:, :], func=AF.Copy, bias=0.0),
            lambda e: e.activation(out=wsev[:, 0:R], in_=pws[:, :], func=AF.Copy, bias=0.0),
            lambda e: e.activation(out=wsev[:, R:R + 24], in_=pwf[:, :], func=AF.Copy, bias=0.0),
        ):
            S.run("scalar", fn, "SS")
        ss_evac = S.cnt["SS"]

        S.wait("sync", "SS", ss_evac)
        S.wait("sync", "VS", vs_rstd)
        S.run("sync", lambda e: e.dma_start(out=o_xp[0:128, :], in_=xpev[:, :]), "ST", 16)
        S.run("sync", lambda e: e.dma_start(out=o_xp[128:160, :], in_=xpev2[:, :]), "ST", 16)
        S.run("sync", lambda e: e.dma_start(out=o_lx, in_=lxev[:, :]), "ST", 16)
        S.run("sync", lambda e: e.dma_start(out=o_m, in_=meanc[:, :]), "ST", 16)
        S.run("sync", lambda e: e.dma_start(out=o_r, in_=rstdc[:, :]), "ST", 16)
        S.run("sync", lambda e: e.dma_start(out=o_mr, in_=mrc[:, :]), "ST", 16)
        S.run("sync", lambda e: e.dma_start(out=o_ws, in_=wsev[:, :]), "ST", 16)
        S.wait("sync", "ST", S.cnt["ST"])

        S.emit(block, sems)
    return nc


_L1 = None


def _get_l1():
    global _L1
    if _L1 is None:
        _L1 = build_l1()
    return _L1


def run_l1(x2d, x_proj_w, wcat):
    nc = _get_l1()
    xpwT = np.ascontiguousarray(x_proj_w.T)
    wcatT = np.ascontiguousarray(wcat.T)
    ones = np.ones((1, 128), np.float32)
    in_maps = []
    for c in range(NCORES):
        xt = np.ascontiguousarray(x2d[c * TOK:(c + 1) * TOK])
        in_maps.append(dict(
            xtok=xt, xtokT=np.ascontiguousarray(xt.T),
            xpwT=xpwT, wcatT=wcatT, ones_in=ones))
    res = run_bass_kernel_spmd(nc, in_maps, core_ids=list(range(NCORES)))
    return res.results




def build_l2():
    nc = bass.Bass("TRN2", target_bir_lowering=False)
    xdT = nc.dram_tensor("xdT", [DC, TT], F32, kind="ExternalInput").ap()
    xp = nc.dram_tensor("xp", [R, TT], F32, kind="ExternalInput").ap()
    st1 = nc.dram_tensor("st1", [3, TT], F32, kind="ExternalInput").ap()
    alog = nc.dram_tensor("alog", [128, 4 * N], F32, kind="ExternalInput").ap()
    dcol = nc.dram_tensor("dcol", [128, 4], F32, kind="ExternalInput").ap()
    dtwT = nc.dram_tensor("dtwT", [128, DC], F32, kind="ExternalInput").ap()
    dtbc = nc.dram_tensor("dtbc", [128, 4], F32, kind="ExternalInput").ap()
    wctT = nc.dram_tensor("wctT", [128, 4 * 24], F32, kind="ExternalInput").ap()
    wsc = nc.dram_tensor("wsc", [128, 1], F32, kind="ExternalInput").ap()
    wsb = nc.dram_tensor("wsb", [32, 1], F32, kind="ExternalInput").ap()

    o_y2 = nc.dram_tensor("y2", [DC, TT], F32, kind="ExternalOutput").ap()
    o_lnp = nc.dram_tensor("lnp", [2, TT], F32, kind="ExternalOutput").ap()
    o_ly = nc.dram_tensor("ly", [24, TT], F32, kind="ExternalOutput").ap()

    bcd = [nc.dram_tensor(f"bcd{i}", [32, CH], F32, kind="Internal").ap()
           for i in range(2)]

    with ExitStack() as ctx:
        def sb(name, shape, dt=F32):
            return ctx.enter_context(nc.sbuf_tensor(name, shape, dt))

        def ps(name, shape):
            return ctx.enter_context(nc.psum_tensor(name, shape, F32))

        _xd = sb("xd", [128, 2 * 4 * CH])
        _xln = sb("xln", [128, 4 * CH])
        _dt4 = sb("dt4", [128, 4 * CH])
        _w4 = sb("w4", [128, 4 * CH])
        _dti = sb("dti", [128, 2 * CH])
        _dta = sb("dta", [128, 2 * CH])
        _bc = sb("bc", [32, 2 * CH])
        _bca = sb("bca", [32, 2 * CH])
        _mrep = sb("mrep", [128, 2 * CH])
        _rrep = sb("rrep", [128, 2 * CH])
        _mrrep = sb("mrrep", [128, 2 * CH])
        _ball = sb("ball", [128, N * CH])
        _call = sb("call", [128, N * CH])
        _u1 = sb("u1", [128, 2 * N * CH])
        _u2 = sb("u2", [128, 2 * N * CH])
        _u3 = sb("u3", [128, 2 * N * CH])
        _y2t = sb("y2t", [128, 4 * CH])
        ysq = sb("ysq", [128, CH])
        sptmp = sb("sptmp", [128, CH])
        carry = sb("carry", [128, 4 * N])
        inj = sb("inj", [128, N])
        t256a = sb("t256a", [128, CH])
        t256b = sb("t256b", [32, CH])
        alog_s = sb("alog_s", [128, 4 * N])
        aneg = sb("aneg", [128, 4 * N])
        dcol_s = sb("dcol_s", [128, 4])
        dtw_s = sb("dtw_s", [128, DC])
        dtb_s = sb("dtb_s", [128, 4])
        wct_s = sb("wct_s", [128, 4 * 24])
        wsc_s = sb("wsc_s", [128, 1])
        wsb_s = sb("wsb_s", [32, 1])
        nwsc = sb("nwsc", [128, 1])
        nwsb = sb("nwsb", [32, 1])
        lyev = sb("lyev", [24, 2 * CH])
        lnev = sb("lnev", [1, 4 * CH])
        csts = sb("csts", [128, 4])
        pdtz = [ps(f"pdtz{i}", [128, CH]) for i in range(2)]
        pln1 = ps("pln1", [1, CH])
        pln2 = ps("pln2", [1, CH])
        ply = ps("ply", [24, CH])
        sems = {n: ctx.enter_context(nc.semaphore(n)) for n in SEMNAMES}
        block = ctx.enter_context(nc.Block())

        S = Sched()
        S.sems(SEMNAMES)
        for j, v in enumerate(CONSTS):
            nc.const_aps.register(F32, v, csts[:, j:j + 1])
        for j, v in enumerate(CONSTS):
            S.run("vector", lambda e, j=j, v=v: e.memset(csts[:, j:j + 1], v), "VS")
        vs_consts = S.cnt["VS"]
        S.wait("scalar", "VS", vs_consts)
        S.wait("tensor", "VS", vs_consts)

        xd = [[_xd[:, (p * 4 + t) * CH:(p * 4 + t + 1) * CH] for t in range(4)]
              for p in range(2)]
        xln = [_xln[:, t * CH:(t + 1) * CH] for t in range(4)]
        dt4 = [_dt4[:, t * CH:(t + 1) * CH] for t in range(4)]
        w4 = [_w4[:, t * CH:(t + 1) * CH] for t in range(4)]
        dti = [_dti[:, p * CH:(p + 1) * CH] for p in range(2)]
        dta = [_dta[:, p * CH:(p + 1) * CH] for p in range(2)]
        bc = [_bc[:, p * CH:(p + 1) * CH] for p in range(2)]
        bca = [_bca[:, p * CH:(p + 1) * CH] for p in range(2)]
        mrep = [_mrep[:, p * CH:(p + 1) * CH] for p in range(2)]
        rrep = [_rrep[:, p * CH:(p + 1) * CH] for p in range(2)]
        mrrep = [_mrrep[:, p * CH:(p + 1) * CH] for p in range(2)]
        ball = [_ball[:, :], _ball[:, :]]
        call = [_call[:, :], _call[:, :]]
        u1 = [_u1[:, p * N * CH:(p + 1) * N * CH] for p in range(2)]
        u2b = [_u2[:, p * N * CH:(p + 1) * N * CH] for p in range(2)]
        u3b = [_u3[:, p * N * CH:(p + 1) * N * CH] for p in range(2)]
        y2t = [_y2t[:, t * CH:(t + 1) * CH] for t in range(4)]
        lyv = [lyev[:, p * CH:(p + 1) * CH] for p in range(2)]
        lnv = [[lnev[:, (p * 2 + j) * CH:(p * 2 + j + 1) * CH] for j in range(2)]
               for p in range(2)]

        for fn in (
            lambda e: e.dma_start(out=alog_s[:, :], in_=alog),
            lambda e: e.dma_start(out=dcol_s[:, :], in_=dcol),
            lambda e: e.dma_start(out=dtw_s[:, :], in_=dtwT),
            lambda e: e.dma_start(out=dtb_s[:, :], in_=dtbc),
            lambda e: e.dma_start(out=wct_s[:, :], in_=wctT),
            lambda e: e.dma_start(out=wsc_s[:, :], in_=wsc),
            lambda e: e.dma_start(out=wsb_s[:, :], in_=wsb),
        ):
            S.run("sync", fn, "LD", 16)
        ld_once = S.cnt["LD"]
        S.wait("scalar", "LD", ld_once)
        S.run("scalar", lambda e: e.activation(
            out=aneg[:, :], in_=alog_s[:, :], func=AF.Exp), "SS")
        ss_aexp = S.cnt["SS"]
        S.wait("vector", "SS", ss_aexp)
        S.run("vector", lambda e: e.tensor_scalar(
            out=aneg[:, :], in0=aneg[:, :], scalar1=-1.0, scalar2=None,
            op0=ALU.mult), "VS")
        S.wait("vector", "LD", ld_once)
        S.run("vector", lambda e: e.tensor_scalar(
            out=nwsc[:, :], in0=wsc_s[:, :], scalar1=-1.0, scalar2=None,
            op0=ALU.mult), "VS")
        S.run("vector", lambda e: e.tensor_scalar(
            out=nwsb[:, :], in0=wsb_s[:, :], scalar1=-1.0, scalar2=None,
            op0=ALU.mult), "VS")
        vs_pre = S.cnt["VS"]
        S.wait("scalar", "VS", vs_pre)
        S.wait("tensor", "LD", ld_once)

        cons_vs = [0, 0]
        cons_ball = [0, 0]
        cons_u1 = [0, 0]
        ss_dtz_sp = [0, 0]
        ts_sq_use = 0
        vs_evac_prev = 0
        st_y2 = [0] * 4
        st_misc = 0
        pool_self = 0

        ld_chunk_p = [0, 0]
        ball_ps_gate = 0
        ball_vs_gate = 0

        def emit_loads(ci2):
            par2 = ci2 % 2
            c02 = ci2 * CH
            cols2 = slice(c02, c02 + CH)
            S.wait("sync", "VS", cons_vs[par2])
            for t in range(4):
                S.run("sync", lambda e, t=t, par2=par2, cols2=cols2: e.dma_start(
                    out=xd[par2][t], in_=xdT[t * 128:(t + 1) * 128, cols2]), "LD", 16)
            S.run("sync", lambda e, par2=par2, cols2=cols2: e.dma_start(
                out=dti[par2], in_=xp[0:128, cols2]), "LD", 16)
            S.run("sync", lambda e, par2=par2, cols2=cols2: e.dma_start(
                out=bc[par2], in_=xp[128:160, cols2]), "LD", 16)
            S.run("sync", lambda e, par2=par2, cols2=cols2: e.dma_start(
                out=mrep[par2][:, :], in_=st1[0:1, cols2].broadcast_to([128, CH])),
                "LD", 16)
            S.run("sync", lambda e, par2=par2, cols2=cols2: e.dma_start(
                out=rrep[par2][:, :], in_=st1[1:2, cols2].broadcast_to([128, CH])),
                "LD", 16)
            S.run("sync", lambda e, par2=par2, cols2=cols2: e.dma_start(
                out=mrrep[par2][:, :], in_=st1[2:3, cols2].broadcast_to([128, CH])),
                "LD", 16)
            ld_chunk_p[par2] = S.cnt["LD"]

        emit_loads(0)
        for ci in range(NCH):
            par = ci % 2
            cb = ci % CHB
            c0 = ci * CH
            cols = slice(c0, c0 + CH)
            ld_chunk = ld_chunk_p[par]
            S.wait("vector", "LD", ld_chunk)
            S.run("vector", lambda e, par=par: e.tensor_tensor(
                out=t256a[:, :], in0=dti[par], in1=rrep[par], op=ALU.mult), "VS")
            S.wait("vector", "VS", S.cnt["VS"])
            S.run("vector", lambda e, par=par: e.scalar_tensor_tensor(
                out=dta[par], in0=mrrep[par], scalar=nwsc[:, 0:1], in1=t256a[:, :],
                op0=ALU.mult, op1=ALU.add), "VS")
            vs_dta = S.cnt["VS"]
            S.run("vector", lambda e, par=par: e.tensor_tensor(
                out=t256b[:, :], in0=bc[par], in1=rrep[par][0:32, :], op=ALU.mult), "VS")
            S.wait("vector", "VS", S.cnt["VS"])
            S.run("vector", lambda e, par=par: e.scalar_tensor_tensor(
                out=bca[par], in0=mrrep[par][0:32, :], scalar=nwsb[:, 0:1],
                in1=t256b[:, :], op0=ALU.mult, op1=ALU.add), "VS")
            vs_bca = S.cnt["VS"]
            vs_xln = [0] * 4
            for t in range(4):
                S.run("vector", lambda e, t=t, par=par: e.tensor_tensor(
                    out=xln[t], in0=xd[par][t], in1=rrep[par], op=ALU.mult), "VS")
                S.wait("vector", "VS", S.cnt["VS"])
                S.run("vector", lambda e, t=t, par=par: e.tensor_tensor(
                    out=xln[t], in0=xln[t], in1=mrrep[par], op=ALU.subtract), "VS")
                vs_xln[t] = S.cnt["VS"]

            S.wait("sync", "VS", vs_bca)
            S.run("sync", lambda e, par=par: e.dma_start(
                out=bcd[par], in_=bca[par]), "LD", 16)
            S.wait("sync", "LD", S.cnt["LD"])
            S.wait("sync", "PS", ball_ps_gate)
            S.wait("sync", "VS", ball_vs_gate)
            S.run("sync", lambda e, par=par: e.dma_start(
                out=ball[par].rearrange("p (n t) -> p n t", n=N),
                in_=bcd[par][0:N, :].rearrange("n (o t) -> o n t", o=1)
                    .broadcast_to([128, N, CH])), "LD", 16)
            S.run("sync", lambda e, par=par: e.dma_start(
                out=call[par].rearrange("p (n t) -> p n t", n=N),
                in_=bcd[par][N:2 * N, :].rearrange("n (o t) -> o n t", o=1)
                    .broadcast_to([128, N, CH])), "LD", 16)
            ld_bcast = S.cnt["LD"]

            vs_w4 = [0] * 4
            for t in range(4):
                pz = pdtz[t % 2]
                S.wait("tensor", "SS", ss_dtz_sp[t % 2])
                S.wait("tensor", "VS", vs_dta)
                S.run("tensor", lambda e, t=t, pz=pz, par=par: e.matmul(
                    pz[:, :], dtw_s[:, t * 128:(t + 1) * 128], dta[par],
                    start=True, stop=True), "TS")
                ts_dtz = S.cnt["TS"]
                S.wait("scalar", "TS", ts_dtz)
                S.run("scalar", lambda e, t=t, pz=pz: e.activation(
                    out=sptmp[:, :], in_=pz[:, :], func=AF.Exp,
                    bias=dtb_s[:, t:t + 1]), "SS")
                S.wait("scalar", "SS", S.cnt["SS"])
                S.run("scalar", lambda e, t=t: e.activation(
                    out=dt4[t], in_=sptmp[:, :], func=AF.Ln, bias=1.0), "SS")
                ss_dtz_sp[t % 2] = S.cnt["SS"]
                ss_sp = S.cnt["SS"]
                S.wait("vector", "SS", ss_sp)
                S.wait("vector", "VS", vs_xln[t])
                S.run("vector", lambda e, t=t: e.tensor_tensor(
                    out=w4[t], in0=dt4[t], in1=xln[t], op=ALU.mult), "VS")
                vs_w4[t] = S.cnt["VS"]
                S.wait("vector", "VS", vs_w4[t])

            vs_tts = [0] * 4
            vs_red = [0] * 4
            z_done = [0] * 4
            carry_done = [0] * 4
            ss_dA = [0] * 4
            for t in range(4):
                u2 = u2b[t % 2]
                u3 = u3b[t % 2]
                up = u1[t % 2]
                vs_w = vs_w4[t]
                S.wait("scalar", "VS", cons_u1[t % 2])
                for n in range(N):
                    S.run("scalar", lambda e, t=t, n=n, up=up: e.activation(
                        out=up[:, n * CH:(n + 1) * CH], in_=dt4[t], func=AF.Exp,
                        scale=aneg[:, t * N + n:t * N + n + 1]), "SS")
                ss_dA[t] = S.cnt["SS"]

                uv = up.rearrange("p (n t) -> p n t", n=N)
                u2v = u2.rearrange("p (n t) -> p n t", n=N)
                S.wait("gpsimd", "VS", vs_red[t - 2] if t >= 2 else
                       (vs_red_last2[t] if ci > 0 else 0))
                S.wait("gpsimd", "LD", ld_bcast)
                S.wait("gpsimd", "VS", vs_w)
                S.run("gpsimd", lambda e, t=t, par=par, u2v=u2v: e.tensor_tensor(
                    out=u2v,
                    in0=w4[t].rearrange("p (o t) -> p o t", o=1)
                        .broadcast_to([128, N, CH]),
                    in1=ball[par].rearrange("p (n t) -> p n t", n=N),
                    op=ALU.mult), "PS")
                S.wait("gpsimd", "PS", S.cnt["PS"])
                S.wait("gpsimd", "SS", ss_dA[t])
                if cb > 0:
                    S.run("gpsimd", lambda e, t=t, uv=uv: e.tensor_tensor(
                        out=inj[:, :], in0=uv[:, :, 0],
                        in1=carry[:, t * N:(t + 1) * N], op=ALU.mult), "PS")
                    S.wait("gpsimd", "PS", S.cnt["PS"])
                S.run("gpsimd", lambda e, uv=uv: e.memset(uv[:, :, 0:1], 0.0), "PS")
                S.wait("gpsimd", "PS", S.cnt["PS"])
                if cb > 0:
                    S.run("gpsimd", lambda e, u2v=u2v: e.tensor_tensor(
                        out=u2v[:, :, 0], in0=u2v[:, :, 0],
                        in1=inj[:, :], op=ALU.add), "PS")
                    S.wait("gpsimd", "PS", S.cnt["PS"])
                ps_dbx = S.cnt["PS"]
                if t >= 1:
                    tc = t - 1
                    S.wait("gpsimd", "VS", vs_tts[tc])
                    S.run("gpsimd", lambda e, tc=tc: e.tensor_copy(
                        out=carry[:, tc * N:(tc + 1) * N],
                        in_=u3b[tc % 2].rearrange("p (n t) -> p n t", n=N)[:, :, CH - 1]),
                        "PS")
                    carry_done[tc] = S.cnt["PS"]
                    S.wait("gpsimd", "PS", S.cnt["PS"])

                S.wait("vector", "SS", ss_dA[t])
                S.wait("vector", "PS", ps_dbx)
                if t >= 2:
                    S.wait("vector", "VS", z_done[t - 2])
                    S.wait("vector", "PS", carry_done[t - 2])
                elif ci > 0:
                    S.wait("vector", "VS", z_last2[t])
                    S.wait("vector", "PS", carry_last2[t])
                S.run("vector", lambda e, up=up, u2=u2, u3=u3: e.tensor_tensor_scan(
                    out=u3[:, :], data0=up[:, :], data1=u2[:, :], initial=0.0,
                    op0=ALU.mult, op1=ALU.add), "VS")
                vs_tts[t] = S.cnt["VS"]
                cons_u1[t % 2] = vs_tts[t]
                S.wait("vector", "VS", vs_tts[t])
                S.run("vector", lambda e, t=t, par=par, u2=u2, u3=u3:
                      e.tensor_tensor(
                          out=u2.rearrange("p (c n) -> p n c", n=N),
                          in0=u3.rearrange("p (n c) -> p n c", n=N),
                          in1=call[par].rearrange("p (n c) -> p n c", n=N),
                          op=ALU.mult), "VS")
                z_done[t] = S.cnt["VS"]
                S.wait("vector", "VS", z_done[t])
                S.wait("vector", "LD", st_y2[t])
                S.run("vector", lambda e, t=t: e.tensor_reduce(
                    out=y2t[t], in_=u2b[t % 2].rearrange("p (c n) -> p c n", n=N),
                    axis=mybir.AxisListType.X, op=ALU.add), "VS")
                S.wait("vector", "VS", S.cnt["VS"])
                S.run("vector", lambda e, t=t: e.scalar_tensor_tensor(
                    out=y2t[t], in0=xln[t], scalar=dcol_s[:, t:t + 1],
                    in1=y2t[t], op0=ALU.mult, op1=ALU.add), "VS")
                vs_red[t] = S.cnt["VS"]
                S.wait("vector", "VS", vs_red[t])
            S.wait("gpsimd", "VS", vs_tts[3])
            S.run("gpsimd", lambda e: e.tensor_copy(
                out=carry[:, 3 * N:4 * N],
                in_=u3b[1].rearrange("p (n t) -> p n t", n=N)[:, :, CH - 1]), "PS")
            carry_done[3] = S.cnt["PS"]
            S.wait("gpsimd", "PS", S.cnt["PS"])
            vs_red_last2 = [vs_red[2], vs_red[3], vs_red[2], vs_red[3]]
            z_last2 = [z_done[2], z_done[3], z_done[2], z_done[3]]
            carry_last2 = [carry_done[2], carry_done[3], carry_done[2], carry_done[3]]
            cons_ball_ps = S.cnt["PS"]
            cons_ball_vs = S.cnt["VS"]

            cons_vs[par] = S.cnt["VS"]
            ball_ps_gate = cons_ball_ps
            ball_vs_gate = cons_ball_vs
            if ci + 1 < NCH:
                emit_loads(ci + 1)

            S.wait("tensor", "VS", vs_evac_prev)
            for t in range(4):
                S.wait("scalar", "VS", vs_red[t])
                S.wait("scalar", "TS", ts_sq_use)
                S.run("scalar", lambda e, t=t: e.activation(
                    out=ysq[:, :], in_=y2t[t], func=AF.Square), "SS")
                ss_sq = S.cnt["SS"]
                S.wait("tensor", "VS", vs_red[t])
                S.run("tensor", lambda e, t=t: e.matmul(
                    pln1[:, :], csts[:, 1:2], y2t[t],
                    start=(t == 0), stop=(t == 3)), "TS" if t == 3 else None)
                S.wait("tensor", "SS", ss_sq)
                S.run("tensor", lambda e, t=t: e.matmul(
                    pln2[:, :], csts[:, 1:2], ysq[:, :],
                    start=(t == 0), stop=(t == 3)), "TS")
                ts_sq_use = S.cnt["TS"]
                S.run("tensor", lambda e, t=t: e.matmul(
                    ply[:, :], wct_s[:, t * 24:(t + 1) * 24], y2t[t],
                    start=(t == 0), stop=(t == 3)), "TS" if t == 3 else None)
            ts_stats = S.cnt["TS"]
            S.wait("vector", "TS", ts_stats)
            S.wait("vector", "LD", st_misc)
            S.run("vector", lambda e, par=par: e.tensor_copy(
                out=lnv[par][0], in_=pln1[:, :]), "VS")
            S.run("vector", lambda e, par=par: e.tensor_copy(
                out=lnv[par][1], in_=pln2[:, :]), "VS")
            S.run("vector", lambda e, par=par: e.tensor_copy(
                out=lyv[par], in_=ply[:, :]), "VS")
            vs_evac = S.cnt["VS"]
            vs_evac_prev = vs_evac
            for t in range(4):
                S.wait("sync", "VS", vs_red[t])
                S.run("sync", lambda e, t=t, cols=cols: e.dma_start(
                    out=o_y2[t * 128:(t + 1) * 128, cols], in_=y2t[t]), "LD", 16)
                st_y2[t] = S.cnt["LD"]
            S.wait("sync", "VS", vs_evac)
            S.run("sync", lambda e, par=par, cols=cols: e.dma_start(
                out=o_lnp[0:1, cols], in_=lnv[par][0]), "LD", 16)
            S.run("sync", lambda e, par=par, cols=cols: e.dma_start(
                out=o_lnp[1:2, cols], in_=lnv[par][1]), "LD", 16)
            S.run("sync", lambda e, par=par, cols=cols: e.dma_start(
                out=o_ly[:, cols], in_=lyv[par]), "LD", 16)
            st_misc = S.cnt["LD"]

        S.wait("sync", "LD", S.cnt["LD"])
        S.wait("sync", "GD", S.cnt["GD"])

        S.emit(block, sems)
    return nc


_L2 = None


def _get_l2():
    global _L2
    if _L2 is None:
        _L2 = build_l2()
    return _L2


def run_l2(x2d, xp_full, st1, A_log, Dv, dt_proj_w, dt_proj_b, wcat, wsum):
    nc = _get_l2()
    in_maps = []
    for c in range(NCORES):
        sl = slice(c * DC, (c + 1) * DC)
        in_maps.append(dict(
            xdT=np.ascontiguousarray(x2d[:, sl].T),
            xp=xp_full, st1=st1,
            alog=np.ascontiguousarray(
                A_log[sl].reshape(4, 128, N).transpose(1, 0, 2).reshape(128, 4 * N)),
            dcol=np.ascontiguousarray(Dv[sl].reshape(4, 128).T),
            dtwT=np.ascontiguousarray(dt_proj_w[sl].T),
            dtbc=np.ascontiguousarray(dt_proj_b[sl].reshape(4, 128).T),
            wctT=np.ascontiguousarray(
                wcat[:, sl].T.reshape(4, 128, 24).transpose(1, 0, 2).reshape(128, 96)),
            wsc=np.ascontiguousarray(wsum[0:128].reshape(128, 1)),
            wsb=np.ascontiguousarray(wsum[128:160].reshape(32, 1)),
        ))
    res = run_bass_kernel_spmd(nc, in_maps, core_ids=list(range(NCORES)))
    return res.results


def build_l3():
    nc = bass.Bass("TRN2", target_bir_lowering=False)
    xtok = nc.dram_tensor("xtok", [TOK, D], F32, kind="ExternalInput").ap()
    y2tok = nc.dram_tensor("y2tok", [TOK, D], F32, kind="ExternalInput").ap()
    lnps = nc.dram_tensor("lnps", [128, 16 * 8], F32, kind="ExternalInput").ap()
    lyt = nc.dram_tensor("lyt", [128, 192 * 8], F32, kind="ExternalInput").ap()
    lxt = nc.dram_tensor("lxt", [128, 192], F32, kind="ExternalInput").ap()
    wsumf = nc.dram_tensor("wsumf", [1, 24], F32, kind="ExternalInput").ap()
    hrb = nc.dram_tensor("hrb", [1, 16], F32, kind="ExternalInput").ap()
    hpb = nc.dram_tensor("hpb", [1, 4], F32, kind="ExternalInput").ap()
    hob = nc.dram_tensor("hob", [1, 4], F32, kind="ExternalInput").ap()
    gatei = nc.dram_tensor("gatei", [1, 1], F32, kind="ExternalInput").ap()
    eye16 = nc.dram_tensor("eye16", [1, 16], F32, kind="ExternalInput").ap()
    ident = nc.dram_tensor("ident", [128, 128], F32, kind="ExternalInput").ap()
    wb = nc.dram_tensor("wb", [1024, 1024], F32, kind="ExternalInput").ap()

    o_x3 = nc.dram_tensor("x3", [TOK, D], F32, kind="ExternalOutput").ap()
    o_dbg = nc.dram_tensor("dbg", [128, 192 + 128 + 32 + 32 + 48], F32,
                           kind="ExternalOutput").ap()

    NTT = TOK // 128

    with ExitStack() as ctx:
        def sb(name, shape, dt=F32):
            return ctx.enter_context(nc.sbuf_tensor(name, shape, dt))

        _xt = sb("xt", [128, 2 * D])
        _yt = sb("yt", [128, 2 * D])
        x2t = sb("x2t", [128, D])
        xint = sb("xint", [128, D])
        xmix = sb("xmix", [128, D])
        xpre32 = sb("xpre32", [128, 1024])
        xpreb = sb("xpreb", [128, 1024], BF16)
        xpT = sb("xpT", [128, 1024], BF16)
        yb = sb("yb", [128, 1024])
        _wbb = sb("wbb", [128, 8 * 1024], BF16)
        idn = sb("idn", [128, 128], BF16)
        lnps_s = sb("lnps_s", [128, 16 * 8])
        sc = sb("sc", [128, 16])
        lyt_s = sb("lyt_s", [128, 192 * 8])
        lyc = sb("lyc", [128, 192])
        lxt_s = sb("lxt_s", [128, 192])
        lgt = sb("lgt", [128, 192])
        zt = sb("zt", [128, 128])
        at = sb("at", [128, 128])
        pe_ = sb("pe_", [128, 128])
        ht = sb("ht", [128, 128])
        spre = sb("spre", [128, 32])
        spost = sb("spost", [128, 32])
        hpre = sb("hpre", [128, 32])
        hpost = sb("hpost", [128, 32])
        ev = sb("ev", [128, 32])
        eu = sb("eu", [128, 32])
        rs = sb("rs", [128, 32])
        zm = sb("zm", [128, 8])
        mu2c = sb("mu2c", [128, 8])
        r2c = sb("r2c", [128, 8])
        nmu2r2c = sb("nmu2r2c", [128, 8])
        spc = sb("spc", [128, 8])
        sqc = sb("sqc", [128, 8])
        mupc = sb("mupc", [128, 8])
        nmupc = sb("nmupc", [128, 8])
        varpc = sb("varpc", [128, 8])
        rpc = sb("rpc", [128, 8])
        alphac = sb("alphac", [128, 8])
        alphatc = sb("alphatc", [128, 8])
        t8a = sb("t8a", [128, 8])
        t8b = sb("t8b", [128, 8])
        s3c = sb("s3c", [128, 1])
        q3c = sb("q3c", [128, 1])
        mu3c = sb("mu3c", [128, 1])
        r3c = sb("r3c", [128, 1])
        nmu3r3c = sb("nmu3r3c", [128, 1])
        t1c = sb("t1c", [128, 1])
        scrb = sb("scrb", [128, D], BF16)
        wsfrep = sb("wsfrep", [128, 24])
        hrbrep = sb("hrbrep", [128, 16])
        hpbrep = sb("hpbrep", [128, 4])
        hobrep = sb("hobrep", [128, 4])
        eyerep = sb("eyerep", [128, 16])
        igrep = sb("igrep", [128, 16])
        gcol = sb("gcol", [128, 1])
        omgc = sb("omgc", [128, 1])
        csts = sb("csts", [128, 4])
        ptb = ctx.enter_context(nc.psum_tensor("ptb", [128, 128], BF16))
        py = [ctx.enter_context(nc.psum_tensor(f"py{i}", [128, 512], F32))
              for i in range(2)]
        sems = {n: ctx.enter_context(nc.semaphore(n)) for n in SEMNAMES}
        block = ctx.enter_context(nc.Block())

        S = Sched()
        S.sems(SEMNAMES)
        for j, v in enumerate(CONSTS):
            nc.const_aps.register(F32, v, csts[:, j:j + 1])
        for j, v in enumerate(CONSTS):
            S.run("vector", lambda e, j=j, v=v: e.memset(csts[:, j:j + 1], v), "VS")
        S.wait("scalar", "VS", S.cnt["VS"])

        xt = [_xt[:, (i % 2) * D:((i % 2) + 1) * D] for i in range(NTT)]
        yt = [_yt[:, (i % 2) * D:((i % 2) + 1) * D] for i in range(NTT)]
        wbb = [_wbb[:, i * 1024:(i + 1) * 1024] for i in range(8)]

        for fn in (
            lambda e: e.dma_start(out=lnps_s[:, :], in_=lnps),
            lambda e: e.dma_start(out=lyt_s[:, :], in_=lyt),
            lambda e: e.dma_start(out=lxt_s[:, :], in_=lxt),
        ):
            S.run("sync", fn, "LD", 16)
        for fn in (
            lambda e: e.dma_start(out=wsfrep[:, :], in_=wsumf.broadcast_to([128, 24])),
            lambda e: e.dma_start(out=hrbrep[:, :], in_=hrb.broadcast_to([128, 16])),
            lambda e: e.dma_start(out=hpbrep[:, :], in_=hpb.broadcast_to([128, 4])),
            lambda e: e.dma_start(out=hobrep[:, :], in_=hob.broadcast_to([128, 4])),
            lambda e: e.dma_start(out=eyerep[:, :], in_=eye16.broadcast_to([128, 16])),
            lambda e: e.dma_start(out=gcol[:, :], in_=gatei.broadcast_to([128, 1])),
            lambda e: e.dma_start(out=idn[:, :], in_=ident),
        ):
            S.run("gpsimd", fn, "GD", 16)
        for i in range(8):
            S.run("gpsimd", lambda e, i=i: e.dma_start(
                out=wbb[i], in_=wb[i * 128:(i + 1) * 128, :]), "GD", 16)
        ld_once = S.cnt["LD"]
        gd_once = S.cnt["GD"]
        S.wait("vector", "GD", gd_once)
        S.run("vector", lambda e: e.tensor_scalar(
            out=hrbrep[:, :], in0=hrbrep[:, :], scalar1=1.0 / TAU, scalar2=None,
            op0=ALU.mult), "VS")
        vs_hrb = S.cnt["VS"]

        S.wait("vector", "LD", ld_once)
        S.run("vector", lambda e: e.tensor_reduce(
            out=sc[:, :], in_=lnps_s.rearrange("p (r c) -> p r c", c=8),
            axis=mybir.AxisListType.X, op=ALU.add), "VS")
        S.run("vector", lambda e: e.tensor_reduce(
            out=lyc[:, :], in_=lyt_s.rearrange("p (r c) -> p r c", c=8),
            axis=mybir.AxisListType.X, op=ALU.add), "VS")
        S.wait("vector", "VS", S.cnt["VS"])
        sview = sc.rearrange("p (t r) -> p t r", r=2)
        S.run("vector", lambda e: e.tensor_scalar(
            out=mu2c[:, :], in0=sview[:, :, 0], scalar1=1.0 / D, scalar2=None,
            op0=ALU.mult), "VS")
        S.run("vector", lambda e: e.tensor_scalar(
            out=t8a[:, :], in0=sview[:, :, 1], scalar1=1.0 / D, scalar2=None,
            op0=ALU.mult), "VS")
        S.wait("vector", "VS", S.cnt["VS"])
        S.run("vector", lambda e: e.tensor_tensor(
            out=t8b[:, :], in0=mu2c[:, :], in1=mu2c[:, :], op=ALU.mult), "VS")
        S.wait("vector", "VS", S.cnt["VS"])
        S.run("vector", lambda e: e.tensor_tensor(
            out=t8a[:, :], in0=t8a[:, :], in1=t8b[:, :], op=ALU.subtract), "VS")
        vs_var2 = S.cnt["VS"]
        S.wait("vector", "VS", vs_var2)
        S.wait("scalar", "VS", vs_var2)
        S.run("scalar", lambda e: e.activation(
            out=t8b[:, :], in_=t8a[:, :], func=AF.Sqrt, bias=LN_EPS), "SS")
        ss_sd2 = S.cnt["SS"]
        S.wait("vector", "SS", ss_sd2)
        S.run("vector", lambda e: e.reciprocal(out=r2c[:, :], in_=t8b[:, :]), "VS")
        S.wait("vector", "VS", S.cnt["VS"])
        S.run("vector", lambda e: e.tensor_tensor(
            out=nmu2r2c[:, :], in0=mu2c[:, :], in1=r2c[:, :], op=ALU.mult), "VS")
        S.wait("vector", "VS", S.cnt["VS"])
        S.run("vector", lambda e: e.tensor_scalar(
            out=nmu2r2c[:, :], in0=nmu2r2c[:, :], scalar1=-1.0, scalar2=None,
            op0=ALU.mult), "VS")
        vs_ln2 = S.cnt["VS"]
        S.wait("vector", "VS", vs_ln2)

        ldA = [0] * NTT
        ssA = [0] * NTT
        vsA = [0] * NTT

        def loadA(tl):
            S.run("sync", lambda e, tl=tl: e.dma_start(
                out=xt[tl], in_=xtok[tl * 128:(tl + 1) * 128, :]), "LD", 16)
            S.run("sync", lambda e, tl=tl: e.dma_start(
                out=yt[tl], in_=y2tok[tl * 128:(tl + 1) * 128, :]), "LD", 16)
            ldA[tl] = S.cnt["LD"]

        loadA(0)
        loadA(1)
        for tl in range(NTT):
            S.wait("vector", "LD", ldA[tl])
            if tl >= 1:
                S.wait("vector", "SS", ssA[tl - 1])
            S.run("vector", lambda e, tl=tl: e.tensor_scalar(
                out=x2t[:, :], in0=yt[tl], scalar1=r2c[:, tl:tl + 1],
                scalar2=nmu2r2c[:, tl:tl + 1], op0=ALU.mult, op1=ALU.add), "VS")
            S.wait("vector", "VS", S.cnt["VS"])
            S.run("vector", lambda e, tl=tl: e.tensor_tensor(
                out=x2t[:, :], in0=x2t[:, :], in1=xt[tl], op=ALU.add), "VS")
            vsA[tl] = S.cnt["VS"]
            S.wait("scalar", "VS", vsA[tl])
            S.run("scalar", lambda e, tl=tl: e.activation(
                out=scrb[:, :], in_=x2t[:, :], func=AF.Copy, bias=0.0,
                accum_out=spc[:, tl:tl + 1]), "SS")
            S.wait("scalar", "SS", S.cnt["SS"])
            S.run("scalar", lambda e, tl=tl: e.activation(
                out=scrb[:, :], in_=x2t[:, :], func=AF.Square,
                accum_out=sqc[:, tl:tl + 1]), "SS")
            ssA[tl] = S.cnt["SS"]
            S.wait("scalar", "SS", ssA[tl])
            if tl + 2 < NTT:
                S.wait("sync", "VS", vsA[tl])
                loadA(tl + 2)
        ss_passa = S.cnt["SS"]

        S.wait("vector", "SS", ss_passa)
        S.run("vector", lambda e: e.tensor_scalar(
            out=mupc[:, :], in0=spc[:, :], scalar1=1.0 / D, scalar2=None,
            op0=ALU.mult), "VS")
        S.run("vector", lambda e: e.tensor_scalar(
            out=varpc[:, :], in0=sqc[:, :], scalar1=1.0 / D, scalar2=None,
            op0=ALU.mult), "VS")
        S.wait("vector", "VS", S.cnt["VS"])
        S.run("vector", lambda e: e.tensor_scalar(
            out=nmupc[:, :], in0=mupc[:, :], scalar1=-1.0, scalar2=None,
            op0=ALU.mult), "VS")
        S.run("vector", lambda e: e.tensor_tensor(
            out=t8a[:, :], in0=mupc[:, :], in1=mupc[:, :], op=ALU.mult), "VS")
        S.wait("vector", "VS", S.cnt["VS"])
        S.run("vector", lambda e: e.tensor_tensor(
            out=varpc[:, :], in0=varpc[:, :], in1=t8a[:, :], op=ALU.subtract), "VS")
        S.wait("vector", "VS", S.cnt["VS"])
        S.run("vector", lambda e: e.tensor_scalar(
            out=t8a[:, :], in0=varpc[:, :], scalar1=LN_EPS, scalar2=None,
            op0=ALU.add), "VS")
        vs_q = S.cnt["VS"]
        S.wait("scalar", "VS", vs_q)
        S.run("scalar", lambda e: e.activation(
            out=t8b[:, :], in_=t8a[:, :], func=AF.Sqrt), "SS")
        S.wait("vector", "SS", S.cnt["SS"])
        S.run("vector", lambda e: e.reciprocal(out=rpc[:, :], in_=t8b[:, :]), "VS")
        S.wait("vector", "VS", S.cnt["VS"])
        S.run("vector", lambda e: e.reciprocal(out=t8b[:, :], in_=t8a[:, :]), "VS")
        S.wait("vector", "VS", S.cnt["VS"])
        S.run("vector", lambda e: e.tensor_tensor(
            out=t8b[:, :], in0=varpc[:, :], in1=t8b[:, :], op=ALU.mult), "VS")
        S.wait("vector", "VS", S.cnt["VS"])
        S.run("vector", lambda e: e.tensor_scalar(
            out=t8b[:, :], in0=t8b[:, :], scalar1=RMS_EPS, scalar2=None,
            op0=ALU.add), "VS")
        vs_fr = S.cnt["VS"]
        S.wait("scalar", "VS", vs_fr)
        S.run("scalar", lambda e: e.activation(
            out=t8b[:, :], in_=t8b[:, :], func=AF.Sqrt), "SS")
        S.wait("vector", "SS", S.cnt["SS"])
        S.run("vector", lambda e: e.reciprocal(out=t8a[:, :], in_=t8b[:, :]), "VS")
        S.wait("vector", "VS", S.cnt["VS"])
        S.run("vector", lambda e: e.tensor_tensor(
            out=alphac[:, :], in0=t8a[:, :], in1=rpc[:, :], op=ALU.mult), "VS")
        S.wait("vector", "VS", S.cnt["VS"])
        S.run("vector", lambda e: e.tensor_scalar(
            out=alphatc[:, :], in0=alphac[:, :], scalar1=1.0 / TAU, scalar2=None,
            op0=ALU.mult), "VS")
        vs_alpha = S.cnt["VS"]
        S.wait("vector", "VS", vs_alpha)
        S.wait("vector", "VS", vs_hrb)

        for tl in range(NTT):
            lysl = lyc[:, tl * 24:(tl + 1) * 24]
            lxsl = lxt_s[:, tl * 24:(tl + 1) * 24]
            lgsl = lgt[:, tl * 24:(tl + 1) * 24]
            S.run("vector", lambda e, lysl=lysl, lxsl=lxsl, lgsl=lgsl, tl=tl:
                  e.scalar_tensor_tensor(
                      out=lgsl, in0=lysl, scalar=r2c[:, tl:tl + 1], in1=lxsl,
                      op0=ALU.mult, op1=ALU.add), "VS")
            S.wait("vector", "VS", S.cnt["VS"])
            S.run("vector", lambda e, lgsl=lgsl, tl=tl: e.scalar_tensor_tensor(
                out=lgsl, in0=wsfrep[:, :], scalar=nmu2r2c[:, tl:tl + 1],
                in1=lgsl, op0=ALU.mult, op1=ALU.add), "VS")
            S.wait("vector", "VS", S.cnt["VS"])
            S.run("vector", lambda e, lgsl=lgsl, tl=tl: e.scalar_tensor_tensor(
                out=lgsl, in0=wsfrep[:, :], scalar=nmupc[:, tl:tl + 1],
                in1=lgsl, op0=ALU.mult, op1=ALU.add), "VS")
            S.wait("vector", "VS", S.cnt["VS"])
            S.run("vector", lambda e, lgsl=lgsl, tl=tl: e.scalar_tensor_tensor(
                out=zt[:, tl * 16:(tl + 1) * 16], in0=lgsl[:, 0:16],
                scalar=alphatc[:, tl:tl + 1], in1=hrbrep[:, :],
                op0=ALU.mult, op1=ALU.add), "VS")
            S.run("vector", lambda e, lgsl=lgsl, tl=tl: e.scalar_tensor_tensor(
                out=spre[:, tl * 4:(tl + 1) * 4], in0=lgsl[:, 16:20],
                scalar=alphac[:, tl:tl + 1], in1=hpbrep[:, :],
                op0=ALU.mult, op1=ALU.add), "VS")
            S.run("vector", lambda e, lgsl=lgsl, tl=tl: e.scalar_tensor_tensor(
                out=spost[:, tl * 4:(tl + 1) * 4], in0=lgsl[:, 20:24],
                scalar=alphac[:, tl:tl + 1], in1=hobrep[:, :],
                op0=ALU.mult, op1=ALU.add), "VS")
            S.wait("vector", "VS", S.cnt["VS"])
        vs_zbuild = S.cnt["VS"]

        S.wait("vector", "VS", vs_zbuild)
        S.run("vector", lambda e: e.tensor_reduce(
            out=zm[:, :], in_=zt.rearrange("p (t k) -> p t k", k=16),
            axis=mybir.AxisListType.X, op=ALU.max), "VS")
        S.wait("vector", "VS", S.cnt["VS"])
        S.run("vector", lambda e: e.tensor_tensor(
            out=zt.rearrange("p (t k) -> p t k", k=16),
            in0=zt.rearrange("p (t k) -> p t k", k=16),
            in1=zm.rearrange("p (t o) -> p t o", o=1).broadcast_to([128, 8, 16]),
            op=ALU.subtract), "VS")
        vs_zshift = S.cnt["VS"]
        S.wait("vector", "VS", vs_zshift)

        zt_ij = zt.rearrange("p (t i j) -> p t i j", i=4, j=4)
        pe_ij = pe_.rearrange("p (t i j) -> p t i j", i=4, j=4)
        nu = eu
        nv = ev
        mx = rs
        sm_ = sb("sm_", [128, 32])
        S.run("vector", lambda e: e.memset(nv[:, :], 0.0), "VS")
        S.wait("vector", "VS", S.cnt["VS"])

        def half_step(bc_in, bc_dims, red_view, out_col):
            S.run("vector", lambda e: e.tensor_tensor(
                out=pe_ij, in0=zt_ij, in1=bc_in, op=ALU.subtract), "VS")
            S.wait("vector", "VS", S.cnt["VS"])
            S.run("vector", lambda e: e.tensor_reduce(
                out=mx[:, :], in_=red_view(pe_), axis=mybir.AxisListType.X,
                op=ALU.max), "VS")
            S.wait("vector", "VS", S.cnt["VS"])
            S.run("vector", lambda e: e.tensor_tensor(
                out=pe_ij, in0=pe_ij, in1=bc_dims(mx), op=ALU.subtract), "VS")
            vs_sub = S.cnt["VS"]
            S.wait("vector", "VS", vs_sub)
            S.wait("scalar", "VS", vs_sub)
            S.run("scalar", lambda e: e.activation(
                out=pe_[:, :], in_=pe_[:, :], func=AF.Exp), "SS")
            ss_e = S.cnt["SS"]
            S.wait("scalar", "SS", ss_e)
            S.wait("vector", "SS", ss_e)
            S.run("vector", lambda e: e.tensor_reduce(
                out=sm_[:, :], in_=red_view(pe_), axis=mybir.AxisListType.X,
                op=ALU.add), "VS")
            vs_s = S.cnt["VS"]
            S.wait("vector", "VS", vs_s)
            S.wait("scalar", "VS", vs_s)
            S.run("scalar", lambda e: e.activation(
                out=sm_[:, :], in_=sm_[:, :], func=AF.Ln), "SS")
            ss_l = S.cnt["SS"]
            S.wait("vector", "SS", ss_l)
            S.run("vector", lambda e: e.tensor_tensor(
                out=out_col[:, :], in0=mx[:, :], in1=sm_[:, :], op=ALU.add), "VS")
            S.wait("vector", "VS", S.cnt["VS"])

        bc_j_of = lambda col: col.rearrange(
            "p (t o j) -> p t o j", o=1, j=4).broadcast_to([128, 8, 4, 4])
        bc_i_of = lambda col: col.rearrange(
            "p (t i o) -> p t i o", o=1, i=4).broadcast_to([128, 8, 4, 4])
        bc_ti = lambda col: col.rearrange(
            "p (t i o) -> p t i o", o=1, i=4).broadcast_to([128, 8, 4, 4])
        def bc_tj(col):
            return col.rearrange("p (t j) -> p t j", j=4) \
                .rearrange("p (t o) j -> p t o j", o=1) \
                .broadcast_to([128, 8, 4, 4])
        red_j = lambda buf: buf.rearrange("p (ti j) -> p ti j", j=4)
        red_i = lambda buf: buf.rearrange("p (t i j) -> p t j i", i=4, j=4)

        for it in range(50):
            half_step(bc_tj(nv), bc_ti, red_j, nu)
            half_step(bc_ti(nu), bc_tj, red_i, nv)
        S.run("vector", lambda e: e.tensor_tensor(
            out=pe_ij, in0=zt_ij, in1=bc_ti(nu), op=ALU.subtract), "VS")
        S.wait("vector", "VS", S.cnt["VS"])
        S.run("vector", lambda e: e.tensor_tensor(
            out=pe_ij, in0=pe_ij, in1=bc_tj(nv), op=ALU.subtract), "VS")
        vs_parg = S.cnt["VS"]
        S.wait("scalar", "VS", vs_parg)
        S.run("scalar", lambda e: e.activation(
            out=pe_[:, :], in_=pe_[:, :], func=AF.Exp), "SS")
        ss_P = S.cnt["SS"]
        S.wait("vector", "SS", ss_P)
        vs_P = S.cnt["VS"]
        S.wait("scalar", "GD", gd_once)
        S.run("scalar", lambda e: e.activation(
            out=gcol[:, :], in_=gcol[:, :], func=AF.Sigmoid), "SS")
        ss_g = S.cnt["SS"]
        S.wait("vector", "SS", ss_g)
        S.run("vector", lambda e: e.tensor_scalar(
            out=omgc[:, :], in0=gcol[:, :], scalar1=-1.0, scalar2=1.0,
            op0=ALU.mult, op1=ALU.add), "VS")
        S.wait("vector", "VS", S.cnt["VS"])
        S.run("vector", lambda e: e.tensor_scalar(
            out=igrep[:, :], in0=eyerep[:, :], scalar1=omgc[:, 0:1], scalar2=None,
            op0=ALU.mult), "VS")
        S.wait("vector", "VS", S.cnt["VS"])
        S.run("vector", lambda e: e.scalar_tensor_tensor(
            out=ht.rearrange("p (t k) -> p t k", k=16),
            in0=pe_.rearrange("p (t k) -> p t k", k=16),
            scalar=gcol[:, 0:1],
            in1=igrep.rearrange("p (o k) -> p o k", o=1).broadcast_to([128, 8, 16]),
            op0=ALU.mult, op1=ALU.add), "VS")
        vs_H = S.cnt["VS"]
        S.wait("vector", "VS", vs_H)

        for (src, dst) in ((spre, hpre), (spost, hpost)):
            S.run("vector", lambda e, src=src: e.tensor_reduce(
                out=zm[:, :], in_=src.rearrange("p (t k) -> p t k", k=4),
                axis=mybir.AxisListType.X, op=ALU.max), "VS")
            S.wait("vector", "VS", S.cnt["VS"])
            S.run("vector", lambda e, src=src: e.tensor_tensor(
                out=src.rearrange("p (t k) -> p t k", k=4),
                in0=src.rearrange("p (t k) -> p t k", k=4),
                in1=zm.rearrange("p (t o) -> p t o", o=1).broadcast_to([128, 8, 4]),
                op=ALU.subtract), "VS")
            vs_sm = S.cnt["VS"]
            S.wait("scalar", "VS", vs_sm)
            S.run("scalar", lambda e, src=src: e.activation(
                out=src[:, :], in_=src[:, :], func=AF.Exp), "SS")
            ss_sm = S.cnt["SS"]
            S.wait("vector", "SS", ss_sm)
            S.run("vector", lambda e, src=src: e.tensor_reduce(
                out=zm[:, :], in_=src.rearrange("p (t k) -> p t k", k=4),
                axis=mybir.AxisListType.X, op=ALU.add), "VS")
            S.wait("vector", "VS", S.cnt["VS"])
            S.run("vector", lambda e: e.reciprocal(out=zm[:, :], in_=zm[:, :]), "VS")
            S.wait("vector", "VS", S.cnt["VS"])
            S.run("vector", lambda e, src=src, dst=dst: e.tensor_tensor(
                out=dst.rearrange("p (t k) -> p t k", k=4),
                in0=src.rearrange("p (t k) -> p t k", k=4),
                in1=zm.rearrange("p (t o) -> p t o", o=1).broadcast_to([128, 8, 4]),
                op=ALU.mult), "VS")
            S.wait("vector", "VS", S.cnt["VS"])
        vs_hs = S.cnt["VS"]

        S.wait("gpsimd", "VS", vs_hs)
        S.run("gpsimd", lambda e: e.dma_start(out=o_dbg[:, 0:192], in_=lgt[:, :]), "GD", 16)
        S.run("gpsimd", lambda e: e.dma_start(out=o_dbg[:, 192:320], in_=ht[:, :]), "GD", 16)
        S.run("gpsimd", lambda e: e.dma_start(out=o_dbg[:, 320:352], in_=hpre[:, :]), "GD", 16)
        S.run("gpsimd", lambda e: e.dma_start(out=o_dbg[:, 352:384], in_=hpost[:, :]), "GD", 16)
        S.run("gpsimd", lambda e: e.dma_start(out=o_dbg[:, 384:392], in_=mupc[:, :]), "GD", 16)
        S.run("gpsimd", lambda e: e.dma_start(out=o_dbg[:, 392:400], in_=varpc[:, :]), "GD", 16)
        S.run("gpsimd", lambda e: e.dma_start(out=o_dbg[:, 400:408], in_=alphac[:, :]), "GD", 16)
        S.run("gpsimd", lambda e: e.dma_start(out=o_dbg[:, 408:416], in_=r2c[:, :]), "GD", 16)
        S.run("gpsimd", lambda e: e.dma_start(out=o_dbg[:, 416:424], in_=mu2c[:, :]), "GD", 16)

        ldB = [0] * NTT
        vsB = [0] * NTT

        def loadB(tl, wait_vs):
            S.wait("sync", "VS", wait_vs)
            S.run("sync", lambda e, tl=tl: e.dma_start(
                out=xt[tl], in_=xtok[tl * 128:(tl + 1) * 128, :]), "LD", 16)
            S.run("sync", lambda e, tl=tl: e.dma_start(
                out=yt[tl], in_=y2tok[tl * 128:(tl + 1) * 128, :]), "LD", 16)
            ldB[tl] = S.cnt["LD"]

        loadB(0, vsA[NTT - 1])
        loadB(1, vsA[NTT - 1])
        ts_tr_prev = 0
        vs_xpT_evac = 0
        vs_yb_evac = 0
        gd_x3 = 0
        for tl in range(NTT):
            S.wait("vector", "LD", ldB[tl])
            S.wait("vector", "VS", vs_hs)
            S.wait("vector", "GD", gd_x3)
            S.run("vector", lambda e, tl=tl: e.tensor_scalar(
                out=x2t[:, :], in0=yt[tl], scalar1=r2c[:, tl:tl + 1],
                scalar2=nmu2r2c[:, tl:tl + 1], op0=ALU.mult, op1=ALU.add), "VS")
            S.wait("vector", "VS", S.cnt["VS"])
            S.run("vector", lambda e, tl=tl: e.tensor_tensor(
                out=x2t[:, :], in0=x2t[:, :], in1=xt[tl], op=ALU.add), "VS")
            vsB[tl] = S.cnt["VS"]
            S.wait("vector", "VS", vsB[tl])
            if tl + 2 < NTT:
                loadB(tl + 2, vsB[tl])
            S.run("vector", lambda e, tl=tl: e.tensor_scalar(
                out=xint[:, :], in0=x2t[:, :], scalar1=nmupc[:, tl:tl + 1],
                scalar2=rpc[:, tl:tl + 1], op0=ALU.add, op1=ALU.mult), "VS")
            S.wait("vector", "VS", S.cnt["VS"])
            for i in range(4):
                S.run("vector", lambda e, tl=tl, i=i: e.tensor_scalar(
                    out=xmix[:, i * 1024:(i + 1) * 1024],
                    in0=xint[:, 0:1024],
                    scalar1=ht[:, tl * 16 + i * 4:tl * 16 + i * 4 + 1],
                    scalar2=None, op0=ALU.mult), "VS")
            S.wait("vector", "VS", S.cnt["VS"])
            for j in range(1, 4):
                for i in range(4):
                    S.run("vector", lambda e, tl=tl, i=i, j=j: e.scalar_tensor_tensor(
                        out=xmix[:, i * 1024:(i + 1) * 1024],
                        in0=xint[:, j * 1024:(j + 1) * 1024],
                        scalar=ht[:, tl * 16 + i * 4 + j:tl * 16 + i * 4 + j + 1],
                        in1=xmix[:, i * 1024:(i + 1) * 1024],
                        op0=ALU.mult, op1=ALU.add), "VS")
                S.wait("vector", "VS", S.cnt["VS"])
            S.run("vector", lambda e, tl=tl: e.tensor_scalar(
                out=xpre32[:, :], in0=xint[:, 0:1024],
                scalar1=hpre[:, tl * 4:tl * 4 + 1], scalar2=None,
                op0=ALU.mult), "VS")
            S.wait("vector", "VS", S.cnt["VS"])
            for n in range(1, 4):
                S.run("vector", lambda e, tl=tl, n=n: e.scalar_tensor_tensor(
                    out=xpre32[:, :], in0=xint[:, n * 1024:(n + 1) * 1024],
                    scalar=hpre[:, tl * 4 + n:tl * 4 + n + 1], in1=xpre32[:, :],
                    op0=ALU.mult, op1=ALU.add), "VS")
                S.wait("vector", "VS", S.cnt["VS"])
            S.wait("vector", "TS", ts_tr_prev)
            S.run("vector", lambda e: e.tensor_copy(
                out=xpreb[:, :], in_=xpre32[:, :]), "VS")
            vs_xpreb = S.cnt["VS"]
            S.wait("vector", "VS", vs_xpreb)
            vs_ev = vs_xpT_evac
            for k in range(8):
                S.wait("tensor", "VS", vs_xpreb if k == 0 else vs_ev)
                S.run("tensor", lambda e, k=k: e.transpose(
                    ptb[:, :], xpreb[:, k * 128:(k + 1) * 128], idn[:, :]), "TS")
                ts_tr = S.cnt["TS"]
                S.wait("vector", "TS", ts_tr)
                S.run("vector", lambda e, k=k: e.tensor_copy(
                    out=xpT[:, k * 128:(k + 1) * 128], in_=ptb[:, :]), "VS")
                vs_ev = S.cnt["VS"]
                S.wait("vector", "VS", vs_ev)
            ts_tr_prev = S.cnt["TS"]
            vs_xpT_evac = vs_ev
            S.wait("tensor", "VS", vs_xpT_evac)
            S.wait("tensor", "GD", gd_once)
            S.wait("tensor", "VS", vs_yb_evac)
            for h in range(2):
                for k in range(8):
                    S.run("tensor", lambda e, k=k, h=h: e.matmul(
                        py[h][:, :], xpT[:, k * 128:(k + 1) * 128],
                        wbb[k][:, h * 512:(h + 1) * 512],
                        start=(k == 0), stop=(k == 7)),
                        "TS" if k == 7 else None)
            ts_mm = S.cnt["TS"]
            S.wait("vector", "TS", ts_mm)
            for h in range(2):
                S.run("vector", lambda e, h=h: e.tensor_copy(
                    out=yb[:, h * 512:(h + 1) * 512], in_=py[h][:, :]), "VS")
            S.wait("vector", "VS", S.cnt["VS"])
            vs_yb_evac = S.cnt["VS"]
            for n in range(4):
                S.run("vector", lambda e, tl=tl, n=n: e.scalar_tensor_tensor(
                    out=xmix[:, n * 1024:(n + 1) * 1024], in0=yb[:, :],
                    scalar=hpost[:, tl * 4 + n:tl * 4 + n + 1],
                    in1=xmix[:, n * 1024:(n + 1) * 1024],
                    op0=ALU.mult, op1=ALU.add), "VS")
            vs_ffn = S.cnt["VS"]
            S.wait("vector", "VS", vs_ffn)
            S.wait("scalar", "VS", vs_ffn)
            S.run("scalar", lambda e: e.activation(
                out=scrb[:, :], in_=xmix[:, :], func=AF.Copy, bias=0.0,
                accum_out=s3c[:, 0:1]), "SS")
            S.wait("scalar", "SS", S.cnt["SS"])
            S.run("scalar", lambda e: e.activation(
                out=scrb[:, :], in_=xmix[:, :], func=AF.Square,
                accum_out=q3c[:, 0:1]), "SS")
            ss_ln3 = S.cnt["SS"]
            S.wait("vector", "SS", ss_ln3)
            S.run("vector", lambda e: e.tensor_scalar(
                out=mu3c[:, :], in0=s3c[:, :], scalar1=1.0 / D, scalar2=None,
                op0=ALU.mult), "VS")
            S.run("vector", lambda e: e.tensor_scalar(
                out=t1c[:, :], in0=q3c[:, :], scalar1=1.0 / D, scalar2=None,
                op0=ALU.mult), "VS")
            S.wait("vector", "VS", S.cnt["VS"])
            S.run("vector", lambda e: e.tensor_tensor(
                out=r3c[:, :], in0=mu3c[:, :], in1=mu3c[:, :], op=ALU.mult), "VS")
            S.wait("vector", "VS", S.cnt["VS"])
            S.run("vector", lambda e: e.tensor_tensor(
                out=t1c[:, :], in0=t1c[:, :], in1=r3c[:, :], op=ALU.subtract), "VS")
            vs_v3 = S.cnt["VS"]
            S.wait("scalar", "VS", vs_v3)
            S.run("scalar", lambda e: e.activation(
                out=t1c[:, :], in_=t1c[:, :], func=AF.Sqrt, bias=LN_EPS), "SS")
            ss_sd3 = S.cnt["SS"]
            S.wait("vector", "SS", ss_sd3)
            S.run("vector", lambda e: e.reciprocal(out=r3c[:, :], in_=t1c[:, :]), "VS")
            S.wait("vector", "VS", S.cnt["VS"])
            S.run("vector", lambda e: e.tensor_tensor(
                out=nmu3r3c[:, :], in0=mu3c[:, :], in1=r3c[:, :], op=ALU.mult), "VS")
            S.wait("vector", "VS", S.cnt["VS"])
            S.run("vector", lambda e: e.tensor_scalar(
                out=nmu3r3c[:, :], in0=nmu3r3c[:, :], scalar1=-1.0, scalar2=None,
                op0=ALU.mult), "VS")
            S.wait("vector", "VS", S.cnt["VS"])
            S.wait("vector", "GD", gd_x3)
            S.run("vector", lambda e: e.tensor_scalar(
                out=xmix[:, :], in0=xmix[:, :], scalar1=r3c[:, 0:1],
                scalar2=nmu3r3c[:, 0:1], op0=ALU.mult, op1=ALU.add), "VS")
            S.wait("vector", "VS", S.cnt["VS"])
            S.run("vector", lambda e: e.tensor_tensor(
                out=x2t[:, :], in0=x2t[:, :], in1=xmix[:, :], op=ALU.add), "VS")
            vs_x3 = S.cnt["VS"]
            S.wait("gpsimd", "VS", vs_x3)
            S.run("gpsimd", lambda e, tl=tl: e.dma_start(
                out=o_x3[tl * 128:(tl + 1) * 128, :], in_=x2t[:, :]), "GD", 16)
            gd_x3 = S.cnt["GD"]
        S.wait("sync", "GD", S.cnt["GD"])

        S.emit(block, sems)
    return nc


_L3 = None


def _get_l3():
    global _L3
    if _L3 is None:
        _L3 = build_l3()
    return _L3


def run_l3(x2d, y2_full, lnp_parts, ly_parts, lx_parts, wsumf, hres_base,
           hpre_base, hpost_base, gate, wbr):
    nc = _get_l3()
    eye = np.eye(4, dtype=np.float32).reshape(1, 16)
    idn = np.eye(128, dtype=np.float32)
    lnp_st = np.stack(lnp_parts, axis=-1)
    ly_st = np.stack(ly_parts, axis=-1)
    in_maps = []
    for c in range(NCORES):
        tsl = slice(c * TOK, (c + 1) * TOK)
        lp = lnp_st[:, tsl, :].reshape(2, 8, 128, 8).transpose(2, 1, 0, 3) \
            .reshape(128, 16 * 8)
        lyp = ly_st[:, tsl, :].reshape(24, 8, 128, 8).transpose(2, 1, 0, 3) \
            .reshape(128, 192 * 8)
        lxp = lx_parts[c].reshape(24, 8, 128).transpose(2, 1, 0).reshape(128, 192)
        in_maps.append(dict(
            xtok=np.ascontiguousarray(x2d[tsl]),
            y2tok=np.ascontiguousarray(y2_full[:, tsl].T),
            lnps=np.ascontiguousarray(lp),
            lyt=np.ascontiguousarray(lyp),
            lxt=np.ascontiguousarray(lxp),
            wsumf=wsumf.reshape(1, 24),
            hrb=hres_base.reshape(1, 16),
            hpb=hpre_base.reshape(1, 4),
            hob=hpost_base.reshape(1, 4),
            gatei=gate.reshape(1, 1),
            eye16=eye,
            ident=idn,
            wb=wbr,
        ))
    res = run_bass_kernel_spmd(nc, in_maps, core_ids=list(range(NCORES)))
    return res.results


def kernel(L0_attn_x, L0_attn_A_log, L0_attn_D, L0_attn_x_proj_w,
           L0_attn_dt_proj_w, L0_attn_dt_proj_b, L0_ffn_W_res, L0_ffn_W_pre,
           L0_ffn_W_post, L0_ffn_H_res_base, L0_ffn_H_pre_base,
           L0_ffn_H_post_base, L0_ffn_gate, L0_ffn_W_branch):
    f32 = np.float32
    x2d = np.ascontiguousarray(np.asarray(L0_attn_x, f32).reshape(TT, D))
    xpw = np.asarray(L0_attn_x_proj_w, f32)
    wcat = np.ascontiguousarray(np.concatenate(
        [np.asarray(L0_ffn_W_res, f32), np.asarray(L0_ffn_W_pre, f32),
         np.asarray(L0_ffn_W_post, f32)], axis=0))

    r1 = run_l1(x2d, xpw, wcat)
    xp_full = np.ascontiguousarray(
        np.concatenate([r["xpraw"] for r in r1], axis=1))
    mean = np.concatenate([r["mcol"].T.reshape(-1) for r in r1])
    rstd = np.concatenate([r["rcol"].T.reshape(-1) for r in r1])
    mur = np.concatenate([r["mrcol"].T.reshape(-1) for r in r1])
    st1 = np.ascontiguousarray(np.stack([mean, rstd, mur]).astype(f32))
    wsum = r1[0]["wsums"][0, :R].astype(f32)
    wsumf = r1[0]["wsums"][0, R:].astype(f32)
    lx_parts = [r["lx"] for r in r1]

    r2 = run_l2(x2d, xp_full, st1, np.asarray(L0_attn_A_log, f32),
                np.asarray(L0_attn_D, f32), np.asarray(L0_attn_dt_proj_w, f32),
                np.asarray(L0_attn_dt_proj_b, f32), wcat, wsum)
    y2_full = np.concatenate([r["y2"] for r in r2], axis=0)
    lnp_parts = [r["lnp"] for r in r2]
    ly_parts = [r["ly"] for r in r2]

    r3 = run_l3(x2d, y2_full, lnp_parts, ly_parts, lx_parts, wsumf,
                np.asarray(L0_ffn_H_res_base, f32),
                np.asarray(L0_ffn_H_pre_base, f32),
                np.asarray(L0_ffn_H_post_base, f32),
                np.asarray(L0_ffn_gate, f32),
                np.ascontiguousarray(np.asarray(L0_ffn_W_branch, f32)))
    out = np.concatenate([r["x3"] for r in r3], axis=0)
    return out.reshape(B, L, D)


# version 06
# speedup vs baseline: 1.2409x; 1.0150x over previous
from contextlib import ExitStack

import numpy as np
import concourse.bass as bass
import concourse.mybir as mybir
from concourse.bass_utils import run_bass_kernel_spmd

F32 = mybir.dt.float32
BF16 = mybir.dt.bfloat16
AF = mybir.ActivationFunctionType
ALU = mybir.AluOpType

NCORES = 8
B, L, D = 2, 4096, 4096
TT = B * L
TOK = TT // NCORES
DC = D // NCORES
N = 16
R = 160
DTR = 128
CH = 256
NCH = TT // CH
CHB = L // CH
LN_EPS = 1e-5
RMS_EPS = 1.1920929e-07
TAU = 0.1


class Sched:

    ENGS = ("sync", "gpsimd", "tensor", "scalar", "vector")

    def __init__(self):
        self.ops = {e: [] for e in self.ENGS}
        self.cnt = {}

    def sems(self, names):
        for n in names:
            self.cnt[n] = 0

    def wait(self, eng, name, val):
        if val > 0:
            self.ops[eng].append(("w", name, val))

    def run(self, eng, fn, inc=None, k=1):
        self.ops[eng].append(("r", fn, inc, k))
        if inc is not None:
            self.cnt[inc] += k
            return self.cnt[inc]
        return None

    def emit(self, block, sems):
        def mk(eng_name):
            def body(e):
                for op in self.ops[eng_name]:
                    if op[0] == "w":
                        e.wait_ge(sems[op[1]], op[2])
                    else:
                        inst = op[1](e)
                        if op[2] is not None:
                            inst.then_inc(sems[op[2]], op[3])
            return body
        block.sync(mk("sync"))
        block.gpsimd(mk("gpsimd"))
        block.tensor(mk("tensor"))
        block.scalar(mk("scalar"))
        block.vector(mk("vector"))


SEMNAMES = ["LD", "GD", "TS", "SS", "VS", "ST", "PS"]
CONSTS = [0.0, 1.0, LN_EPS, RMS_EPS]


def build_l1():
    nc = bass.Bass("TRN2", target_bir_lowering=False)
    xtok = nc.dram_tensor("xtok", [TOK, D], F32, kind="ExternalInput").ap()
    xtokT = nc.dram_tensor("xtokT", [D, TOK], F32, kind="ExternalInput").ap()
    xpwT = nc.dram_tensor("xpwT", [D, R], F32, kind="ExternalInput").ap()
    wcatT = nc.dram_tensor("wcatT", [D, 24], F32, kind="ExternalInput").ap()
    ones_in = nc.dram_tensor("ones_in", [1, 128], F32, kind="ExternalInput").ap()

    o_xp = nc.dram_tensor("xpraw", [R, TOK], F32, kind="ExternalOutput").ap()
    o_lx = nc.dram_tensor("lx", [24, TOK], F32, kind="ExternalOutput").ap()
    o_m = nc.dram_tensor("mcol", [128, 8], F32, kind="ExternalOutput").ap()
    o_r = nc.dram_tensor("rcol", [128, 8], F32, kind="ExternalOutput").ap()
    o_mr = nc.dram_tensor("mrcol", [128, 8], F32, kind="ExternalOutput").ap()
    o_ws = nc.dram_tensor("wsums", [1, R + 24], F32, kind="ExternalOutput").ap()

    NTT = TOK // 128
    ND = D // 128

    with ExitStack() as ctx:
        def sb(name, shape, dt=F32):
            return ctx.enter_context(nc.sbuf_tensor(name, shape, dt))

        def ps(name, shape):
            return ctx.enter_context(nc.psum_tensor(name, shape, F32))

        _xk = sb("xk", [128, 2 * D])
        _xb = sb("xb", [128, 4 * TOK])
        _wpb = sb("wpb", [128, ND * R])
        _wcb = sb("wcb", [128, ND * 24])
        onec = sb("onec", [128, 1])
        scr = sb("scr", [128, D], BF16)
        sumc = sb("sumc", [128, 8])
        sqc = sb("sqc", [128, 8])
        meanc = sb("meanc", [128, 8])
        varc = sb("varc", [128, 8])
        sdc = sb("sdc", [128, 8])
        rstdc = sb("rstdc", [128, 8])
        t8 = sb("t8", [128, 8])
        mrc = sb("mrc", [128, 8])
        xpev = sb("xpev", [128, TOK])
        xpev2 = sb("xpev2", [32, TOK])
        lxev = sb("lxev", [24, TOK])
        wsev = sb("wsev", [1, R + 24])
        csts = sb("csts", [128, 4])
        p1 = ps("p1", [128, TOK])
        p2 = ps("p2", [32, TOK])
        plx = ps("plx", [24, TOK])
        pws = ps("pws", [1, R])
        pwf = ps("pwf", [1, 24])
        sems = {n: ctx.enter_context(nc.semaphore(n)) for n in SEMNAMES}
        block = ctx.enter_context(nc.Block())

        S = Sched()
        S.sems(SEMNAMES)
        for j, v in enumerate(CONSTS):
            nc.const_aps.register(F32, v, csts[:, j:j + 1])
        for j, v in enumerate(CONSTS):
            S.run("vector", lambda e, j=j, v=v: e.memset(csts[:, j:j + 1], v), "VS")
        vs_consts = S.cnt["VS"]
        S.wait("scalar", "VS", vs_consts)

        xk = [_xk[:, (i % 2) * D:((i % 2) + 1) * D] for i in range(NTT)]
        xb = [_xb[:, (i % 4) * TOK:((i % 4) + 1) * TOK] for i in range(ND)]
        wpb = [_wpb[:, i * R:(i + 1) * R] for i in range(ND)]
        wcb = [_wcb[:, i * 24:(i + 1) * 24] for i in range(ND)]

        for i in range(ND):
            S.run("gpsimd", lambda e, i=i: e.dma_start(
                out=wpb[i], in_=xpwT[i * 128:(i + 1) * 128, :]), "GD", 16)
        for i in range(ND):
            S.run("gpsimd", lambda e, i=i: e.dma_start(
                out=wcb[i], in_=wcatT[i * 128:(i + 1) * 128, :]), "GD", 16)
        S.run("gpsimd", lambda e: e.dma_start(
            out=onec[:, :],
            in_=ones_in.rearrange("o (p q) -> (o p) q", q=1)), "GD", 16)
        gd_all = S.cnt["GD"]

        ld_tile = []
        ss_tile = [0] * NTT
        for i in range(2):
            S.run("sync", lambda e, i=i: e.dma_start(
                out=xk[i], in_=xtok[i * 128:(i + 1) * 128, :]), "LD", 16)
            ld_tile.append(S.cnt["LD"])
        for i in range(NTT):
            S.wait("scalar", "LD", ld_tile[i])
            S.run("scalar", lambda e, i=i: e.activation(
                out=scr[:, :], in_=xk[i], func=AF.Copy, bias=0.0,
                accum_out=sumc[:, i:i + 1]), "SS")
            S.wait("scalar", "SS", S.cnt["SS"])
            S.run("scalar", lambda e, i=i: e.activation(
                out=scr[:, :], in_=xk[i], func=AF.Square,
                accum_out=sqc[:, i:i + 1]), "SS")
            ss_tile[i] = S.cnt["SS"]
            S.wait("scalar", "SS", ss_tile[i])
            if i + 2 < NTT:
                S.wait("sync", "SS", ss_tile[i])
                S.run("sync", lambda e, i=i: e.dma_start(
                    out=xk[i + 2], in_=xtok[(i + 2) * 128:(i + 3) * 128, :]),
                    "LD", 16)
                ld_tile.append(S.cnt["LD"])
        ss_stats = S.cnt["SS"]

        S.wait("vector", "SS", ss_stats)
        S.run("vector", lambda e: e.tensor_scalar(
            out=meanc[:, :], in0=sumc[:, :], scalar1=1.0 / D, scalar2=None,
            op0=ALU.mult), "VS")
        S.wait("vector", "VS", S.cnt["VS"])
        S.run("vector", lambda e: e.tensor_scalar(
            out=varc[:, :], in0=sqc[:, :], scalar1=1.0 / D, scalar2=None,
            op0=ALU.mult), "VS")
        S.wait("vector", "VS", S.cnt["VS"])
        S.run("vector", lambda e: e.tensor_tensor(
            out=t8[:, :], in0=meanc[:, :], in1=meanc[:, :], op=ALU.mult), "VS")
        S.wait("vector", "VS", S.cnt["VS"])
        S.run("vector", lambda e: e.tensor_tensor(
            out=varc[:, :], in0=varc[:, :], in1=t8[:, :], op=ALU.subtract), "VS")
        vs_var = S.cnt["VS"]
        S.wait("scalar", "VS", vs_var)
        S.run("scalar", lambda e: e.activation(
            out=sdc[:, :], in_=varc[:, :], func=AF.Sqrt, bias=LN_EPS), "SS")
        ss_sd = S.cnt["SS"]
        S.wait("vector", "SS", ss_sd)
        S.run("vector", lambda e: e.reciprocal(out=rstdc[:, :], in_=sdc[:, :]), "VS")
        S.wait("vector", "VS", S.cnt["VS"])
        S.run("vector", lambda e: e.tensor_tensor(
            out=mrc[:, :], in0=meanc[:, :], in1=rstdc[:, :], op=ALU.mult), "VS")
        vs_rstd = S.cnt["VS"]

        S.wait("tensor", "GD", gd_all)
        ldx = [0] * ND
        ts_used = [0] * ND
        for i in range(4):
            S.run("sync", lambda e, i=i: e.dma_start(
                out=xb[i], in_=xtokT[i * 128:(i + 1) * 128, :]), "LD", 16)
            ldx[i] = S.cnt["LD"]
        for i in range(ND):
            S.wait("tensor", "LD", ldx[i])
            for h in range(2):
                lo, hi = h * 512, (h + 1) * 512
                S.run("tensor", lambda e, i=i, lo=lo, hi=hi: e.matmul(
                    p1[:, lo:hi], wpb[i][:, 0:128], xb[i][:, lo:hi],
                    start=(i == 0), stop=(i == ND - 1)), None)
                S.run("tensor", lambda e, i=i, lo=lo, hi=hi: e.matmul(
                    p2[:, lo:hi], wpb[i][:, 128:160], xb[i][:, lo:hi],
                    start=(i == 0), stop=(i == ND - 1)), None)
                S.run("tensor", lambda e, i=i, lo=lo, hi=hi: e.matmul(
                    plx[:, lo:hi], wcb[i], xb[i][:, lo:hi],
                    start=(i == 0), stop=(i == ND - 1)), None)
            S.run("tensor", lambda e, i=i: e.matmul(
                pws[:, :], onec[:, :], wpb[i],
                start=(i == 0), stop=(i == ND - 1)), None)
            S.run("tensor", lambda e, i=i: e.matmul(
                pwf[:, :], onec[:, :], wcb[i],
                start=(i == 0), stop=(i == ND - 1)), "TS")
            ts_used[i] = S.cnt["TS"]
            if i + 4 < ND:
                S.wait("sync", "TS", ts_used[i])
                S.run("sync", lambda e, i=i: e.dma_start(
                    out=xb[i + 4], in_=xtokT[(i + 4) * 128:(i + 5) * 128, :]),
                    "LD", 16)
                ldx[i + 4] = S.cnt["LD"]
        ts_all = S.cnt["TS"]

        S.wait("scalar", "TS", ts_all)
        for fn in (
            lambda e: e.activation(out=xpev[:, :], in_=p1[:, :], func=AF.Copy, bias=0.0),
            lambda e: e.activation(out=xpev2[:, :], in_=p2[:, :], func=AF.Copy, bias=0.0),
            lambda e: e.activation(out=lxev[:, :], in_=plx[:, :], func=AF.Copy, bias=0.0),
            lambda e: e.activation(out=wsev[:, 0:R], in_=pws[:, :], func=AF.Copy, bias=0.0),
            lambda e: e.activation(out=wsev[:, R:R + 24], in_=pwf[:, :], func=AF.Copy, bias=0.0),
        ):
            S.run("scalar", fn, "SS")
        ss_evac = S.cnt["SS"]

        S.wait("sync", "SS", ss_evac)
        S.wait("sync", "VS", vs_rstd)
        S.run("sync", lambda e: e.dma_start(out=o_xp[0:128, :], in_=xpev[:, :]), "ST", 16)
        S.run("sync", lambda e: e.dma_start(out=o_xp[128:160, :], in_=xpev2[:, :]), "ST", 16)
        S.run("sync", lambda e: e.dma_start(out=o_lx, in_=lxev[:, :]), "ST", 16)
        S.run("sync", lambda e: e.dma_start(out=o_m, in_=meanc[:, :]), "ST", 16)
        S.run("sync", lambda e: e.dma_start(out=o_r, in_=rstdc[:, :]), "ST", 16)
        S.run("sync", lambda e: e.dma_start(out=o_mr, in_=mrc[:, :]), "ST", 16)
        S.run("sync", lambda e: e.dma_start(out=o_ws, in_=wsev[:, :]), "ST", 16)
        S.wait("sync", "ST", S.cnt["ST"])

        S.emit(block, sems)
    return nc


_L1 = None


def _get_l1():
    global _L1
    if _L1 is None:
        _L1 = build_l1()
    return _L1


def run_l1(x2d, x_proj_w, wcat):
    nc = _get_l1()
    xpwT = np.ascontiguousarray(x_proj_w.T)
    wcatT = np.ascontiguousarray(wcat.T)
    ones = np.ones((1, 128), np.float32)
    in_maps = []
    for c in range(NCORES):
        xt = np.ascontiguousarray(x2d[c * TOK:(c + 1) * TOK])
        in_maps.append(dict(
            xtok=xt, xtokT=np.ascontiguousarray(xt.T),
            xpwT=xpwT, wcatT=wcatT, ones_in=ones))
    res = run_bass_kernel_spmd(nc, in_maps, core_ids=list(range(NCORES)))
    return res.results




def build_l2():
    nc = bass.Bass("TRN2", target_bir_lowering=False)
    xdT = nc.dram_tensor("xdT", [DC, TT], F32, kind="ExternalInput").ap()
    xp = nc.dram_tensor("xp", [R, TT], F32, kind="ExternalInput").ap()
    st1 = nc.dram_tensor("st1", [3, TT], F32, kind="ExternalInput").ap()
    alog = nc.dram_tensor("alog", [128, 4 * N], F32, kind="ExternalInput").ap()
    dcol = nc.dram_tensor("dcol", [128, 4], F32, kind="ExternalInput").ap()
    dtwT = nc.dram_tensor("dtwT", [128, DC], F32, kind="ExternalInput").ap()
    dtbc = nc.dram_tensor("dtbc", [128, 4], F32, kind="ExternalInput").ap()
    wctT = nc.dram_tensor("wctT", [128, 4 * 24], F32, kind="ExternalInput").ap()
    wsc = nc.dram_tensor("wsc", [128, 1], F32, kind="ExternalInput").ap()
    wsb = nc.dram_tensor("wsb", [32, 1], F32, kind="ExternalInput").ap()

    o_y2 = nc.dram_tensor("y2", [DC, TT], F32, kind="ExternalOutput").ap()
    o_lnp = nc.dram_tensor("lnp", [2, TT], F32, kind="ExternalOutput").ap()
    o_ly = nc.dram_tensor("ly", [24, TT], F32, kind="ExternalOutput").ap()

    bcd = [nc.dram_tensor(f"bcd{i}", [32, CH], F32, kind="Internal").ap()
           for i in range(2)]

    with ExitStack() as ctx:
        def sb(name, shape, dt=F32):
            return ctx.enter_context(nc.sbuf_tensor(name, shape, dt))

        def ps(name, shape):
            return ctx.enter_context(nc.psum_tensor(name, shape, F32))

        _xd = sb("xd", [128, 2 * 4 * CH])
        _xln = sb("xln", [128, 2 * 4 * CH])
        _dt4 = sb("dt4", [128, 2 * 4 * CH])
        _w4 = sb("w4", [128, 2 * 4 * CH])
        _dti = sb("dti", [128, 2 * CH])
        _dta = sb("dta", [128, 2 * CH])
        _bc = sb("bc", [32, 2 * CH])
        _bca = sb("bca", [32, 2 * CH])
        _mrep = sb("mrep", [128, 2 * CH])
        _rrep = sb("rrep", [128, 2 * CH])
        _mrrep = sb("mrrep", [128, 2 * CH])
        _ball = sb("ball", [128, N * CH])
        _call = sb("call", [128, N * CH])
        _u1 = sb("u1", [128, 2 * N * CH])
        _u2 = sb("u2", [128, 2 * N * CH])
        _u3 = sb("u3", [128, 2 * N * CH])
        _y2t = sb("y2t", [128, 4 * CH])
        ysq = sb("ysq", [128, CH])
        sptmp = sb("sptmp", [128, CH])
        carry = sb("carry", [128, 4 * N])
        inj = sb("inj", [128, N])
        t256a = sb("t256a", [128, CH])
        t256b = sb("t256b", [32, CH])
        alog_s = sb("alog_s", [128, 4 * N])
        aneg = sb("aneg", [128, 4 * N])
        dcol_s = sb("dcol_s", [128, 4])
        dtw_s = sb("dtw_s", [128, DC])
        dtb_s = sb("dtb_s", [128, 4])
        wct_s = sb("wct_s", [128, 4 * 24])
        wsc_s = sb("wsc_s", [128, 1])
        wsb_s = sb("wsb_s", [32, 1])
        nwsc = sb("nwsc", [128, 1])
        nwsb = sb("nwsb", [32, 1])
        lyev = sb("lyev", [24, 2 * CH])
        lnev = sb("lnev", [1, 4 * CH])
        csts = sb("csts", [128, 4])
        pdtz = [ps(f"pdtz{i}", [128, CH]) for i in range(2)]
        pln1 = ps("pln1", [1, CH])
        pln2 = ps("pln2", [1, CH])
        ply = ps("ply", [24, CH])
        sems = {n: ctx.enter_context(nc.semaphore(n)) for n in SEMNAMES}
        block = ctx.enter_context(nc.Block())

        S = Sched()
        S.sems(SEMNAMES)
        for j, v in enumerate(CONSTS):
            nc.const_aps.register(F32, v, csts[:, j:j + 1])
        for j, v in enumerate(CONSTS):
            S.run("vector", lambda e, j=j, v=v: e.memset(csts[:, j:j + 1], v), "VS")
        vs_consts = S.cnt["VS"]
        S.wait("scalar", "VS", vs_consts)
        S.wait("tensor", "VS", vs_consts)

        xd = [[_xd[:, (p * 4 + t) * CH:(p * 4 + t + 1) * CH] for t in range(4)]
              for p in range(2)]
        xln = [[_xln[:, (p * 4 + t) * CH:(p * 4 + t + 1) * CH] for t in range(4)]
               for p in range(2)]
        dt4 = [[_dt4[:, (p * 4 + t) * CH:(p * 4 + t + 1) * CH] for t in range(4)]
               for p in range(2)]
        w4 = [[_w4[:, (p * 4 + t) * CH:(p * 4 + t + 1) * CH] for t in range(4)]
              for p in range(2)]
        dti = [_dti[:, p * CH:(p + 1) * CH] for p in range(2)]
        dta = [_dta[:, p * CH:(p + 1) * CH] for p in range(2)]
        bc = [_bc[:, p * CH:(p + 1) * CH] for p in range(2)]
        bca = [_bca[:, p * CH:(p + 1) * CH] for p in range(2)]
        mrep = [_mrep[:, p * CH:(p + 1) * CH] for p in range(2)]
        rrep = [_rrep[:, p * CH:(p + 1) * CH] for p in range(2)]
        mrrep = [_mrrep[:, p * CH:(p + 1) * CH] for p in range(2)]
        ball = [_ball[:, :], _ball[:, :]]
        call = [_call[:, :], _call[:, :]]
        u1 = [_u1[:, p * N * CH:(p + 1) * N * CH] for p in range(2)]
        u2b = [_u2[:, p * N * CH:(p + 1) * N * CH] for p in range(2)]
        u3b = [_u3[:, p * N * CH:(p + 1) * N * CH] for p in range(2)]
        y2t = [_y2t[:, t * CH:(t + 1) * CH] for t in range(4)]
        lyv = [lyev[:, p * CH:(p + 1) * CH] for p in range(2)]
        lnv = [[lnev[:, (p * 2 + j) * CH:(p * 2 + j + 1) * CH] for j in range(2)]
               for p in range(2)]

        for fn in (
            lambda e: e.dma_start(out=alog_s[:, :], in_=alog),
            lambda e: e.dma_start(out=dcol_s[:, :], in_=dcol),
            lambda e: e.dma_start(out=dtw_s[:, :], in_=dtwT),
            lambda e: e.dma_start(out=dtb_s[:, :], in_=dtbc),
            lambda e: e.dma_start(out=wct_s[:, :], in_=wctT),
            lambda e: e.dma_start(out=wsc_s[:, :], in_=wsc),
            lambda e: e.dma_start(out=wsb_s[:, :], in_=wsb),
        ):
            S.run("sync", fn, "LD", 16)
        ld_once = S.cnt["LD"]
        S.wait("scalar", "LD", ld_once)
        S.run("scalar", lambda e: e.activation(
            out=aneg[:, :], in_=alog_s[:, :], func=AF.Exp), "SS")
        ss_aexp = S.cnt["SS"]
        S.wait("vector", "SS", ss_aexp)
        S.run("vector", lambda e: e.tensor_scalar(
            out=aneg[:, :], in0=aneg[:, :], scalar1=-1.0, scalar2=None,
            op0=ALU.mult), "VS")
        S.wait("vector", "LD", ld_once)
        S.run("vector", lambda e: e.tensor_scalar(
            out=nwsc[:, :], in0=wsc_s[:, :], scalar1=-1.0, scalar2=None,
            op0=ALU.mult), "VS")
        S.run("vector", lambda e: e.tensor_scalar(
            out=nwsb[:, :], in0=wsb_s[:, :], scalar1=-1.0, scalar2=None,
            op0=ALU.mult), "VS")
        vs_pre = S.cnt["VS"]
        S.wait("scalar", "VS", vs_pre)
        S.wait("tensor", "LD", ld_once)

        cons_vs = [0, 0]
        cons_ball = [0, 0]
        cons_u1 = [0, 0]
        ss_dtz_sp = [0, 0]
        ts_sq_use = 0
        vs_evac_prev = 0
        st_y2 = [0] * 4
        st_misc = 0
        pool_self = 0

        ld_chunk_p = [0, 0]
        ball_ps_gate = 0
        ball_vs_gate = 0

        def emit_loads(ci2):
            par2 = ci2 % 2
            c02 = ci2 * CH
            cols2 = slice(c02, c02 + CH)
            S.wait("sync", "VS", cons_vs[par2])
            for t in range(4):
                S.run("sync", lambda e, t=t, par2=par2, cols2=cols2: e.dma_start(
                    out=xd[par2][t], in_=xdT[t * 128:(t + 1) * 128, cols2]), "LD", 16)
            S.run("sync", lambda e, par2=par2, cols2=cols2: e.dma_start(
                out=dti[par2], in_=xp[0:128, cols2]), "LD", 16)
            S.run("sync", lambda e, par2=par2, cols2=cols2: e.dma_start(
                out=bc[par2], in_=xp[128:160, cols2]), "LD", 16)
            S.run("sync", lambda e, par2=par2, cols2=cols2: e.dma_start(
                out=mrep[par2][:, :], in_=st1[0:1, cols2].broadcast_to([128, CH])),
                "LD", 16)
            S.run("sync", lambda e, par2=par2, cols2=cols2: e.dma_start(
                out=rrep[par2][:, :], in_=st1[1:2, cols2].broadcast_to([128, CH])),
                "LD", 16)
            S.run("sync", lambda e, par2=par2, cols2=cols2: e.dma_start(
                out=mrrep[par2][:, :], in_=st1[2:3, cols2].broadcast_to([128, CH])),
                "LD", 16)
            ld_chunk_p[par2] = S.cnt["LD"]

        vs_w4_p = [[0] * 4, [0] * 4]
        vs_xln_p = [[0] * 4, [0] * 4]
        ld_bcast_p = [0, 0]
        cons_dta = [0, 0]
        cons_bca = [0, 0]

        def emit_prep(ci2):
            par2 = ci2 % 2
            S.wait("vector", "LD", ld_chunk_p[par2])
            S.wait("vector", "TS", cons_dta[par2])
            S.run("vector", lambda e, par2=par2: e.tensor_tensor(
                out=t256a[:, :], in0=dti[par2], in1=rrep[par2], op=ALU.mult), "VS")
            S.wait("vector", "VS", S.cnt["VS"])
            S.run("vector", lambda e, par2=par2: e.scalar_tensor_tensor(
                out=dta[par2], in0=mrrep[par2], scalar=nwsc[:, 0:1], in1=t256a[:, :],
                op0=ALU.mult, op1=ALU.add), "VS")
            vs_dta2 = S.cnt["VS"]
            S.wait("vector", "LD", cons_bca[par2])
            S.run("vector", lambda e, par2=par2: e.tensor_tensor(
                out=t256b[:, :], in0=bc[par2], in1=rrep[par2][0:32, :], op=ALU.mult), "VS")
            S.wait("vector", "VS", S.cnt["VS"])
            S.run("vector", lambda e, par2=par2: e.scalar_tensor_tensor(
                out=bca[par2], in0=mrrep[par2][0:32, :], scalar=nwsb[:, 0:1],
                in1=t256b[:, :], op0=ALU.mult, op1=ALU.add), "VS")
            vs_bca2 = S.cnt["VS"]
            for t in range(4):
                S.run("vector", lambda e, t=t, par2=par2: e.tensor_tensor(
                    out=xln[par2][t], in0=xd[par2][t], in1=rrep[par2], op=ALU.mult), "VS")
                S.wait("vector", "VS", S.cnt["VS"])
                S.run("vector", lambda e, t=t, par2=par2: e.tensor_tensor(
                    out=xln[par2][t], in0=xln[par2][t], in1=mrrep[par2],
                    op=ALU.subtract), "VS")
                vs_xln_p[par2][t] = S.cnt["VS"]
            for t in range(4):
                pz = pdtz[t % 2]
                S.wait("tensor", "SS", ss_dtz_sp[t % 2])
                S.wait("tensor", "VS", vs_dta2)
                S.run("tensor", lambda e, t=t, pz=pz, par2=par2: e.matmul(
                    pz[:, :], dtw_s[:, t * 128:(t + 1) * 128], dta[par2],
                    start=True, stop=True), "TS")
                ts_dtz = S.cnt["TS"]
                if t == 3:
                    cons_dta[par2] = ts_dtz
                S.wait("scalar", "TS", ts_dtz)
                S.run("scalar", lambda e, t=t, pz=pz: e.activation(
                    out=sptmp[:, :], in_=pz[:, :], func=AF.Exp,
                    bias=dtb_s[:, t:t + 1]), "SS")
                S.wait("scalar", "SS", S.cnt["SS"])
                S.run("scalar", lambda e, t=t, par2=par2: e.activation(
                    out=dt4[par2][t], in_=sptmp[:, :], func=AF.Ln, bias=1.0), "SS")
                ss_dtz_sp[t % 2] = S.cnt["SS"]
                ss_sp = S.cnt["SS"]
                S.wait("vector", "SS", ss_sp)
                S.wait("vector", "VS", vs_xln_p[par2][t])
                S.run("vector", lambda e, t=t, par2=par2: e.tensor_tensor(
                    out=w4[par2][t], in0=dt4[par2][t], in1=xln[par2][t],
                    op=ALU.mult), "VS")
                vs_w4_p[par2][t] = S.cnt["VS"]
                S.wait("vector", "VS", vs_w4_p[par2][t])
            S.wait("sync", "VS", vs_bca2)
            S.run("sync", lambda e, par2=par2: e.dma_start(
                out=bcd[par2], in_=bca[par2]), "LD", 16)
            cons_bca[par2] = S.cnt["LD"]
            S.wait("sync", "LD", S.cnt["LD"])
            S.wait("sync", "PS", ball_ps_gate)
            S.wait("sync", "VS", ball_vs_gate)
            S.run("sync", lambda e, par2=par2: e.dma_start(
                out=ball[par2].rearrange("p (n t) -> p n t", n=N),
                in_=bcd[par2][0:N, :].rearrange("n (o t) -> o n t", o=1)
                    .broadcast_to([128, N, CH])), "LD", 16)
            S.run("sync", lambda e, par2=par2: e.dma_start(
                out=call[par2].rearrange("p (n t) -> p n t", n=N),
                in_=bcd[par2][N:2 * N, :].rearrange("n (o t) -> o n t", o=1)
                    .broadcast_to([128, N, CH])), "LD", 16)
            ld_bcast_p[par2] = S.cnt["LD"]

        emit_loads(0)
        emit_prep(0)
        for ci in range(NCH):
            par = ci % 2
            cb = ci % CHB
            c0 = ci * CH
            cols = slice(c0, c0 + CH)
            ld_bcast = ld_bcast_p[par]

            vs_tts = [0] * 4
            vs_red = [0] * 4
            z_done = [0] * 4
            carry_done = [0] * 4
            ss_dA = [0] * 4
            for t in range(4):
                u2 = u2b[t % 2]
                u3 = u3b[t % 2]
                up = u1[t % 2]
                vs_w = vs_w4_p[par][t]
                S.wait("scalar", "VS", cons_u1[t % 2])
                for n in range(N):
                    S.run("scalar", lambda e, t=t, n=n, up=up, par=par: e.activation(
                        out=up[:, n * CH:(n + 1) * CH], in_=dt4[par][t], func=AF.Exp,
                        scale=aneg[:, t * N + n:t * N + n + 1]), "SS")
                ss_dA[t] = S.cnt["SS"]

                uv = up.rearrange("p (n t) -> p n t", n=N)
                u2v = u2.rearrange("p (n t) -> p n t", n=N)
                S.wait("gpsimd", "VS", vs_red[t - 2] if t >= 2 else
                       (vs_red_last2[t] if ci > 0 else 0))
                S.wait("gpsimd", "LD", ld_bcast)
                S.wait("gpsimd", "VS", vs_w)
                S.run("gpsimd", lambda e, t=t, par=par, u2v=u2v: e.tensor_tensor(
                    out=u2v,
                    in0=w4[par][t].rearrange("p (o t) -> p o t", o=1)
                        .broadcast_to([128, N, CH]),
                    in1=ball[par].rearrange("p (n t) -> p n t", n=N),
                    op=ALU.mult), "PS")
                S.wait("gpsimd", "PS", S.cnt["PS"])
                S.wait("gpsimd", "SS", ss_dA[t])
                if cb > 0:
                    S.run("gpsimd", lambda e, t=t, uv=uv: e.tensor_tensor(
                        out=inj[:, :], in0=uv[:, :, 0],
                        in1=carry[:, t * N:(t + 1) * N], op=ALU.mult), "PS")
                    S.wait("gpsimd", "PS", S.cnt["PS"])
                S.run("gpsimd", lambda e, uv=uv: e.memset(uv[:, :, 0:1], 0.0), "PS")
                S.wait("gpsimd", "PS", S.cnt["PS"])
                if cb > 0:
                    S.run("gpsimd", lambda e, u2v=u2v: e.tensor_tensor(
                        out=u2v[:, :, 0], in0=u2v[:, :, 0],
                        in1=inj[:, :], op=ALU.add), "PS")
                    S.wait("gpsimd", "PS", S.cnt["PS"])
                ps_dbx = S.cnt["PS"]
                if t >= 1:
                    tc = t - 1
                    S.wait("gpsimd", "VS", vs_tts[tc])
                    S.run("gpsimd", lambda e, tc=tc: e.tensor_copy(
                        out=carry[:, tc * N:(tc + 1) * N],
                        in_=u3b[tc % 2].rearrange("p (n t) -> p n t", n=N)[:, :, CH - 1]),
                        "PS")
                    carry_done[tc] = S.cnt["PS"]
                    S.wait("gpsimd", "PS", S.cnt["PS"])

                S.wait("vector", "SS", ss_dA[t])
                S.wait("vector", "PS", ps_dbx)
                if t >= 2:
                    S.wait("vector", "VS", z_done[t - 2])
                    S.wait("vector", "PS", carry_done[t - 2])
                elif ci > 0:
                    S.wait("vector", "VS", z_last2[t])
                    S.wait("vector", "PS", carry_last2[t])
                S.run("vector", lambda e, up=up, u2=u2, u3=u3: e.tensor_tensor_scan(
                    out=u3[:, :], data0=up[:, :], data1=u2[:, :], initial=0.0,
                    op0=ALU.mult, op1=ALU.add), "VS")
                vs_tts[t] = S.cnt["VS"]
                cons_u1[t % 2] = vs_tts[t]
                S.wait("vector", "VS", vs_tts[t])
                S.run("vector", lambda e, t=t, par=par, u2=u2, u3=u3:
                      e.tensor_tensor(
                          out=u2.rearrange("p (c n) -> p n c", n=N),
                          in0=u3.rearrange("p (n c) -> p n c", n=N),
                          in1=call[par].rearrange("p (n c) -> p n c", n=N),
                          op=ALU.mult), "VS")
                z_done[t] = S.cnt["VS"]
                S.wait("vector", "VS", z_done[t])
                S.wait("vector", "LD", st_y2[t])
                S.run("vector", lambda e, t=t: e.tensor_reduce(
                    out=y2t[t], in_=u2b[t % 2].rearrange("p (c n) -> p c n", n=N),
                    axis=mybir.AxisListType.X, op=ALU.add), "VS")
                S.wait("vector", "VS", S.cnt["VS"])
                S.run("vector", lambda e, t=t, par=par: e.scalar_tensor_tensor(
                    out=y2t[t], in0=xln[par][t], scalar=dcol_s[:, t:t + 1],
                    in1=y2t[t], op0=ALU.mult, op1=ALU.add), "VS")
                vs_red[t] = S.cnt["VS"]
                S.wait("vector", "VS", vs_red[t])
                if t == 1 and ci + 1 < NCH:
                    emit_loads(ci + 1)
            S.wait("gpsimd", "VS", vs_tts[3])
            S.run("gpsimd", lambda e: e.tensor_copy(
                out=carry[:, 3 * N:4 * N],
                in_=u3b[1].rearrange("p (n t) -> p n t", n=N)[:, :, CH - 1]), "PS")
            carry_done[3] = S.cnt["PS"]
            S.wait("gpsimd", "PS", S.cnt["PS"])
            vs_red_last2 = [vs_red[2], vs_red[3], vs_red[2], vs_red[3]]
            z_last2 = [z_done[2], z_done[3], z_done[2], z_done[3]]
            carry_last2 = [carry_done[2], carry_done[3], carry_done[2], carry_done[3]]
            cons_vs[par] = S.cnt["VS"]
            ball_ps_gate = S.cnt["PS"]
            ball_vs_gate = S.cnt["VS"]
            if ci + 1 < NCH:
                emit_prep(ci + 1)

            S.wait("tensor", "VS", vs_evac_prev)
            for t in range(4):
                S.wait("scalar", "VS", vs_red[t])
                S.wait("scalar", "TS", ts_sq_use)
                S.run("scalar", lambda e, t=t: e.activation(
                    out=ysq[:, :], in_=y2t[t], func=AF.Square), "SS")
                ss_sq = S.cnt["SS"]
                S.wait("tensor", "VS", vs_red[t])
                S.run("tensor", lambda e, t=t: e.matmul(
                    pln1[:, :], csts[:, 1:2], y2t[t],
                    start=(t == 0), stop=(t == 3)), "TS" if t == 3 else None)
                S.wait("tensor", "SS", ss_sq)
                S.run("tensor", lambda e, t=t: e.matmul(
                    pln2[:, :], csts[:, 1:2], ysq[:, :],
                    start=(t == 0), stop=(t == 3)), "TS")
                ts_sq_use = S.cnt["TS"]
                S.run("tensor", lambda e, t=t: e.matmul(
                    ply[:, :], wct_s[:, t * 24:(t + 1) * 24], y2t[t],
                    start=(t == 0), stop=(t == 3)), "TS" if t == 3 else None)
            ts_stats = S.cnt["TS"]
            S.wait("vector", "TS", ts_stats)
            S.wait("vector", "LD", st_misc)
            S.run("vector", lambda e, par=par: e.tensor_copy(
                out=lnv[par][0], in_=pln1[:, :]), "VS")
            S.run("vector", lambda e, par=par: e.tensor_copy(
                out=lnv[par][1], in_=pln2[:, :]), "VS")
            S.run("vector", lambda e, par=par: e.tensor_copy(
                out=lyv[par], in_=ply[:, :]), "VS")
            vs_evac = S.cnt["VS"]
            vs_evac_prev = vs_evac
            for t in range(4):
                S.wait("sync", "VS", vs_red[t])
                S.run("sync", lambda e, t=t, cols=cols: e.dma_start(
                    out=o_y2[t * 128:(t + 1) * 128, cols], in_=y2t[t]), "LD", 16)
                st_y2[t] = S.cnt["LD"]
            S.wait("sync", "VS", vs_evac)
            S.run("sync", lambda e, par=par, cols=cols: e.dma_start(
                out=o_lnp[0:1, cols], in_=lnv[par][0]), "LD", 16)
            S.run("sync", lambda e, par=par, cols=cols: e.dma_start(
                out=o_lnp[1:2, cols], in_=lnv[par][1]), "LD", 16)
            S.run("sync", lambda e, par=par, cols=cols: e.dma_start(
                out=o_ly[:, cols], in_=lyv[par]), "LD", 16)
            st_misc = S.cnt["LD"]

        S.wait("sync", "LD", S.cnt["LD"])
        S.wait("sync", "GD", S.cnt["GD"])

        S.emit(block, sems)
    return nc


_L2 = None


def _get_l2():
    global _L2
    if _L2 is None:
        _L2 = build_l2()
    return _L2


def run_l2(x2d, xp_full, st1, A_log, Dv, dt_proj_w, dt_proj_b, wcat, wsum):
    nc = _get_l2()
    in_maps = []
    for c in range(NCORES):
        sl = slice(c * DC, (c + 1) * DC)
        in_maps.append(dict(
            xdT=np.ascontiguousarray(x2d[:, sl].T),
            xp=xp_full, st1=st1,
            alog=np.ascontiguousarray(
                A_log[sl].reshape(4, 128, N).transpose(1, 0, 2).reshape(128, 4 * N)),
            dcol=np.ascontiguousarray(Dv[sl].reshape(4, 128).T),
            dtwT=np.ascontiguousarray(dt_proj_w[sl].T),
            dtbc=np.ascontiguousarray(dt_proj_b[sl].reshape(4, 128).T),
            wctT=np.ascontiguousarray(
                wcat[:, sl].T.reshape(4, 128, 24).transpose(1, 0, 2).reshape(128, 96)),
            wsc=np.ascontiguousarray(wsum[0:128].reshape(128, 1)),
            wsb=np.ascontiguousarray(wsum[128:160].reshape(32, 1)),
        ))
    res = run_bass_kernel_spmd(nc, in_maps, core_ids=list(range(NCORES)))
    return res.results


def build_l3():
    nc = bass.Bass("TRN2", target_bir_lowering=False)
    xtok = nc.dram_tensor("xtok", [TOK, D], F32, kind="ExternalInput").ap()
    y2tok = nc.dram_tensor("y2tok", [TOK, D], F32, kind="ExternalInput").ap()
    lnps = nc.dram_tensor("lnps", [128, 16 * 8], F32, kind="ExternalInput").ap()
    lyt = nc.dram_tensor("lyt", [128, 192 * 8], F32, kind="ExternalInput").ap()
    lxt = nc.dram_tensor("lxt", [128, 192], F32, kind="ExternalInput").ap()
    wsumf = nc.dram_tensor("wsumf", [1, 24], F32, kind="ExternalInput").ap()
    hrb = nc.dram_tensor("hrb", [1, 16], F32, kind="ExternalInput").ap()
    hpb = nc.dram_tensor("hpb", [1, 4], F32, kind="ExternalInput").ap()
    hob = nc.dram_tensor("hob", [1, 4], F32, kind="ExternalInput").ap()
    gatei = nc.dram_tensor("gatei", [1, 1], F32, kind="ExternalInput").ap()
    eye16 = nc.dram_tensor("eye16", [1, 16], F32, kind="ExternalInput").ap()
    ident = nc.dram_tensor("ident", [128, 128], F32, kind="ExternalInput").ap()
    wb = nc.dram_tensor("wb", [1024, 1024], F32, kind="ExternalInput").ap()

    o_x3 = nc.dram_tensor("x3", [TOK, D], F32, kind="ExternalOutput").ap()
    o_dbg = nc.dram_tensor("dbg", [128, 192 + 128 + 32 + 32 + 48], F32,
                           kind="ExternalOutput").ap()

    NTT = TOK // 128

    with ExitStack() as ctx:
        def sb(name, shape, dt=F32):
            return ctx.enter_context(nc.sbuf_tensor(name, shape, dt))

        _xt = sb("xt", [128, 2 * D])
        _yt = sb("yt", [128, 2 * D])
        x2t = sb("x2t", [128, D])
        xint = sb("xint", [128, D])
        xmix = sb("xmix", [128, D])
        xpre32 = sb("xpre32", [128, 1024])
        xpreb = sb("xpreb", [128, 1024], BF16)
        xpT = sb("xpT", [128, 1024], BF16)
        yb = sb("yb", [128, 1024])
        _wbb = sb("wbb", [128, 8 * 1024], BF16)
        idn = sb("idn", [128, 128], BF16)
        lnps_s = sb("lnps_s", [128, 16 * 8])
        sc = sb("sc", [128, 16])
        lyt_s = sb("lyt_s", [128, 192 * 8])
        lyc = sb("lyc", [128, 192])
        lxt_s = sb("lxt_s", [128, 192])
        lgt = sb("lgt", [128, 192])
        zt = sb("zt", [128, 128])
        at = sb("at", [128, 128])
        pe_ = sb("pe_", [128, 128])
        ht = sb("ht", [128, 128])
        spre = sb("spre", [128, 32])
        spost = sb("spost", [128, 32])
        hpre = sb("hpre", [128, 32])
        hpost = sb("hpost", [128, 32])
        ev = sb("ev", [128, 32])
        eu = sb("eu", [128, 32])
        rs = sb("rs", [128, 32])
        zm = sb("zm", [128, 8])
        mu2c = sb("mu2c", [128, 8])
        r2c = sb("r2c", [128, 8])
        nmu2r2c = sb("nmu2r2c", [128, 8])
        spc = sb("spc", [128, 8])
        sqc = sb("sqc", [128, 8])
        mupc = sb("mupc", [128, 8])
        nmupc = sb("nmupc", [128, 8])
        varpc = sb("varpc", [128, 8])
        rpc = sb("rpc", [128, 8])
        alphac = sb("alphac", [128, 8])
        alphatc = sb("alphatc", [128, 8])
        t8a = sb("t8a", [128, 8])
        t8b = sb("t8b", [128, 8])
        s3c = sb("s3c", [128, 1])
        q3c = sb("q3c", [128, 1])
        mu3c = sb("mu3c", [128, 1])
        r3c = sb("r3c", [128, 1])
        nmu3r3c = sb("nmu3r3c", [128, 1])
        t1c = sb("t1c", [128, 1])
        scrb = sb("scrb", [128, D], BF16)
        wsfrep = sb("wsfrep", [128, 24])
        hrbrep = sb("hrbrep", [128, 16])
        hpbrep = sb("hpbrep", [128, 4])
        hobrep = sb("hobrep", [128, 4])
        eyerep = sb("eyerep", [128, 16])
        igrep = sb("igrep", [128, 16])
        gcol = sb("gcol", [128, 1])
        omgc = sb("omgc", [128, 1])
        csts = sb("csts", [128, 4])
        ptb = ctx.enter_context(nc.psum_tensor("ptb", [128, 128], BF16))
        py = [ctx.enter_context(nc.psum_tensor(f"py{i}", [128, 512], F32))
              for i in range(2)]
        sems = {n: ctx.enter_context(nc.semaphore(n)) for n in SEMNAMES}
        block = ctx.enter_context(nc.Block())

        S = Sched()
        S.sems(SEMNAMES)
        for j, v in enumerate(CONSTS):
            nc.const_aps.register(F32, v, csts[:, j:j + 1])
        for j, v in enumerate(CONSTS):
            S.run("vector", lambda e, j=j, v=v: e.memset(csts[:, j:j + 1], v), "VS")
        S.wait("scalar", "VS", S.cnt["VS"])

        xt = [_xt[:, (i % 2) * D:((i % 2) + 1) * D] for i in range(NTT)]
        yt = [_yt[:, (i % 2) * D:((i % 2) + 1) * D] for i in range(NTT)]
        wbb = [_wbb[:, i * 1024:(i + 1) * 1024] for i in range(8)]

        for fn in (
            lambda e: e.dma_start(out=lnps_s[:, :], in_=lnps),
            lambda e: e.dma_start(out=lyt_s[:, :], in_=lyt),
            lambda e: e.dma_start(out=lxt_s[:, :], in_=lxt),
        ):
            S.run("sync", fn, "LD", 16)
        for fn in (
            lambda e: e.dma_start(out=wsfrep[:, :], in_=wsumf.broadcast_to([128, 24])),
            lambda e: e.dma_start(out=hrbrep[:, :], in_=hrb.broadcast_to([128, 16])),
            lambda e: e.dma_start(out=hpbrep[:, :], in_=hpb.broadcast_to([128, 4])),
            lambda e: e.dma_start(out=hobrep[:, :], in_=hob.broadcast_to([128, 4])),
            lambda e: e.dma_start(out=eyerep[:, :], in_=eye16.broadcast_to([128, 16])),
            lambda e: e.dma_start(out=gcol[:, :], in_=gatei.broadcast_to([128, 1])),
            lambda e: e.dma_start(out=idn[:, :], in_=ident),
        ):
            S.run("gpsimd", fn, "GD", 16)
        for i in range(8):
            S.run("gpsimd", lambda e, i=i: e.dma_start(
                out=wbb[i], in_=wb[i * 128:(i + 1) * 128, :]), "GD", 16)
        ld_once = S.cnt["LD"]
        gd_once = S.cnt["GD"]
        S.wait("vector", "GD", gd_once)
        S.run("vector", lambda e: e.tensor_scalar(
            out=hrbrep[:, :], in0=hrbrep[:, :], scalar1=1.0 / TAU, scalar2=None,
            op0=ALU.mult), "VS")
        vs_hrb = S.cnt["VS"]

        S.wait("vector", "LD", ld_once)
        S.run("vector", lambda e: e.tensor_reduce(
            out=sc[:, :], in_=lnps_s.rearrange("p (r c) -> p r c", c=8),
            axis=mybir.AxisListType.X, op=ALU.add), "VS")
        S.run("vector", lambda e: e.tensor_reduce(
            out=lyc[:, :], in_=lyt_s.rearrange("p (r c) -> p r c", c=8),
            axis=mybir.AxisListType.X, op=ALU.add), "VS")
        S.wait("vector", "VS", S.cnt["VS"])
        sview = sc.rearrange("p (t r) -> p t r", r=2)
        S.run("vector", lambda e: e.tensor_scalar(
            out=mu2c[:, :], in0=sview[:, :, 0], scalar1=1.0 / D, scalar2=None,
            op0=ALU.mult), "VS")
        S.run("vector", lambda e: e.tensor_scalar(
            out=t8a[:, :], in0=sview[:, :, 1], scalar1=1.0 / D, scalar2=None,
            op0=ALU.mult), "VS")
        S.wait("vector", "VS", S.cnt["VS"])
        S.run("vector", lambda e: e.tensor_tensor(
            out=t8b[:, :], in0=mu2c[:, :], in1=mu2c[:, :], op=ALU.mult), "VS")
        S.wait("vector", "VS", S.cnt["VS"])
        S.run("vector", lambda e: e.tensor_tensor(
            out=t8a[:, :], in0=t8a[:, :], in1=t8b[:, :], op=ALU.subtract), "VS")
        vs_var2 = S.cnt["VS"]
        S.wait("vector", "VS", vs_var2)
        S.wait("scalar", "VS", vs_var2)
        S.run("scalar", lambda e: e.activation(
            out=t8b[:, :], in_=t8a[:, :], func=AF.Sqrt, bias=LN_EPS), "SS")
        ss_sd2 = S.cnt["SS"]
        S.wait("vector", "SS", ss_sd2)
        S.run("vector", lambda e: e.reciprocal(out=r2c[:, :], in_=t8b[:, :]), "VS")
        S.wait("vector", "VS", S.cnt["VS"])
        S.run("vector", lambda e: e.tensor_tensor(
            out=nmu2r2c[:, :], in0=mu2c[:, :], in1=r2c[:, :], op=ALU.mult), "VS")
        S.wait("vector", "VS", S.cnt["VS"])
        S.run("vector", lambda e: e.tensor_scalar(
            out=nmu2r2c[:, :], in0=nmu2r2c[:, :], scalar1=-1.0, scalar2=None,
            op0=ALU.mult), "VS")
        vs_ln2 = S.cnt["VS"]
        S.wait("vector", "VS", vs_ln2)

        ldA = [0] * NTT
        ssA = [0] * NTT
        vsA = [0] * NTT

        def loadA(tl):
            S.run("sync", lambda e, tl=tl: e.dma_start(
                out=xt[tl], in_=xtok[tl * 128:(tl + 1) * 128, :]), "LD", 16)
            S.run("sync", lambda e, tl=tl: e.dma_start(
                out=yt[tl], in_=y2tok[tl * 128:(tl + 1) * 128, :]), "LD", 16)
            ldA[tl] = S.cnt["LD"]

        loadA(0)
        loadA(1)
        for tl in range(NTT):
            S.wait("vector", "LD", ldA[tl])
            if tl >= 1:
                S.wait("vector", "SS", ssA[tl - 1])
            S.run("vector", lambda e, tl=tl: e.tensor_scalar(
                out=x2t[:, :], in0=yt[tl], scalar1=r2c[:, tl:tl + 1],
                scalar2=nmu2r2c[:, tl:tl + 1], op0=ALU.mult, op1=ALU.add), "VS")
            S.wait("vector", "VS", S.cnt["VS"])
            S.run("vector", lambda e, tl=tl: e.tensor_tensor(
                out=x2t[:, :], in0=x2t[:, :], in1=xt[tl], op=ALU.add), "VS")
            vsA[tl] = S.cnt["VS"]
            S.wait("scalar", "VS", vsA[tl])
            S.run("scalar", lambda e, tl=tl: e.activation(
                out=scrb[:, :], in_=x2t[:, :], func=AF.Copy, bias=0.0,
                accum_out=spc[:, tl:tl + 1]), "SS")
            S.wait("scalar", "SS", S.cnt["SS"])
            S.run("scalar", lambda e, tl=tl: e.activation(
                out=scrb[:, :], in_=x2t[:, :], func=AF.Square,
                accum_out=sqc[:, tl:tl + 1]), "SS")
            ssA[tl] = S.cnt["SS"]
            S.wait("scalar", "SS", ssA[tl])
            if tl + 2 < NTT:
                S.wait("sync", "VS", vsA[tl])
                loadA(tl + 2)
        ss_passa = S.cnt["SS"]

        S.wait("vector", "SS", ss_passa)
        S.run("vector", lambda e: e.tensor_scalar(
            out=mupc[:, :], in0=spc[:, :], scalar1=1.0 / D, scalar2=None,
            op0=ALU.mult), "VS")
        S.run("vector", lambda e: e.tensor_scalar(
            out=varpc[:, :], in0=sqc[:, :], scalar1=1.0 / D, scalar2=None,
            op0=ALU.mult), "VS")
        S.wait("vector", "VS", S.cnt["VS"])
        S.run("vector", lambda e: e.tensor_scalar(
            out=nmupc[:, :], in0=mupc[:, :], scalar1=-1.0, scalar2=None,
            op0=ALU.mult), "VS")
        S.run("vector", lambda e: e.tensor_tensor(
            out=t8a[:, :], in0=mupc[:, :], in1=mupc[:, :], op=ALU.mult), "VS")
        S.wait("vector", "VS", S.cnt["VS"])
        S.run("vector", lambda e: e.tensor_tensor(
            out=varpc[:, :], in0=varpc[:, :], in1=t8a[:, :], op=ALU.subtract), "VS")
        S.wait("vector", "VS", S.cnt["VS"])
        S.run("vector", lambda e: e.tensor_scalar(
            out=t8a[:, :], in0=varpc[:, :], scalar1=LN_EPS, scalar2=None,
            op0=ALU.add), "VS")
        vs_q = S.cnt["VS"]
        S.wait("scalar", "VS", vs_q)
        S.run("scalar", lambda e: e.activation(
            out=t8b[:, :], in_=t8a[:, :], func=AF.Sqrt), "SS")
        S.wait("vector", "SS", S.cnt["SS"])
        S.run("vector", lambda e: e.reciprocal(out=rpc[:, :], in_=t8b[:, :]), "VS")
        S.wait("vector", "VS", S.cnt["VS"])
        S.run("vector", lambda e: e.reciprocal(out=t8b[:, :], in_=t8a[:, :]), "VS")
        S.wait("vector", "VS", S.cnt["VS"])
        S.run("vector", lambda e: e.tensor_tensor(
            out=t8b[:, :], in0=varpc[:, :], in1=t8b[:, :], op=ALU.mult), "VS")
        S.wait("vector", "VS", S.cnt["VS"])
        S.run("vector", lambda e: e.tensor_scalar(
            out=t8b[:, :], in0=t8b[:, :], scalar1=RMS_EPS, scalar2=None,
            op0=ALU.add), "VS")
        vs_fr = S.cnt["VS"]
        S.wait("scalar", "VS", vs_fr)
        S.run("scalar", lambda e: e.activation(
            out=t8b[:, :], in_=t8b[:, :], func=AF.Sqrt), "SS")
        S.wait("vector", "SS", S.cnt["SS"])
        S.run("vector", lambda e: e.reciprocal(out=t8a[:, :], in_=t8b[:, :]), "VS")
        S.wait("vector", "VS", S.cnt["VS"])
        S.run("vector", lambda e: e.tensor_tensor(
            out=alphac[:, :], in0=t8a[:, :], in1=rpc[:, :], op=ALU.mult), "VS")
        S.wait("vector", "VS", S.cnt["VS"])
        S.run("vector", lambda e: e.tensor_scalar(
            out=alphatc[:, :], in0=alphac[:, :], scalar1=1.0 / TAU, scalar2=None,
            op0=ALU.mult), "VS")
        vs_alpha = S.cnt["VS"]
        S.wait("vector", "VS", vs_alpha)
        S.wait("vector", "VS", vs_hrb)

        for tl in range(NTT):
            lysl = lyc[:, tl * 24:(tl + 1) * 24]
            lxsl = lxt_s[:, tl * 24:(tl + 1) * 24]
            lgsl = lgt[:, tl * 24:(tl + 1) * 24]
            S.run("vector", lambda e, lysl=lysl, lxsl=lxsl, lgsl=lgsl, tl=tl:
                  e.scalar_tensor_tensor(
                      out=lgsl, in0=lysl, scalar=r2c[:, tl:tl + 1], in1=lxsl,
                      op0=ALU.mult, op1=ALU.add), "VS")
            S.wait("vector", "VS", S.cnt["VS"])
            S.run("vector", lambda e, lgsl=lgsl, tl=tl: e.scalar_tensor_tensor(
                out=lgsl, in0=wsfrep[:, :], scalar=nmu2r2c[:, tl:tl + 1],
                in1=lgsl, op0=ALU.mult, op1=ALU.add), "VS")
            S.wait("vector", "VS", S.cnt["VS"])
            S.run("vector", lambda e, lgsl=lgsl, tl=tl: e.scalar_tensor_tensor(
                out=lgsl, in0=wsfrep[:, :], scalar=nmupc[:, tl:tl + 1],
                in1=lgsl, op0=ALU.mult, op1=ALU.add), "VS")
            S.wait("vector", "VS", S.cnt["VS"])
            S.run("vector", lambda e, lgsl=lgsl, tl=tl: e.scalar_tensor_tensor(
                out=zt[:, tl * 16:(tl + 1) * 16], in0=lgsl[:, 0:16],
                scalar=alphatc[:, tl:tl + 1], in1=hrbrep[:, :],
                op0=ALU.mult, op1=ALU.add), "VS")
            S.run("vector", lambda e, lgsl=lgsl, tl=tl: e.scalar_tensor_tensor(
                out=spre[:, tl * 4:(tl + 1) * 4], in0=lgsl[:, 16:20],
                scalar=alphac[:, tl:tl + 1], in1=hpbrep[:, :],
                op0=ALU.mult, op1=ALU.add), "VS")
            S.run("vector", lambda e, lgsl=lgsl, tl=tl: e.scalar_tensor_tensor(
                out=spost[:, tl * 4:(tl + 1) * 4], in0=lgsl[:, 20:24],
                scalar=alphac[:, tl:tl + 1], in1=hobrep[:, :],
                op0=ALU.mult, op1=ALU.add), "VS")
            S.wait("vector", "VS", S.cnt["VS"])
        vs_zbuild = S.cnt["VS"]

        S.wait("vector", "VS", vs_zbuild)
        S.run("vector", lambda e: e.tensor_reduce(
            out=zm[:, :], in_=zt.rearrange("p (t k) -> p t k", k=16),
            axis=mybir.AxisListType.X, op=ALU.max), "VS")
        S.wait("vector", "VS", S.cnt["VS"])
        S.run("vector", lambda e: e.tensor_tensor(
            out=zt.rearrange("p (t k) -> p t k", k=16),
            in0=zt.rearrange("p (t k) -> p t k", k=16),
            in1=zm.rearrange("p (t o) -> p t o", o=1).broadcast_to([128, 8, 16]),
            op=ALU.subtract), "VS")
        vs_zshift = S.cnt["VS"]
        S.wait("vector", "VS", vs_zshift)

        zt_ij = zt.rearrange("p (t i j) -> p t i j", i=4, j=4)
        pe_ij = pe_.rearrange("p (t i j) -> p t i j", i=4, j=4)
        nu = eu
        nv = ev
        mx = rs
        sm_ = sb("sm_", [128, 32])
        S.run("vector", lambda e: e.memset(nv[:, :], 0.0), "VS")
        S.wait("vector", "VS", S.cnt["VS"])

        def half_step(bc_in, bc_dims, red_view, out_col):
            S.run("vector", lambda e: e.tensor_tensor(
                out=pe_ij, in0=zt_ij, in1=bc_in, op=ALU.subtract), "VS")
            S.wait("vector", "VS", S.cnt["VS"])
            S.run("vector", lambda e: e.tensor_reduce(
                out=mx[:, :], in_=red_view(pe_), axis=mybir.AxisListType.X,
                op=ALU.max), "VS")
            S.wait("vector", "VS", S.cnt["VS"])
            S.run("vector", lambda e: e.tensor_tensor(
                out=pe_ij, in0=pe_ij, in1=bc_dims(mx), op=ALU.subtract), "VS")
            vs_sub = S.cnt["VS"]
            S.wait("vector", "VS", vs_sub)
            S.wait("scalar", "VS", vs_sub)
            S.run("scalar", lambda e: e.activation(
                out=pe_[:, :], in_=pe_[:, :], func=AF.Exp), "SS")
            ss_e = S.cnt["SS"]
            S.wait("scalar", "SS", ss_e)
            S.wait("vector", "SS", ss_e)
            S.run("vector", lambda e: e.tensor_reduce(
                out=sm_[:, :], in_=red_view(pe_), axis=mybir.AxisListType.X,
                op=ALU.add), "VS")
            vs_s = S.cnt["VS"]
            S.wait("vector", "VS", vs_s)
            S.wait("scalar", "VS", vs_s)
            S.run("scalar", lambda e: e.activation(
                out=sm_[:, :], in_=sm_[:, :], func=AF.Ln), "SS")
            ss_l = S.cnt["SS"]
            S.wait("vector", "SS", ss_l)
            S.run("vector", lambda e: e.tensor_tensor(
                out=out_col[:, :], in0=mx[:, :], in1=sm_[:, :], op=ALU.add), "VS")
            S.wait("vector", "VS", S.cnt["VS"])

        bc_j_of = lambda col: col.rearrange(
            "p (t o j) -> p t o j", o=1, j=4).broadcast_to([128, 8, 4, 4])
        bc_i_of = lambda col: col.rearrange(
            "p (t i o) -> p t i o", o=1, i=4).broadcast_to([128, 8, 4, 4])
        bc_ti = lambda col: col.rearrange(
            "p (t i o) -> p t i o", o=1, i=4).broadcast_to([128, 8, 4, 4])
        def bc_tj(col):
            return col.rearrange("p (t j) -> p t j", j=4) \
                .rearrange("p (t o) j -> p t o j", o=1) \
                .broadcast_to([128, 8, 4, 4])
        red_j = lambda buf: buf.rearrange("p (ti j) -> p ti j", j=4)
        red_i = lambda buf: buf.rearrange("p (t i j) -> p t j i", i=4, j=4)

        for it in range(50):
            half_step(bc_tj(nv), bc_ti, red_j, nu)
            half_step(bc_ti(nu), bc_tj, red_i, nv)
        S.run("vector", lambda e: e.tensor_tensor(
            out=pe_ij, in0=zt_ij, in1=bc_ti(nu), op=ALU.subtract), "VS")
        S.wait("vector", "VS", S.cnt["VS"])
        S.run("vector", lambda e: e.tensor_tensor(
            out=pe_ij, in0=pe_ij, in1=bc_tj(nv), op=ALU.subtract), "VS")
        vs_parg = S.cnt["VS"]
        S.wait("scalar", "VS", vs_parg)
        S.run("scalar", lambda e: e.activation(
            out=pe_[:, :], in_=pe_[:, :], func=AF.Exp), "SS")
        ss_P = S.cnt["SS"]
        S.wait("vector", "SS", ss_P)
        vs_P = S.cnt["VS"]
        S.wait("scalar", "GD", gd_once)
        S.run("scalar", lambda e: e.activation(
            out=gcol[:, :], in_=gcol[:, :], func=AF.Sigmoid), "SS")
        ss_g = S.cnt["SS"]
        S.wait("vector", "SS", ss_g)
        S.run("vector", lambda e: e.tensor_scalar(
            out=omgc[:, :], in0=gcol[:, :], scalar1=-1.0, scalar2=1.0,
            op0=ALU.mult, op1=ALU.add), "VS")
        S.wait("vector", "VS", S.cnt["VS"])
        S.run("vector", lambda e: e.tensor_scalar(
            out=igrep[:, :], in0=eyerep[:, :], scalar1=omgc[:, 0:1], scalar2=None,
            op0=ALU.mult), "VS")
        S.wait("vector", "VS", S.cnt["VS"])
        S.run("vector", lambda e: e.scalar_tensor_tensor(
            out=ht.rearrange("p (t k) -> p t k", k=16),
            in0=pe_.rearrange("p (t k) -> p t k", k=16),
            scalar=gcol[:, 0:1],
            in1=igrep.rearrange("p (o k) -> p o k", o=1).broadcast_to([128, 8, 16]),
            op0=ALU.mult, op1=ALU.add), "VS")
        vs_H = S.cnt["VS"]
        S.wait("vector", "VS", vs_H)

        for (src, dst) in ((spre, hpre), (spost, hpost)):
            S.run("vector", lambda e, src=src: e.tensor_reduce(
                out=zm[:, :], in_=src.rearrange("p (t k) -> p t k", k=4),
                axis=mybir.AxisListType.X, op=ALU.max), "VS")
            S.wait("vector", "VS", S.cnt["VS"])
            S.run("vector", lambda e, src=src: e.tensor_tensor(
                out=src.rearrange("p (t k) -> p t k", k=4),
                in0=src.rearrange("p (t k) -> p t k", k=4),
                in1=zm.rearrange("p (t o) -> p t o", o=1).broadcast_to([128, 8, 4]),
                op=ALU.subtract), "VS")
            vs_sm = S.cnt["VS"]
            S.wait("scalar", "VS", vs_sm)
            S.run("scalar", lambda e, src=src: e.activation(
                out=src[:, :], in_=src[:, :], func=AF.Exp), "SS")
            ss_sm = S.cnt["SS"]
            S.wait("vector", "SS", ss_sm)
            S.run("vector", lambda e, src=src: e.tensor_reduce(
                out=zm[:, :], in_=src.rearrange("p (t k) -> p t k", k=4),
                axis=mybir.AxisListType.X, op=ALU.add), "VS")
            S.wait("vector", "VS", S.cnt["VS"])
            S.run("vector", lambda e: e.reciprocal(out=zm[:, :], in_=zm[:, :]), "VS")
            S.wait("vector", "VS", S.cnt["VS"])
            S.run("vector", lambda e, src=src, dst=dst: e.tensor_tensor(
                out=dst.rearrange("p (t k) -> p t k", k=4),
                in0=src.rearrange("p (t k) -> p t k", k=4),
                in1=zm.rearrange("p (t o) -> p t o", o=1).broadcast_to([128, 8, 4]),
                op=ALU.mult), "VS")
            S.wait("vector", "VS", S.cnt["VS"])
        vs_hs = S.cnt["VS"]

        S.wait("gpsimd", "VS", vs_hs)
        S.run("gpsimd", lambda e: e.dma_start(out=o_dbg[:, 0:192], in_=lgt[:, :]), "GD", 16)
        S.run("gpsimd", lambda e: e.dma_start(out=o_dbg[:, 192:320], in_=ht[:, :]), "GD", 16)
        S.run("gpsimd", lambda e: e.dma_start(out=o_dbg[:, 320:352], in_=hpre[:, :]), "GD", 16)
        S.run("gpsimd", lambda e: e.dma_start(out=o_dbg[:, 352:384], in_=hpost[:, :]), "GD", 16)
        S.run("gpsimd", lambda e: e.dma_start(out=o_dbg[:, 384:392], in_=mupc[:, :]), "GD", 16)
        S.run("gpsimd", lambda e: e.dma_start(out=o_dbg[:, 392:400], in_=varpc[:, :]), "GD", 16)
        S.run("gpsimd", lambda e: e.dma_start(out=o_dbg[:, 400:408], in_=alphac[:, :]), "GD", 16)
        S.run("gpsimd", lambda e: e.dma_start(out=o_dbg[:, 408:416], in_=r2c[:, :]), "GD", 16)
        S.run("gpsimd", lambda e: e.dma_start(out=o_dbg[:, 416:424], in_=mu2c[:, :]), "GD", 16)

        ldB = [0] * NTT
        vsB = [0] * NTT

        def loadB(tl, wait_vs):
            S.wait("sync", "VS", wait_vs)
            S.run("sync", lambda e, tl=tl: e.dma_start(
                out=xt[tl], in_=xtok[tl * 128:(tl + 1) * 128, :]), "LD", 16)
            S.run("sync", lambda e, tl=tl: e.dma_start(
                out=yt[tl], in_=y2tok[tl * 128:(tl + 1) * 128, :]), "LD", 16)
            ldB[tl] = S.cnt["LD"]

        loadB(0, vsA[NTT - 1])
        loadB(1, vsA[NTT - 1])
        ts_tr_prev = 0
        vs_xpT_evac = 0
        vs_yb_evac = 0
        gd_x3 = 0
        for tl in range(NTT):
            S.wait("vector", "LD", ldB[tl])
            S.wait("vector", "VS", vs_hs)
            S.wait("vector", "GD", gd_x3)
            S.run("vector", lambda e, tl=tl: e.tensor_scalar(
                out=x2t[:, :], in0=yt[tl], scalar1=r2c[:, tl:tl + 1],
                scalar2=nmu2r2c[:, tl:tl + 1], op0=ALU.mult, op1=ALU.add), "VS")
            S.wait("vector", "VS", S.cnt["VS"])
            S.run("vector", lambda e, tl=tl: e.tensor_tensor(
                out=x2t[:, :], in0=x2t[:, :], in1=xt[tl], op=ALU.add), "VS")
            vsB[tl] = S.cnt["VS"]
            S.wait("vector", "VS", vsB[tl])
            if tl + 2 < NTT:
                loadB(tl + 2, vsB[tl])
            S.run("vector", lambda e, tl=tl: e.tensor_scalar(
                out=xint[:, :], in0=x2t[:, :], scalar1=nmupc[:, tl:tl + 1],
                scalar2=rpc[:, tl:tl + 1], op0=ALU.add, op1=ALU.mult), "VS")
            S.wait("vector", "VS", S.cnt["VS"])
            for i in range(4):
                S.run("vector", lambda e, tl=tl, i=i: e.tensor_scalar(
                    out=xmix[:, i * 1024:(i + 1) * 1024],
                    in0=xint[:, 0:1024],
                    scalar1=ht[:, tl * 16 + i * 4:tl * 16 + i * 4 + 1],
                    scalar2=None, op0=ALU.mult), "VS")
            S.wait("vector", "VS", S.cnt["VS"])
            for j in range(1, 4):
                for i in range(4):
                    S.run("vector", lambda e, tl=tl, i=i, j=j: e.scalar_tensor_tensor(
                        out=xmix[:, i * 1024:(i + 1) * 1024],
                        in0=xint[:, j * 1024:(j + 1) * 1024],
                        scalar=ht[:, tl * 16 + i * 4 + j:tl * 16 + i * 4 + j + 1],
                        in1=xmix[:, i * 1024:(i + 1) * 1024],
                        op0=ALU.mult, op1=ALU.add), "VS")
                S.wait("vector", "VS", S.cnt["VS"])
            S.run("vector", lambda e, tl=tl: e.tensor_scalar(
                out=xpre32[:, :], in0=xint[:, 0:1024],
                scalar1=hpre[:, tl * 4:tl * 4 + 1], scalar2=None,
                op0=ALU.mult), "VS")
            S.wait("vector", "VS", S.cnt["VS"])
            for n in range(1, 4):
                S.run("vector", lambda e, tl=tl, n=n: e.scalar_tensor_tensor(
                    out=xpre32[:, :], in0=xint[:, n * 1024:(n + 1) * 1024],
                    scalar=hpre[:, tl * 4 + n:tl * 4 + n + 1], in1=xpre32[:, :],
                    op0=ALU.mult, op1=ALU.add), "VS")
                S.wait("vector", "VS", S.cnt["VS"])
            S.wait("vector", "TS", ts_tr_prev)
            S.run("vector", lambda e: e.tensor_copy(
                out=xpreb[:, :], in_=xpre32[:, :]), "VS")
            vs_xpreb = S.cnt["VS"]
            S.wait("vector", "VS", vs_xpreb)
            vs_ev = vs_xpT_evac
            for k in range(8):
                S.wait("tensor", "VS", vs_xpreb if k == 0 else vs_ev)
                S.run("tensor", lambda e, k=k: e.transpose(
                    ptb[:, :], xpreb[:, k * 128:(k + 1) * 128], idn[:, :]), "TS")
                ts_tr = S.cnt["TS"]
                S.wait("vector", "TS", ts_tr)
                S.run("vector", lambda e, k=k: e.tensor_copy(
                    out=xpT[:, k * 128:(k + 1) * 128], in_=ptb[:, :]), "VS")
                vs_ev = S.cnt["VS"]
                S.wait("vector", "VS", vs_ev)
            ts_tr_prev = S.cnt["TS"]
            vs_xpT_evac = vs_ev
            S.wait("tensor", "VS", vs_xpT_evac)
            S.wait("tensor", "GD", gd_once)
            S.wait("tensor", "VS", vs_yb_evac)
            for h in range(2):
                for k in range(8):
                    S.run("tensor", lambda e, k=k, h=h: e.matmul(
                        py[h][:, :], xpT[:, k * 128:(k + 1) * 128],
                        wbb[k][:, h * 512:(h + 1) * 512],
                        start=(k == 0), stop=(k == 7)),
                        "TS" if k == 7 else None)
            ts_mm = S.cnt["TS"]
            S.wait("vector", "TS", ts_mm)
            for h in range(2):
                S.run("vector", lambda e, h=h: e.tensor_copy(
                    out=yb[:, h * 512:(h + 1) * 512], in_=py[h][:, :]), "VS")
            S.wait("vector", "VS", S.cnt["VS"])
            vs_yb_evac = S.cnt["VS"]
            for n in range(4):
                S.run("vector", lambda e, tl=tl, n=n: e.scalar_tensor_tensor(
                    out=xmix[:, n * 1024:(n + 1) * 1024], in0=yb[:, :],
                    scalar=hpost[:, tl * 4 + n:tl * 4 + n + 1],
                    in1=xmix[:, n * 1024:(n + 1) * 1024],
                    op0=ALU.mult, op1=ALU.add), "VS")
            vs_ffn = S.cnt["VS"]
            S.wait("vector", "VS", vs_ffn)
            S.wait("scalar", "VS", vs_ffn)
            S.run("scalar", lambda e: e.activation(
                out=scrb[:, :], in_=xmix[:, :], func=AF.Copy, bias=0.0,
                accum_out=s3c[:, 0:1]), "SS")
            S.wait("scalar", "SS", S.cnt["SS"])
            S.run("scalar", lambda e: e.activation(
                out=scrb[:, :], in_=xmix[:, :], func=AF.Square,
                accum_out=q3c[:, 0:1]), "SS")
            ss_ln3 = S.cnt["SS"]
            S.wait("vector", "SS", ss_ln3)
            S.run("vector", lambda e: e.tensor_scalar(
                out=mu3c[:, :], in0=s3c[:, :], scalar1=1.0 / D, scalar2=None,
                op0=ALU.mult), "VS")
            S.run("vector", lambda e: e.tensor_scalar(
                out=t1c[:, :], in0=q3c[:, :], scalar1=1.0 / D, scalar2=None,
                op0=ALU.mult), "VS")
            S.wait("vector", "VS", S.cnt["VS"])
            S.run("vector", lambda e: e.tensor_tensor(
                out=r3c[:, :], in0=mu3c[:, :], in1=mu3c[:, :], op=ALU.mult), "VS")
            S.wait("vector", "VS", S.cnt["VS"])
            S.run("vector", lambda e: e.tensor_tensor(
                out=t1c[:, :], in0=t1c[:, :], in1=r3c[:, :], op=ALU.subtract), "VS")
            vs_v3 = S.cnt["VS"]
            S.wait("scalar", "VS", vs_v3)
            S.run("scalar", lambda e: e.activation(
                out=t1c[:, :], in_=t1c[:, :], func=AF.Sqrt, bias=LN_EPS), "SS")
            ss_sd3 = S.cnt["SS"]
            S.wait("vector", "SS", ss_sd3)
            S.run("vector", lambda e: e.reciprocal(out=r3c[:, :], in_=t1c[:, :]), "VS")
            S.wait("vector", "VS", S.cnt["VS"])
            S.run("vector", lambda e: e.tensor_tensor(
                out=nmu3r3c[:, :], in0=mu3c[:, :], in1=r3c[:, :], op=ALU.mult), "VS")
            S.wait("vector", "VS", S.cnt["VS"])
            S.run("vector", lambda e: e.tensor_scalar(
                out=nmu3r3c[:, :], in0=nmu3r3c[:, :], scalar1=-1.0, scalar2=None,
                op0=ALU.mult), "VS")
            S.wait("vector", "VS", S.cnt["VS"])
            S.wait("vector", "GD", gd_x3)
            S.run("vector", lambda e: e.tensor_scalar(
                out=xmix[:, :], in0=xmix[:, :], scalar1=r3c[:, 0:1],
                scalar2=nmu3r3c[:, 0:1], op0=ALU.mult, op1=ALU.add), "VS")
            S.wait("vector", "VS", S.cnt["VS"])
            S.run("vector", lambda e: e.tensor_tensor(
                out=x2t[:, :], in0=x2t[:, :], in1=xmix[:, :], op=ALU.add), "VS")
            vs_x3 = S.cnt["VS"]
            S.wait("gpsimd", "VS", vs_x3)
            S.run("gpsimd", lambda e, tl=tl: e.dma_start(
                out=o_x3[tl * 128:(tl + 1) * 128, :], in_=x2t[:, :]), "GD", 16)
            gd_x3 = S.cnt["GD"]
        S.wait("sync", "GD", S.cnt["GD"])

        S.emit(block, sems)
    return nc


_L3 = None


def _get_l3():
    global _L3
    if _L3 is None:
        _L3 = build_l3()
    return _L3


def run_l3(x2d, y2_full, lnp_parts, ly_parts, lx_parts, wsumf, hres_base,
           hpre_base, hpost_base, gate, wbr):
    nc = _get_l3()
    eye = np.eye(4, dtype=np.float32).reshape(1, 16)
    idn = np.eye(128, dtype=np.float32)
    lnp_st = np.stack(lnp_parts, axis=-1)
    ly_st = np.stack(ly_parts, axis=-1)
    in_maps = []
    for c in range(NCORES):
        tsl = slice(c * TOK, (c + 1) * TOK)
        lp = lnp_st[:, tsl, :].reshape(2, 8, 128, 8).transpose(2, 1, 0, 3) \
            .reshape(128, 16 * 8)
        lyp = ly_st[:, tsl, :].reshape(24, 8, 128, 8).transpose(2, 1, 0, 3) \
            .reshape(128, 192 * 8)
        lxp = lx_parts[c].reshape(24, 8, 128).transpose(2, 1, 0).reshape(128, 192)
        in_maps.append(dict(
            xtok=np.ascontiguousarray(x2d[tsl]),
            y2tok=np.ascontiguousarray(y2_full[:, tsl].T),
            lnps=np.ascontiguousarray(lp),
            lyt=np.ascontiguousarray(lyp),
            lxt=np.ascontiguousarray(lxp),
            wsumf=wsumf.reshape(1, 24),
            hrb=hres_base.reshape(1, 16),
            hpb=hpre_base.reshape(1, 4),
            hob=hpost_base.reshape(1, 4),
            gatei=gate.reshape(1, 1),
            eye16=eye,
            ident=idn,
            wb=wbr,
        ))
    res = run_bass_kernel_spmd(nc, in_maps, core_ids=list(range(NCORES)))
    return res.results


def kernel(L0_attn_x, L0_attn_A_log, L0_attn_D, L0_attn_x_proj_w,
           L0_attn_dt_proj_w, L0_attn_dt_proj_b, L0_ffn_W_res, L0_ffn_W_pre,
           L0_ffn_W_post, L0_ffn_H_res_base, L0_ffn_H_pre_base,
           L0_ffn_H_post_base, L0_ffn_gate, L0_ffn_W_branch):
    f32 = np.float32
    x2d = np.ascontiguousarray(np.asarray(L0_attn_x, f32).reshape(TT, D))
    xpw = np.asarray(L0_attn_x_proj_w, f32)
    wcat = np.ascontiguousarray(np.concatenate(
        [np.asarray(L0_ffn_W_res, f32), np.asarray(L0_ffn_W_pre, f32),
         np.asarray(L0_ffn_W_post, f32)], axis=0))

    r1 = run_l1(x2d, xpw, wcat)
    xp_full = np.ascontiguousarray(
        np.concatenate([r["xpraw"] for r in r1], axis=1))
    mean = np.concatenate([r["mcol"].T.reshape(-1) for r in r1])
    rstd = np.concatenate([r["rcol"].T.reshape(-1) for r in r1])
    mur = np.concatenate([r["mrcol"].T.reshape(-1) for r in r1])
    st1 = np.ascontiguousarray(np.stack([mean, rstd, mur]).astype(f32))
    wsum = r1[0]["wsums"][0, :R].astype(f32)
    wsumf = r1[0]["wsums"][0, R:].astype(f32)
    lx_parts = [r["lx"] for r in r1]

    r2 = run_l2(x2d, xp_full, st1, np.asarray(L0_attn_A_log, f32),
                np.asarray(L0_attn_D, f32), np.asarray(L0_attn_dt_proj_w, f32),
                np.asarray(L0_attn_dt_proj_b, f32), wcat, wsum)
    y2_full = np.concatenate([r["y2"] for r in r2], axis=0)
    lnp_parts = [r["lnp"] for r in r2]
    ly_parts = [r["ly"] for r in r2]

    r3 = run_l3(x2d, y2_full, lnp_parts, ly_parts, lx_parts, wsumf,
                np.asarray(L0_ffn_H_res_base, f32),
                np.asarray(L0_ffn_H_pre_base, f32),
                np.asarray(L0_ffn_H_post_base, f32),
                np.asarray(L0_ffn_gate, f32),
                np.ascontiguousarray(np.asarray(L0_ffn_W_branch, f32)))
    out = np.concatenate([r["x3"] for r in r3], axis=0)
    return out.reshape(B, L, D)


# version 07
# speedup vs baseline: 1.3316x; 1.0731x over previous
from contextlib import ExitStack

import numpy as np
import concourse.bass as bass
import concourse.mybir as mybir
from concourse.bass_utils import run_bass_kernel_spmd

F32 = mybir.dt.float32
BF16 = mybir.dt.bfloat16
AF = mybir.ActivationFunctionType
ALU = mybir.AluOpType

NCORES = 8
B, L, D = 2, 4096, 4096
TT = B * L
TOK = TT // NCORES
DC = D // NCORES
N = 16
R = 160
DTR = 128
CH = 256
NCH = TT // CH
CHB = L // CH
LN_EPS = 1e-5
RMS_EPS = 1.1920929e-07
TAU = 0.1


class Sched:

    ENGS = ("sync", "gpsimd", "tensor", "scalar", "vector")

    def __init__(self):
        self.ops = {e: [] for e in self.ENGS}
        self.cnt = {}

    def sems(self, names):
        for n in names:
            self.cnt[n] = 0

    def wait(self, eng, name, val):
        if val > 0:
            self.ops[eng].append(("w", name, val))

    def run(self, eng, fn, inc=None, k=1):
        self.ops[eng].append(("r", fn, inc, k))
        if inc is not None:
            self.cnt[inc] += k
            return self.cnt[inc]
        return None

    def emit(self, block, sems):
        def mk(eng_name):
            def body(e):
                for op in self.ops[eng_name]:
                    if op[0] == "w":
                        e.wait_ge(sems[op[1]], op[2])
                    else:
                        inst = op[1](e)
                        if op[2] is not None:
                            inst.then_inc(sems[op[2]], op[3])
            return body
        block.sync(mk("sync"))
        block.gpsimd(mk("gpsimd"))
        block.tensor(mk("tensor"))
        block.scalar(mk("scalar"))
        block.vector(mk("vector"))


SEMNAMES = ["LD", "GD", "TS", "SS", "VS", "ST", "PS"]
CONSTS = [0.0, 1.0, LN_EPS, RMS_EPS]


def build_l1():
    nc = bass.Bass("TRN2", target_bir_lowering=False)
    xtok = nc.dram_tensor("xtok", [TOK, D], F32, kind="ExternalInput").ap()
    xtokT = nc.dram_tensor("xtokT", [D, TOK], F32, kind="ExternalInput").ap()
    xpwT = nc.dram_tensor("xpwT", [D, R], F32, kind="ExternalInput").ap()
    wcatT = nc.dram_tensor("wcatT", [D, 24], F32, kind="ExternalInput").ap()
    ones_in = nc.dram_tensor("ones_in", [1, 128], F32, kind="ExternalInput").ap()

    o_xp = nc.dram_tensor("xpraw", [R, TOK], F32, kind="ExternalOutput").ap()
    o_lx = nc.dram_tensor("lx", [24, TOK], F32, kind="ExternalOutput").ap()
    o_m = nc.dram_tensor("mcol", [128, 8], F32, kind="ExternalOutput").ap()
    o_r = nc.dram_tensor("rcol", [128, 8], F32, kind="ExternalOutput").ap()
    o_mr = nc.dram_tensor("mrcol", [128, 8], F32, kind="ExternalOutput").ap()
    o_ws = nc.dram_tensor("wsums", [1, R + 24], F32, kind="ExternalOutput").ap()

    NTT = TOK // 128
    ND = D // 128

    with ExitStack() as ctx:
        def sb(name, shape, dt=F32):
            return ctx.enter_context(nc.sbuf_tensor(name, shape, dt))

        def ps(name, shape):
            return ctx.enter_context(nc.psum_tensor(name, shape, F32))

        _xk = sb("xk", [128, 2 * D])
        _xb = sb("xb", [128, 4 * TOK])
        _wpb = sb("wpb", [128, ND * R])
        _wcb = sb("wcb", [128, ND * 24])
        onec = sb("onec", [128, 1])
        scr = sb("scr", [128, D], BF16)
        sumc = sb("sumc", [128, 8])
        sqc = sb("sqc", [128, 8])
        meanc = sb("meanc", [128, 8])
        varc = sb("varc", [128, 8])
        sdc = sb("sdc", [128, 8])
        rstdc = sb("rstdc", [128, 8])
        t8 = sb("t8", [128, 8])
        mrc = sb("mrc", [128, 8])
        xpev = sb("xpev", [128, TOK])
        xpev2 = sb("xpev2", [32, TOK])
        lxev = sb("lxev", [24, TOK])
        wsev = sb("wsev", [1, R + 24])
        csts = sb("csts", [128, 4])
        p1 = ps("p1", [128, TOK])
        p2 = ps("p2", [32, TOK])
        plx = ps("plx", [24, TOK])
        pws = ps("pws", [1, R])
        pwf = ps("pwf", [1, 24])
        sems = {n: ctx.enter_context(nc.semaphore(n)) for n in SEMNAMES}
        block = ctx.enter_context(nc.Block())

        S = Sched()
        S.sems(SEMNAMES)
        for j, v in enumerate(CONSTS):
            nc.const_aps.register(F32, v, csts[:, j:j + 1])
        for j, v in enumerate(CONSTS):
            S.run("vector", lambda e, j=j, v=v: e.memset(csts[:, j:j + 1], v), "VS")
        vs_consts = S.cnt["VS"]
        S.wait("scalar", "VS", vs_consts)

        xk = [_xk[:, (i % 2) * D:((i % 2) + 1) * D] for i in range(NTT)]
        xb = [_xb[:, (i % 4) * TOK:((i % 4) + 1) * TOK] for i in range(ND)]
        wpb = [_wpb[:, i * R:(i + 1) * R] for i in range(ND)]
        wcb = [_wcb[:, i * 24:(i + 1) * 24] for i in range(ND)]

        for i in range(ND):
            S.run("gpsimd", lambda e, i=i: e.dma_start(
                out=wpb[i], in_=xpwT[i * 128:(i + 1) * 128, :]), "GD", 16)
        for i in range(ND):
            S.run("gpsimd", lambda e, i=i: e.dma_start(
                out=wcb[i], in_=wcatT[i * 128:(i + 1) * 128, :]), "GD", 16)
        S.run("gpsimd", lambda e: e.dma_start(
            out=onec[:, :],
            in_=ones_in.rearrange("o (p q) -> (o p) q", q=1)), "GD", 16)
        gd_all = S.cnt["GD"]

        ld_tile = []
        ss_tile = [0] * NTT
        for i in range(2):
            S.run("sync", lambda e, i=i: e.dma_start(
                out=xk[i], in_=xtok[i * 128:(i + 1) * 128, :]), "LD", 16)
            ld_tile.append(S.cnt["LD"])
        for i in range(NTT):
            S.wait("scalar", "LD", ld_tile[i])
            S.run("scalar", lambda e, i=i: e.activation(
                out=scr[:, :], in_=xk[i], func=AF.Copy, bias=0.0,
                accum_out=sumc[:, i:i + 1]), "SS")
            S.wait("scalar", "SS", S.cnt["SS"])
            S.run("scalar", lambda e, i=i: e.activation(
                out=scr[:, :], in_=xk[i], func=AF.Square,
                accum_out=sqc[:, i:i + 1]), "SS")
            ss_tile[i] = S.cnt["SS"]
            S.wait("scalar", "SS", ss_tile[i])
            if i + 2 < NTT:
                S.wait("sync", "SS", ss_tile[i])
                S.run("sync", lambda e, i=i: e.dma_start(
                    out=xk[i + 2], in_=xtok[(i + 2) * 128:(i + 3) * 128, :]),
                    "LD", 16)
                ld_tile.append(S.cnt["LD"])
        ss_stats = S.cnt["SS"]

        S.wait("vector", "SS", ss_stats)
        S.run("vector", lambda e: e.tensor_scalar(
            out=meanc[:, :], in0=sumc[:, :], scalar1=1.0 / D, scalar2=None,
            op0=ALU.mult), "VS")
        S.wait("vector", "VS", S.cnt["VS"])
        S.run("vector", lambda e: e.tensor_scalar(
            out=varc[:, :], in0=sqc[:, :], scalar1=1.0 / D, scalar2=None,
            op0=ALU.mult), "VS")
        S.wait("vector", "VS", S.cnt["VS"])
        S.run("vector", lambda e: e.tensor_tensor(
            out=t8[:, :], in0=meanc[:, :], in1=meanc[:, :], op=ALU.mult), "VS")
        S.wait("vector", "VS", S.cnt["VS"])
        S.run("vector", lambda e: e.tensor_tensor(
            out=varc[:, :], in0=varc[:, :], in1=t8[:, :], op=ALU.subtract), "VS")
        vs_var = S.cnt["VS"]
        S.wait("scalar", "VS", vs_var)
        S.run("scalar", lambda e: e.activation(
            out=sdc[:, :], in_=varc[:, :], func=AF.Sqrt, bias=LN_EPS), "SS")
        ss_sd = S.cnt["SS"]
        S.wait("vector", "SS", ss_sd)
        S.run("vector", lambda e: e.reciprocal(out=rstdc[:, :], in_=sdc[:, :]), "VS")
        S.wait("vector", "VS", S.cnt["VS"])
        S.run("vector", lambda e: e.tensor_tensor(
            out=mrc[:, :], in0=meanc[:, :], in1=rstdc[:, :], op=ALU.mult), "VS")
        vs_rstd = S.cnt["VS"]

        S.wait("tensor", "GD", gd_all)
        ldx = [0] * ND
        ts_used = [0] * ND
        for i in range(4):
            S.run("sync", lambda e, i=i: e.dma_start(
                out=xb[i], in_=xtokT[i * 128:(i + 1) * 128, :]), "LD", 16)
            ldx[i] = S.cnt["LD"]
        for i in range(ND):
            S.wait("tensor", "LD", ldx[i])
            for h in range(2):
                lo, hi = h * 512, (h + 1) * 512
                S.run("tensor", lambda e, i=i, lo=lo, hi=hi: e.matmul(
                    p1[:, lo:hi], wpb[i][:, 0:128], xb[i][:, lo:hi],
                    start=(i == 0), stop=(i == ND - 1)), None)
                S.run("tensor", lambda e, i=i, lo=lo, hi=hi: e.matmul(
                    p2[:, lo:hi], wpb[i][:, 128:160], xb[i][:, lo:hi],
                    start=(i == 0), stop=(i == ND - 1)), None)
                S.run("tensor", lambda e, i=i, lo=lo, hi=hi: e.matmul(
                    plx[:, lo:hi], wcb[i], xb[i][:, lo:hi],
                    start=(i == 0), stop=(i == ND - 1)), None)
            S.run("tensor", lambda e, i=i: e.matmul(
                pws[:, :], onec[:, :], wpb[i],
                start=(i == 0), stop=(i == ND - 1)), None)
            S.run("tensor", lambda e, i=i: e.matmul(
                pwf[:, :], onec[:, :], wcb[i],
                start=(i == 0), stop=(i == ND - 1)), "TS")
            ts_used[i] = S.cnt["TS"]
            if i + 4 < ND:
                S.wait("sync", "TS", ts_used[i])
                S.run("sync", lambda e, i=i: e.dma_start(
                    out=xb[i + 4], in_=xtokT[(i + 4) * 128:(i + 5) * 128, :]),
                    "LD", 16)
                ldx[i + 4] = S.cnt["LD"]
        ts_all = S.cnt["TS"]

        S.wait("scalar", "TS", ts_all)
        for fn in (
            lambda e: e.activation(out=xpev[:, :], in_=p1[:, :], func=AF.Copy, bias=0.0),
            lambda e: e.activation(out=xpev2[:, :], in_=p2[:, :], func=AF.Copy, bias=0.0),
            lambda e: e.activation(out=lxev[:, :], in_=plx[:, :], func=AF.Copy, bias=0.0),
            lambda e: e.activation(out=wsev[:, 0:R], in_=pws[:, :], func=AF.Copy, bias=0.0),
            lambda e: e.activation(out=wsev[:, R:R + 24], in_=pwf[:, :], func=AF.Copy, bias=0.0),
        ):
            S.run("scalar", fn, "SS")
        ss_evac = S.cnt["SS"]

        S.wait("sync", "SS", ss_evac)
        S.wait("sync", "VS", vs_rstd)
        S.run("sync", lambda e: e.dma_start(out=o_xp[0:128, :], in_=xpev[:, :]), "ST", 16)
        S.run("sync", lambda e: e.dma_start(out=o_xp[128:160, :], in_=xpev2[:, :]), "ST", 16)
        S.run("sync", lambda e: e.dma_start(out=o_lx, in_=lxev[:, :]), "ST", 16)
        S.run("sync", lambda e: e.dma_start(out=o_m, in_=meanc[:, :]), "ST", 16)
        S.run("sync", lambda e: e.dma_start(out=o_r, in_=rstdc[:, :]), "ST", 16)
        S.run("sync", lambda e: e.dma_start(out=o_mr, in_=mrc[:, :]), "ST", 16)
        S.run("sync", lambda e: e.dma_start(out=o_ws, in_=wsev[:, :]), "ST", 16)
        S.wait("sync", "ST", S.cnt["ST"])

        S.emit(block, sems)
    return nc


_L1 = None


def _get_l1():
    global _L1
    if _L1 is None:
        _L1 = build_l1()
    return _L1


def run_l1(x2d, x_proj_w, wcat):
    nc = _get_l1()
    xpwT = np.ascontiguousarray(x_proj_w.T)
    wcatT = np.ascontiguousarray(wcat.T)
    ones = np.ones((1, 128), np.float32)
    in_maps = []
    for c in range(NCORES):
        xt = np.ascontiguousarray(x2d[c * TOK:(c + 1) * TOK])
        in_maps.append(dict(
            xtok=xt, xtokT=np.ascontiguousarray(xt.T),
            xpwT=xpwT, wcatT=wcatT, ones_in=ones))
    res = run_bass_kernel_spmd(nc, in_maps, core_ids=list(range(NCORES)))
    return res.results




def build_l2():
    nc = bass.Bass("TRN2", target_bir_lowering=False)
    xdT = nc.dram_tensor("xdT", [DC, TT], F32, kind="ExternalInput").ap()
    xp = nc.dram_tensor("xp", [R, TT], F32, kind="ExternalInput").ap()
    st1 = nc.dram_tensor("st1", [3, TT], F32, kind="ExternalInput").ap()
    alog = nc.dram_tensor("alog", [128, 4 * N], F32, kind="ExternalInput").ap()
    dcol = nc.dram_tensor("dcol", [128, 4], F32, kind="ExternalInput").ap()
    dtwT = nc.dram_tensor("dtwT", [128, DC], F32, kind="ExternalInput").ap()
    dtbc = nc.dram_tensor("dtbc", [128, 4], F32, kind="ExternalInput").ap()
    wctT = nc.dram_tensor("wctT", [128, 4 * 24], F32, kind="ExternalInput").ap()
    wsc = nc.dram_tensor("wsc", [128, 1], F32, kind="ExternalInput").ap()
    wsb = nc.dram_tensor("wsb", [32, 1], F32, kind="ExternalInput").ap()

    o_y2 = nc.dram_tensor("y2", [DC, TT], F32, kind="ExternalOutput").ap()
    o_lnp = nc.dram_tensor("lnp", [2, TT], F32, kind="ExternalOutput").ap()
    o_ly = nc.dram_tensor("ly", [24, TT], F32, kind="ExternalOutput").ap()

    bcd = [nc.dram_tensor(f"bcd{i}", [32, CH], F32, kind="Internal").ap()
           for i in range(2)]

    with ExitStack() as ctx:
        def sb(name, shape, dt=F32):
            return ctx.enter_context(nc.sbuf_tensor(name, shape, dt))

        def ps(name, shape):
            return ctx.enter_context(nc.psum_tensor(name, shape, F32))

        _xd = sb("xd", [128, 2 * 4 * CH])
        _xln = sb("xln", [128, 2 * 4 * CH])
        _dt4 = sb("dt4", [128, 2 * 4 * CH])
        _w4 = sb("w4", [128, 2 * 4 * CH])
        _dti = sb("dti", [128, 2 * CH])
        _dta = sb("dta", [128, 2 * CH])
        _bc = sb("bc", [32, 2 * CH])
        _bca = sb("bca", [32, 2 * CH])
        _mrep = sb("mrep", [128, 2 * CH])
        _rrep = sb("rrep", [128, 2 * CH])
        _mrrep = sb("mrrep", [128, 2 * CH])
        _ball = sb("ball", [128, N * CH])
        _call = sb("call", [128, N * CH])
        _u1 = sb("u1", [128, 2 * N * CH])
        _u2 = sb("u2", [128, 2 * N * CH])
        _u3 = sb("u3", [128, 2 * N * CH])
        _y2t = sb("y2t", [128, 4 * CH])
        ysq = sb("ysq", [128, CH])
        sptmp = sb("sptmp", [128, CH])
        carry = sb("carry", [128, 4 * N])
        inj = sb("inj", [128, N])
        t256a = sb("t256a", [128, CH])
        t256b = sb("t256b", [32, CH])
        alog_s = sb("alog_s", [128, 4 * N])
        aneg = sb("aneg", [128, 4 * N])
        dcol_s = sb("dcol_s", [128, 4])
        dtw_s = sb("dtw_s", [128, DC])
        dtb_s = sb("dtb_s", [128, 4])
        wct_s = sb("wct_s", [128, 4 * 24])
        wsc_s = sb("wsc_s", [128, 1])
        wsb_s = sb("wsb_s", [32, 1])
        nwsc = sb("nwsc", [128, 1])
        nwsb = sb("nwsb", [32, 1])
        lyev = sb("lyev", [24, 2 * CH])
        lnev = sb("lnev", [1, 4 * CH])
        csts = sb("csts", [128, 4])
        pdtz = [ps(f"pdtz{i}", [128, CH]) for i in range(2)]
        pln1 = ps("pln1", [1, CH])
        pln2 = ps("pln2", [1, CH])
        ply = ps("ply", [24, CH])
        sems = {n: ctx.enter_context(nc.semaphore(n)) for n in SEMNAMES}
        block = ctx.enter_context(nc.Block())

        S = Sched()
        S.sems(SEMNAMES)
        for j, v in enumerate(CONSTS):
            nc.const_aps.register(F32, v, csts[:, j:j + 1])
        for j, v in enumerate(CONSTS):
            S.run("vector", lambda e, j=j, v=v: e.memset(csts[:, j:j + 1], v), "VS")
        vs_consts = S.cnt["VS"]
        S.wait("scalar", "VS", vs_consts)
        S.wait("tensor", "VS", vs_consts)

        xd = [[_xd[:, (p * 4 + t) * CH:(p * 4 + t + 1) * CH] for t in range(4)]
              for p in range(2)]
        xln = [[_xln[:, (p * 4 + t) * CH:(p * 4 + t + 1) * CH] for t in range(4)]
               for p in range(2)]
        dt4 = [[_dt4[:, (p * 4 + t) * CH:(p * 4 + t + 1) * CH] for t in range(4)]
               for p in range(2)]
        w4 = [[_w4[:, (p * 4 + t) * CH:(p * 4 + t + 1) * CH] for t in range(4)]
              for p in range(2)]
        dti = [_dti[:, p * CH:(p + 1) * CH] for p in range(2)]
        dta = [_dta[:, p * CH:(p + 1) * CH] for p in range(2)]
        bc = [_bc[:, p * CH:(p + 1) * CH] for p in range(2)]
        bca = [_bca[:, p * CH:(p + 1) * CH] for p in range(2)]
        mrep = [_mrep[:, p * CH:(p + 1) * CH] for p in range(2)]
        rrep = [_rrep[:, p * CH:(p + 1) * CH] for p in range(2)]
        mrrep = [_mrrep[:, p * CH:(p + 1) * CH] for p in range(2)]
        ball = [_ball[:, :], _ball[:, :]]
        call = [_call[:, :], _call[:, :]]
        u1 = [_u1[:, p * N * CH:(p + 1) * N * CH] for p in range(2)]
        u2b = [_u2[:, p * N * CH:(p + 1) * N * CH] for p in range(2)]
        u3b = [_u3[:, p * N * CH:(p + 1) * N * CH] for p in range(2)]
        y2t = [_y2t[:, t * CH:(t + 1) * CH] for t in range(4)]
        lyv = [lyev[:, p * CH:(p + 1) * CH] for p in range(2)]
        lnv = [[lnev[:, (p * 2 + j) * CH:(p * 2 + j + 1) * CH] for j in range(2)]
               for p in range(2)]

        for fn in (
            lambda e: e.dma_start(out=alog_s[:, :], in_=alog),
            lambda e: e.dma_start(out=dcol_s[:, :], in_=dcol),
            lambda e: e.dma_start(out=dtw_s[:, :], in_=dtwT),
            lambda e: e.dma_start(out=dtb_s[:, :], in_=dtbc),
            lambda e: e.dma_start(out=wct_s[:, :], in_=wctT),
            lambda e: e.dma_start(out=wsc_s[:, :], in_=wsc),
            lambda e: e.dma_start(out=wsb_s[:, :], in_=wsb),
        ):
            S.run("sync", fn, "LD", 16)
        ld_once = S.cnt["LD"]
        S.wait("scalar", "LD", ld_once)
        S.run("scalar", lambda e: e.activation(
            out=aneg[:, :], in_=alog_s[:, :], func=AF.Exp), "SS")
        ss_aexp = S.cnt["SS"]
        S.wait("vector", "SS", ss_aexp)
        S.run("vector", lambda e: e.tensor_scalar(
            out=aneg[:, :], in0=aneg[:, :], scalar1=-1.0, scalar2=None,
            op0=ALU.mult), "VS")
        S.wait("vector", "LD", ld_once)
        S.run("vector", lambda e: e.tensor_scalar(
            out=nwsc[:, :], in0=wsc_s[:, :], scalar1=-1.0, scalar2=None,
            op0=ALU.mult), "VS")
        S.run("vector", lambda e: e.tensor_scalar(
            out=nwsb[:, :], in0=wsb_s[:, :], scalar1=-1.0, scalar2=None,
            op0=ALU.mult), "VS")
        vs_pre = S.cnt["VS"]
        S.wait("scalar", "VS", vs_pre)
        S.wait("tensor", "LD", ld_once)

        cons_vs = [0, 0]
        cons_ball = [0, 0]
        cons_u1 = [0, 0]
        ss_dtz_sp = [0, 0]
        ts_sq_use = 0
        vs_evac_prev = 0
        st_y2 = [0] * 4
        st_misc = 0
        pool_self = 0

        ld_chunk_p = [0, 0]
        ball_ps_gate = 0
        ball_vs_gate = 0

        def emit_loads(ci2):
            par2 = ci2 % 2
            c02 = ci2 * CH
            cols2 = slice(c02, c02 + CH)
            S.wait("sync", "VS", cons_vs[par2])
            for t in range(4):
                S.run("sync", lambda e, t=t, par2=par2, cols2=cols2: e.dma_start(
                    out=xd[par2][t], in_=xdT[t * 128:(t + 1) * 128, cols2]), "LD", 16)
            S.run("sync", lambda e, par2=par2, cols2=cols2: e.dma_start(
                out=dti[par2], in_=xp[0:128, cols2]), "LD", 16)
            S.run("sync", lambda e, par2=par2, cols2=cols2: e.dma_start(
                out=bc[par2], in_=xp[128:160, cols2]), "LD", 16)
            S.run("sync", lambda e, par2=par2, cols2=cols2: e.dma_start(
                out=mrep[par2][:, :], in_=st1[0:1, cols2].broadcast_to([128, CH])),
                "LD", 16)
            S.run("sync", lambda e, par2=par2, cols2=cols2: e.dma_start(
                out=rrep[par2][:, :], in_=st1[1:2, cols2].broadcast_to([128, CH])),
                "LD", 16)
            S.run("sync", lambda e, par2=par2, cols2=cols2: e.dma_start(
                out=mrrep[par2][:, :], in_=st1[2:3, cols2].broadcast_to([128, CH])),
                "LD", 16)
            ld_chunk_p[par2] = S.cnt["LD"]

        vs_w4_p = [[0] * 4, [0] * 4]
        vs_xln_p = [[0] * 4, [0] * 4]
        ld_bcast_p = [0, 0]
        cons_dta = [0, 0]
        cons_bca = [0, 0]

        def emit_prep(ci2):
            par2 = ci2 % 2
            S.wait("vector", "LD", ld_chunk_p[par2])
            S.wait("vector", "TS", cons_dta[par2])
            S.run("vector", lambda e, par2=par2: e.tensor_tensor(
                out=t256a[:, :], in0=dti[par2], in1=rrep[par2], op=ALU.mult), "VS")
            S.wait("vector", "VS", S.cnt["VS"])
            S.run("vector", lambda e, par2=par2: e.scalar_tensor_tensor(
                out=dta[par2], in0=mrrep[par2], scalar=nwsc[:, 0:1], in1=t256a[:, :],
                op0=ALU.mult, op1=ALU.add), "VS")
            vs_dta2 = S.cnt["VS"]
            S.wait("vector", "LD", cons_bca[par2])
            S.run("vector", lambda e, par2=par2: e.tensor_tensor(
                out=t256b[:, :], in0=bc[par2], in1=rrep[par2][0:32, :], op=ALU.mult), "VS")
            S.wait("vector", "VS", S.cnt["VS"])
            S.run("vector", lambda e, par2=par2: e.scalar_tensor_tensor(
                out=bca[par2], in0=mrrep[par2][0:32, :], scalar=nwsb[:, 0:1],
                in1=t256b[:, :], op0=ALU.mult, op1=ALU.add), "VS")
            vs_bca2 = S.cnt["VS"]
            for t in range(4):
                S.run("vector", lambda e, t=t, par2=par2: e.tensor_tensor(
                    out=xln[par2][t], in0=xd[par2][t], in1=rrep[par2], op=ALU.mult), "VS")
                S.wait("vector", "VS", S.cnt["VS"])
                S.run("vector", lambda e, t=t, par2=par2: e.tensor_tensor(
                    out=xln[par2][t], in0=xln[par2][t], in1=mrrep[par2],
                    op=ALU.subtract), "VS")
                vs_xln_p[par2][t] = S.cnt["VS"]
            for t in range(4):
                pz = pdtz[t % 2]
                S.wait("tensor", "SS", ss_dtz_sp[t % 2])
                S.wait("tensor", "VS", vs_dta2)
                S.run("tensor", lambda e, t=t, pz=pz, par2=par2: e.matmul(
                    pz[:, :], dtw_s[:, t * 128:(t + 1) * 128], dta[par2],
                    start=True, stop=True), "TS")
                ts_dtz = S.cnt["TS"]
                if t == 3:
                    cons_dta[par2] = ts_dtz
                S.wait("scalar", "TS", ts_dtz)
                S.run("scalar", lambda e, t=t, pz=pz: e.activation(
                    out=sptmp[:, :], in_=pz[:, :], func=AF.Exp,
                    bias=dtb_s[:, t:t + 1]), "SS")
                S.wait("scalar", "SS", S.cnt["SS"])
                S.run("scalar", lambda e, t=t, par2=par2: e.activation(
                    out=dt4[par2][t], in_=sptmp[:, :], func=AF.Ln, bias=1.0), "SS")
                ss_dtz_sp[t % 2] = S.cnt["SS"]
                ss_sp = S.cnt["SS"]
                S.wait("vector", "SS", ss_sp)
                S.wait("vector", "VS", vs_xln_p[par2][t])
                S.run("vector", lambda e, t=t, par2=par2: e.tensor_tensor(
                    out=w4[par2][t], in0=dt4[par2][t], in1=xln[par2][t],
                    op=ALU.mult), "VS")
                vs_w4_p[par2][t] = S.cnt["VS"]
                S.wait("vector", "VS", vs_w4_p[par2][t])
            S.wait("sync", "VS", vs_bca2)
            S.run("sync", lambda e, par2=par2: e.dma_start(
                out=bcd[par2], in_=bca[par2]), "LD", 16)
            cons_bca[par2] = S.cnt["LD"]
            S.wait("sync", "LD", S.cnt["LD"])
            S.wait("sync", "PS", ball_ps_gate)
            S.wait("sync", "VS", ball_vs_gate)
            S.run("sync", lambda e, par2=par2: e.dma_start(
                out=ball[par2].rearrange("p (n t) -> p n t", n=N),
                in_=bcd[par2][0:N, :].rearrange("n (o t) -> o n t", o=1)
                    .broadcast_to([128, N, CH])), "LD", 16)
            S.run("sync", lambda e, par2=par2: e.dma_start(
                out=call[par2].rearrange("p (n t) -> p n t", n=N),
                in_=bcd[par2][N:2 * N, :].rearrange("n (o t) -> o n t", o=1)
                    .broadcast_to([128, N, CH])), "LD", 16)
            ld_bcast_p[par2] = S.cnt["LD"]

        emit_loads(0)
        emit_prep(0)
        for ci in range(NCH):
            par = ci % 2
            cb = ci % CHB
            c0 = ci * CH
            cols = slice(c0, c0 + CH)
            ld_bcast = ld_bcast_p[par]

            vs_tts = [0] * 4
            vs_red = [0] * 4
            z_done = [0] * 4
            carry_done = [0] * 4
            ss_dA = [0] * 4
            for t in range(4):
                u2 = u2b[t % 2]
                u3 = u3b[t % 2]
                up = u1[t % 2]
                vs_w = vs_w4_p[par][t]
                S.wait("scalar", "VS", cons_u1[t % 2])
                for n in range(N):
                    S.run("scalar", lambda e, t=t, n=n, up=up, par=par: e.activation(
                        out=up[:, n * CH:(n + 1) * CH], in_=dt4[par][t], func=AF.Exp,
                        scale=aneg[:, t * N + n:t * N + n + 1]), "SS")
                ss_dA[t] = S.cnt["SS"]

                uv = up.rearrange("p (n t) -> p n t", n=N)
                u2v = u2.rearrange("p (n t) -> p n t", n=N)
                S.wait("vector", "VS", vs_red[t - 2] if t >= 2 else
                       (vs_red_last2[t] if ci > 0 else 0))
                S.wait("vector", "LD", ld_bcast)
                S.wait("vector", "VS", vs_w)
                S.run("vector", lambda e, t=t, par=par, u2v=u2v: e.tensor_tensor(
                    out=u2v,
                    in0=w4[par][t].rearrange("p (o t) -> p o t", o=1)
                        .broadcast_to([128, N, CH]),
                    in1=ball[par].rearrange("p (n t) -> p n t", n=N),
                    op=ALU.mult), "VS")
                S.wait("vector", "VS", S.cnt["VS"])
                S.wait("vector", "SS", ss_dA[t])
                if cb > 0:
                    S.run("vector", lambda e, t=t, uv=uv: e.tensor_tensor(
                        out=inj[:, :], in0=uv[:, :, 0],
                        in1=carry[:, t * N:(t + 1) * N], op=ALU.mult), "VS")
                    S.wait("vector", "VS", S.cnt["VS"])
                S.run("vector", lambda e, uv=uv: e.memset(uv[:, :, 0:1], 0.0), "VS")
                S.wait("vector", "VS", S.cnt["VS"])
                if cb > 0:
                    S.run("vector", lambda e, u2v=u2v: e.tensor_tensor(
                        out=u2v[:, :, 0], in0=u2v[:, :, 0],
                        in1=inj[:, :], op=ALU.add), "VS")
                    S.wait("vector", "VS", S.cnt["VS"])
                ps_dbx = S.cnt["PS"]
                if t >= 1:
                    tc = t - 1
                    S.wait("gpsimd", "VS", vs_tts[tc])
                    S.run("gpsimd", lambda e, tc=tc: e.tensor_copy(
                        out=carry[:, tc * N:(tc + 1) * N],
                        in_=u3b[tc % 2].rearrange("p (n t) -> p n t", n=N)[:, :, CH - 1]),
                        "PS")
                    carry_done[tc] = S.cnt["PS"]
                    S.wait("gpsimd", "PS", S.cnt["PS"])

                S.wait("vector", "SS", ss_dA[t])
                S.wait("vector", "PS", ps_dbx)
                if t >= 2:
                    S.wait("vector", "VS", z_done[t - 2])
                    S.wait("vector", "PS", carry_done[t - 2])
                elif ci > 0:
                    S.wait("vector", "VS", z_last2[t])
                    S.wait("vector", "PS", carry_last2[t])
                S.run("vector", lambda e, up=up, u2=u2, u3=u3: e.tensor_tensor_scan(
                    out=u3[:, :], data0=up[:, :], data1=u2[:, :], initial=0.0,
                    op0=ALU.mult, op1=ALU.add), "VS")
                vs_tts[t] = S.cnt["VS"]
                cons_u1[t % 2] = vs_tts[t]
                S.wait("vector", "VS", vs_tts[t])
                S.run("vector", lambda e, t=t, par=par, u2=u2, u3=u3:
                      e.tensor_tensor(
                          out=u2.rearrange("p (c n) -> p n c", n=N),
                          in0=u3.rearrange("p (n c) -> p n c", n=N),
                          in1=call[par].rearrange("p (n c) -> p n c", n=N),
                          op=ALU.mult), "VS")
                z_done[t] = S.cnt["VS"]
                S.wait("vector", "VS", z_done[t])
                S.wait("vector", "LD", st_y2[t])
                S.run("vector", lambda e, t=t: e.tensor_reduce(
                    out=y2t[t], in_=u2b[t % 2].rearrange("p (c n) -> p c n", n=N),
                    axis=mybir.AxisListType.X, op=ALU.add), "VS")
                S.wait("vector", "VS", S.cnt["VS"])
                S.run("vector", lambda e, t=t, par=par: e.scalar_tensor_tensor(
                    out=y2t[t], in0=xln[par][t], scalar=dcol_s[:, t:t + 1],
                    in1=y2t[t], op0=ALU.mult, op1=ALU.add), "VS")
                vs_red[t] = S.cnt["VS"]
                S.wait("vector", "VS", vs_red[t])
                if t == 1 and ci + 1 < NCH:
                    emit_loads(ci + 1)
            S.wait("gpsimd", "VS", vs_tts[3])
            S.run("gpsimd", lambda e: e.tensor_copy(
                out=carry[:, 3 * N:4 * N],
                in_=u3b[1].rearrange("p (n t) -> p n t", n=N)[:, :, CH - 1]), "PS")
            carry_done[3] = S.cnt["PS"]
            S.wait("gpsimd", "PS", S.cnt["PS"])
            vs_red_last2 = [vs_red[2], vs_red[3], vs_red[2], vs_red[3]]
            z_last2 = [z_done[2], z_done[3], z_done[2], z_done[3]]
            carry_last2 = [carry_done[2], carry_done[3], carry_done[2], carry_done[3]]
            cons_vs[par] = S.cnt["VS"]
            ball_ps_gate = S.cnt["PS"]
            ball_vs_gate = S.cnt["VS"]
            if ci + 1 < NCH:
                emit_prep(ci + 1)

            S.wait("tensor", "VS", vs_evac_prev)
            for t in range(4):
                S.wait("scalar", "VS", vs_red[t])
                S.wait("scalar", "TS", ts_sq_use)
                S.run("scalar", lambda e, t=t: e.activation(
                    out=ysq[:, :], in_=y2t[t], func=AF.Square), "SS")
                ss_sq = S.cnt["SS"]
                S.wait("tensor", "VS", vs_red[t])
                S.run("tensor", lambda e, t=t: e.matmul(
                    pln1[:, :], csts[:, 1:2], y2t[t],
                    start=(t == 0), stop=(t == 3)), "TS" if t == 3 else None)
                S.wait("tensor", "SS", ss_sq)
                S.run("tensor", lambda e, t=t: e.matmul(
                    pln2[:, :], csts[:, 1:2], ysq[:, :],
                    start=(t == 0), stop=(t == 3)), "TS")
                ts_sq_use = S.cnt["TS"]
                S.run("tensor", lambda e, t=t: e.matmul(
                    ply[:, :], wct_s[:, t * 24:(t + 1) * 24], y2t[t],
                    start=(t == 0), stop=(t == 3)), "TS" if t == 3 else None)
            ts_stats = S.cnt["TS"]
            S.wait("vector", "TS", ts_stats)
            S.wait("vector", "LD", st_misc)
            S.run("vector", lambda e, par=par: e.tensor_copy(
                out=lnv[par][0], in_=pln1[:, :]), "VS")
            S.run("vector", lambda e, par=par: e.tensor_copy(
                out=lnv[par][1], in_=pln2[:, :]), "VS")
            S.run("vector", lambda e, par=par: e.tensor_copy(
                out=lyv[par], in_=ply[:, :]), "VS")
            vs_evac = S.cnt["VS"]
            vs_evac_prev = vs_evac
            for t in range(4):
                S.wait("sync", "VS", vs_red[t])
                S.run("sync", lambda e, t=t, cols=cols: e.dma_start(
                    out=o_y2[t * 128:(t + 1) * 128, cols], in_=y2t[t]), "LD", 16)
                st_y2[t] = S.cnt["LD"]
            S.wait("sync", "VS", vs_evac)
            S.run("sync", lambda e, par=par, cols=cols: e.dma_start(
                out=o_lnp[0:1, cols], in_=lnv[par][0]), "LD", 16)
            S.run("sync", lambda e, par=par, cols=cols: e.dma_start(
                out=o_lnp[1:2, cols], in_=lnv[par][1]), "LD", 16)
            S.run("sync", lambda e, par=par, cols=cols: e.dma_start(
                out=o_ly[:, cols], in_=lyv[par]), "LD", 16)
            st_misc = S.cnt["LD"]

        S.wait("sync", "LD", S.cnt["LD"])
        S.wait("sync", "GD", S.cnt["GD"])

        S.emit(block, sems)
    return nc


_L2 = None


def _get_l2():
    global _L2
    if _L2 is None:
        _L2 = build_l2()
    return _L2


def run_l2(x2d, xp_full, st1, A_log, Dv, dt_proj_w, dt_proj_b, wcat, wsum):
    nc = _get_l2()
    in_maps = []
    for c in range(NCORES):
        sl = slice(c * DC, (c + 1) * DC)
        in_maps.append(dict(
            xdT=np.ascontiguousarray(x2d[:, sl].T),
            xp=xp_full, st1=st1,
            alog=np.ascontiguousarray(
                A_log[sl].reshape(4, 128, N).transpose(1, 0, 2).reshape(128, 4 * N)),
            dcol=np.ascontiguousarray(Dv[sl].reshape(4, 128).T),
            dtwT=np.ascontiguousarray(dt_proj_w[sl].T),
            dtbc=np.ascontiguousarray(dt_proj_b[sl].reshape(4, 128).T),
            wctT=np.ascontiguousarray(
                wcat[:, sl].T.reshape(4, 128, 24).transpose(1, 0, 2).reshape(128, 96)),
            wsc=np.ascontiguousarray(wsum[0:128].reshape(128, 1)),
            wsb=np.ascontiguousarray(wsum[128:160].reshape(32, 1)),
        ))
    res = run_bass_kernel_spmd(nc, in_maps, core_ids=list(range(NCORES)))
    return res.results


def build_l3():
    nc = bass.Bass("TRN2", target_bir_lowering=False)
    xtok = nc.dram_tensor("xtok", [TOK, D], F32, kind="ExternalInput").ap()
    y2tok = nc.dram_tensor("y2tok", [TOK, D], F32, kind="ExternalInput").ap()
    lnps = nc.dram_tensor("lnps", [128, 16 * 8], F32, kind="ExternalInput").ap()
    lyt = nc.dram_tensor("lyt", [128, 192 * 8], F32, kind="ExternalInput").ap()
    lxt = nc.dram_tensor("lxt", [128, 192], F32, kind="ExternalInput").ap()
    wsumf = nc.dram_tensor("wsumf", [1, 24], F32, kind="ExternalInput").ap()
    hrb = nc.dram_tensor("hrb", [1, 16], F32, kind="ExternalInput").ap()
    hpb = nc.dram_tensor("hpb", [1, 4], F32, kind="ExternalInput").ap()
    hob = nc.dram_tensor("hob", [1, 4], F32, kind="ExternalInput").ap()
    gatei = nc.dram_tensor("gatei", [1, 1], F32, kind="ExternalInput").ap()
    eye16 = nc.dram_tensor("eye16", [1, 16], F32, kind="ExternalInput").ap()
    ident = nc.dram_tensor("ident", [128, 128], F32, kind="ExternalInput").ap()
    wb = nc.dram_tensor("wb", [1024, 1024], F32, kind="ExternalInput").ap()

    o_x3 = nc.dram_tensor("x3", [TOK, D], F32, kind="ExternalOutput").ap()
    o_dbg = nc.dram_tensor("dbg", [128, 192 + 128 + 32 + 32 + 48], F32,
                           kind="ExternalOutput").ap()

    NTT = TOK // 128

    with ExitStack() as ctx:
        def sb(name, shape, dt=F32):
            return ctx.enter_context(nc.sbuf_tensor(name, shape, dt))

        _xt = sb("xt", [128, 2 * D])
        _yt = sb("yt", [128, 2 * D])
        x2t = sb("x2t", [128, D])
        xint = sb("xint", [128, D])
        xmix = sb("xmix", [128, D])
        xpre32 = sb("xpre32", [128, 1024])
        xpreb = sb("xpreb", [128, 1024], BF16)
        xpT = sb("xpT", [128, 1024], BF16)
        yb = sb("yb", [128, 1024])
        _wbb = sb("wbb", [128, 8 * 1024], BF16)
        idn = sb("idn", [128, 128], BF16)
        lnps_s = sb("lnps_s", [128, 16 * 8])
        sc = sb("sc", [128, 16])
        lyt_s = sb("lyt_s", [128, 192 * 8])
        lyc = sb("lyc", [128, 192])
        lxt_s = sb("lxt_s", [128, 192])
        lgt = sb("lgt", [128, 192])
        zt = sb("zt", [128, 128])
        at = sb("at", [128, 128])
        pe_ = sb("pe_", [128, 128])
        ht = sb("ht", [128, 128])
        spre = sb("spre", [128, 32])
        spost = sb("spost", [128, 32])
        hpre = sb("hpre", [128, 32])
        hpost = sb("hpost", [128, 32])
        ev = sb("ev", [128, 32])
        eu = sb("eu", [128, 32])
        rs = sb("rs", [128, 32])
        zm = sb("zm", [128, 8])
        mu2c = sb("mu2c", [128, 8])
        r2c = sb("r2c", [128, 8])
        nmu2r2c = sb("nmu2r2c", [128, 8])
        spc = sb("spc", [128, 8])
        sqc = sb("sqc", [128, 8])
        mupc = sb("mupc", [128, 8])
        nmupc = sb("nmupc", [128, 8])
        varpc = sb("varpc", [128, 8])
        rpc = sb("rpc", [128, 8])
        alphac = sb("alphac", [128, 8])
        alphatc = sb("alphatc", [128, 8])
        t8a = sb("t8a", [128, 8])
        t8b = sb("t8b", [128, 8])
        s3c = sb("s3c", [128, 1])
        q3c = sb("q3c", [128, 1])
        mu3c = sb("mu3c", [128, 1])
        r3c = sb("r3c", [128, 1])
        nmu3r3c = sb("nmu3r3c", [128, 1])
        t1c = sb("t1c", [128, 1])
        scrb = sb("scrb", [128, D], BF16)
        wsfrep = sb("wsfrep", [128, 24])
        hrbrep = sb("hrbrep", [128, 16])
        hpbrep = sb("hpbrep", [128, 4])
        hobrep = sb("hobrep", [128, 4])
        eyerep = sb("eyerep", [128, 16])
        igrep = sb("igrep", [128, 16])
        gcol = sb("gcol", [128, 1])
        omgc = sb("omgc", [128, 1])
        csts = sb("csts", [128, 4])
        ptb = ctx.enter_context(nc.psum_tensor("ptb", [128, 128], BF16))
        py = [ctx.enter_context(nc.psum_tensor(f"py{i}", [128, 512], F32))
              for i in range(2)]
        sems = {n: ctx.enter_context(nc.semaphore(n)) for n in SEMNAMES}
        block = ctx.enter_context(nc.Block())

        S = Sched()
        S.sems(SEMNAMES)
        for j, v in enumerate(CONSTS):
            nc.const_aps.register(F32, v, csts[:, j:j + 1])
        for j, v in enumerate(CONSTS):
            S.run("vector", lambda e, j=j, v=v: e.memset(csts[:, j:j + 1], v), "VS")
        S.wait("scalar", "VS", S.cnt["VS"])

        xt = [_xt[:, (i % 2) * D:((i % 2) + 1) * D] for i in range(NTT)]
        yt = [_yt[:, (i % 2) * D:((i % 2) + 1) * D] for i in range(NTT)]
        wbb = [_wbb[:, i * 1024:(i + 1) * 1024] for i in range(8)]

        for fn in (
            lambda e: e.dma_start(out=lnps_s[:, :], in_=lnps),
            lambda e: e.dma_start(out=lyt_s[:, :], in_=lyt),
            lambda e: e.dma_start(out=lxt_s[:, :], in_=lxt),
        ):
            S.run("sync", fn, "LD", 16)
        for fn in (
            lambda e: e.dma_start(out=wsfrep[:, :], in_=wsumf.broadcast_to([128, 24])),
            lambda e: e.dma_start(out=hrbrep[:, :], in_=hrb.broadcast_to([128, 16])),
            lambda e: e.dma_start(out=hpbrep[:, :], in_=hpb.broadcast_to([128, 4])),
            lambda e: e.dma_start(out=hobrep[:, :], in_=hob.broadcast_to([128, 4])),
            lambda e: e.dma_start(out=eyerep[:, :], in_=eye16.broadcast_to([128, 16])),
            lambda e: e.dma_start(out=gcol[:, :], in_=gatei.broadcast_to([128, 1])),
            lambda e: e.dma_start(out=idn[:, :], in_=ident),
        ):
            S.run("gpsimd", fn, "GD", 16)
        for i in range(8):
            S.run("gpsimd", lambda e, i=i: e.dma_start(
                out=wbb[i], in_=wb[i * 128:(i + 1) * 128, :]), "GD", 16)
        ld_once = S.cnt["LD"]
        gd_once = S.cnt["GD"]
        S.wait("vector", "GD", gd_once)
        S.run("vector", lambda e: e.tensor_scalar(
            out=hrbrep[:, :], in0=hrbrep[:, :], scalar1=1.0 / TAU, scalar2=None,
            op0=ALU.mult), "VS")
        vs_hrb = S.cnt["VS"]

        S.wait("vector", "LD", ld_once)
        S.run("vector", lambda e: e.tensor_reduce(
            out=sc[:, :], in_=lnps_s.rearrange("p (r c) -> p r c", c=8),
            axis=mybir.AxisListType.X, op=ALU.add), "VS")
        S.run("vector", lambda e: e.tensor_reduce(
            out=lyc[:, :], in_=lyt_s.rearrange("p (r c) -> p r c", c=8),
            axis=mybir.AxisListType.X, op=ALU.add), "VS")
        S.wait("vector", "VS", S.cnt["VS"])
        sview = sc.rearrange("p (t r) -> p t r", r=2)
        S.run("vector", lambda e: e.tensor_scalar(
            out=mu2c[:, :], in0=sview[:, :, 0], scalar1=1.0 / D, scalar2=None,
            op0=ALU.mult), "VS")
        S.run("vector", lambda e: e.tensor_scalar(
            out=t8a[:, :], in0=sview[:, :, 1], scalar1=1.0 / D, scalar2=None,
            op0=ALU.mult), "VS")
        S.wait("vector", "VS", S.cnt["VS"])
        S.run("vector", lambda e: e.tensor_tensor(
            out=t8b[:, :], in0=mu2c[:, :], in1=mu2c[:, :], op=ALU.mult), "VS")
        S.wait("vector", "VS", S.cnt["VS"])
        S.run("vector", lambda e: e.tensor_tensor(
            out=t8a[:, :], in0=t8a[:, :], in1=t8b[:, :], op=ALU.subtract), "VS")
        vs_var2 = S.cnt["VS"]
        S.wait("vector", "VS", vs_var2)
        S.wait("scalar", "VS", vs_var2)
        S.run("scalar", lambda e: e.activation(
            out=t8b[:, :], in_=t8a[:, :], func=AF.Sqrt, bias=LN_EPS), "SS")
        ss_sd2 = S.cnt["SS"]
        S.wait("vector", "SS", ss_sd2)
        S.run("vector", lambda e: e.reciprocal(out=r2c[:, :], in_=t8b[:, :]), "VS")
        S.wait("vector", "VS", S.cnt["VS"])
        S.run("vector", lambda e: e.tensor_tensor(
            out=nmu2r2c[:, :], in0=mu2c[:, :], in1=r2c[:, :], op=ALU.mult), "VS")
        S.wait("vector", "VS", S.cnt["VS"])
        S.run("vector", lambda e: e.tensor_scalar(
            out=nmu2r2c[:, :], in0=nmu2r2c[:, :], scalar1=-1.0, scalar2=None,
            op0=ALU.mult), "VS")
        vs_ln2 = S.cnt["VS"]
        S.wait("vector", "VS", vs_ln2)

        ldA = [0] * NTT
        ssA = [0] * NTT
        vsA = [0] * NTT

        def loadA(tl):
            S.run("sync", lambda e, tl=tl: e.dma_start(
                out=xt[tl], in_=xtok[tl * 128:(tl + 1) * 128, :]), "LD", 16)
            S.run("sync", lambda e, tl=tl: e.dma_start(
                out=yt[tl], in_=y2tok[tl * 128:(tl + 1) * 128, :]), "LD", 16)
            ldA[tl] = S.cnt["LD"]

        loadA(0)
        loadA(1)
        for tl in range(NTT):
            S.wait("vector", "LD", ldA[tl])
            if tl >= 1:
                S.wait("vector", "SS", ssA[tl - 1])
            S.run("vector", lambda e, tl=tl: e.tensor_scalar(
                out=x2t[:, :], in0=yt[tl], scalar1=r2c[:, tl:tl + 1],
                scalar2=nmu2r2c[:, tl:tl + 1], op0=ALU.mult, op1=ALU.add), "VS")
            S.wait("vector", "VS", S.cnt["VS"])
            S.run("vector", lambda e, tl=tl: e.tensor_tensor(
                out=x2t[:, :], in0=x2t[:, :], in1=xt[tl], op=ALU.add), "VS")
            vsA[tl] = S.cnt["VS"]
            S.wait("scalar", "VS", vsA[tl])
            S.run("scalar", lambda e, tl=tl: e.activation(
                out=scrb[:, :], in_=x2t[:, :], func=AF.Copy, bias=0.0,
                accum_out=spc[:, tl:tl + 1]), "SS")
            S.wait("scalar", "SS", S.cnt["SS"])
            S.run("scalar", lambda e, tl=tl: e.activation(
                out=scrb[:, :], in_=x2t[:, :], func=AF.Square,
                accum_out=sqc[:, tl:tl + 1]), "SS")
            ssA[tl] = S.cnt["SS"]
            S.wait("scalar", "SS", ssA[tl])
            if tl + 2 < NTT:
                S.wait("sync", "VS", vsA[tl])
                loadA(tl + 2)
        ss_passa = S.cnt["SS"]

        S.wait("vector", "SS", ss_passa)
        S.run("vector", lambda e: e.tensor_scalar(
            out=mupc[:, :], in0=spc[:, :], scalar1=1.0 / D, scalar2=None,
            op0=ALU.mult), "VS")
        S.run("vector", lambda e: e.tensor_scalar(
            out=varpc[:, :], in0=sqc[:, :], scalar1=1.0 / D, scalar2=None,
            op0=ALU.mult), "VS")
        S.wait("vector", "VS", S.cnt["VS"])
        S.run("vector", lambda e: e.tensor_scalar(
            out=nmupc[:, :], in0=mupc[:, :], scalar1=-1.0, scalar2=None,
            op0=ALU.mult), "VS")
        S.run("vector", lambda e: e.tensor_tensor(
            out=t8a[:, :], in0=mupc[:, :], in1=mupc[:, :], op=ALU.mult), "VS")
        S.wait("vector", "VS", S.cnt["VS"])
        S.run("vector", lambda e: e.tensor_tensor(
            out=varpc[:, :], in0=varpc[:, :], in1=t8a[:, :], op=ALU.subtract), "VS")
        S.wait("vector", "VS", S.cnt["VS"])
        S.run("vector", lambda e: e.tensor_scalar(
            out=t8a[:, :], in0=varpc[:, :], scalar1=LN_EPS, scalar2=None,
            op0=ALU.add), "VS")
        vs_q = S.cnt["VS"]
        S.wait("scalar", "VS", vs_q)
        S.run("scalar", lambda e: e.activation(
            out=t8b[:, :], in_=t8a[:, :], func=AF.Sqrt), "SS")
        S.wait("vector", "SS", S.cnt["SS"])
        S.run("vector", lambda e: e.reciprocal(out=rpc[:, :], in_=t8b[:, :]), "VS")
        S.wait("vector", "VS", S.cnt["VS"])
        S.run("vector", lambda e: e.reciprocal(out=t8b[:, :], in_=t8a[:, :]), "VS")
        S.wait("vector", "VS", S.cnt["VS"])
        S.run("vector", lambda e: e.tensor_tensor(
            out=t8b[:, :], in0=varpc[:, :], in1=t8b[:, :], op=ALU.mult), "VS")
        S.wait("vector", "VS", S.cnt["VS"])
        S.run("vector", lambda e: e.tensor_scalar(
            out=t8b[:, :], in0=t8b[:, :], scalar1=RMS_EPS, scalar2=None,
            op0=ALU.add), "VS")
        vs_fr = S.cnt["VS"]
        S.wait("scalar", "VS", vs_fr)
        S.run("scalar", lambda e: e.activation(
            out=t8b[:, :], in_=t8b[:, :], func=AF.Sqrt), "SS")
        S.wait("vector", "SS", S.cnt["SS"])
        S.run("vector", lambda e: e.reciprocal(out=t8a[:, :], in_=t8b[:, :]), "VS")
        S.wait("vector", "VS", S.cnt["VS"])
        S.run("vector", lambda e: e.tensor_tensor(
            out=alphac[:, :], in0=t8a[:, :], in1=rpc[:, :], op=ALU.mult), "VS")
        S.wait("vector", "VS", S.cnt["VS"])
        S.run("vector", lambda e: e.tensor_scalar(
            out=alphatc[:, :], in0=alphac[:, :], scalar1=1.0 / TAU, scalar2=None,
            op0=ALU.mult), "VS")
        vs_alpha = S.cnt["VS"]
        S.wait("vector", "VS", vs_alpha)
        S.wait("vector", "VS", vs_hrb)

        for tl in range(NTT):
            lysl = lyc[:, tl * 24:(tl + 1) * 24]
            lxsl = lxt_s[:, tl * 24:(tl + 1) * 24]
            lgsl = lgt[:, tl * 24:(tl + 1) * 24]
            S.run("vector", lambda e, lysl=lysl, lxsl=lxsl, lgsl=lgsl, tl=tl:
                  e.scalar_tensor_tensor(
                      out=lgsl, in0=lysl, scalar=r2c[:, tl:tl + 1], in1=lxsl,
                      op0=ALU.mult, op1=ALU.add), "VS")
            S.wait("vector", "VS", S.cnt["VS"])
            S.run("vector", lambda e, lgsl=lgsl, tl=tl: e.scalar_tensor_tensor(
                out=lgsl, in0=wsfrep[:, :], scalar=nmu2r2c[:, tl:tl + 1],
                in1=lgsl, op0=ALU.mult, op1=ALU.add), "VS")
            S.wait("vector", "VS", S.cnt["VS"])
            S.run("vector", lambda e, lgsl=lgsl, tl=tl: e.scalar_tensor_tensor(
                out=lgsl, in0=wsfrep[:, :], scalar=nmupc[:, tl:tl + 1],
                in1=lgsl, op0=ALU.mult, op1=ALU.add), "VS")
            S.wait("vector", "VS", S.cnt["VS"])
            S.run("vector", lambda e, lgsl=lgsl, tl=tl: e.scalar_tensor_tensor(
                out=zt[:, tl * 16:(tl + 1) * 16], in0=lgsl[:, 0:16],
                scalar=alphatc[:, tl:tl + 1], in1=hrbrep[:, :],
                op0=ALU.mult, op1=ALU.add), "VS")
            S.run("vector", lambda e, lgsl=lgsl, tl=tl: e.scalar_tensor_tensor(
                out=spre[:, tl * 4:(tl + 1) * 4], in0=lgsl[:, 16:20],
                scalar=alphac[:, tl:tl + 1], in1=hpbrep[:, :],
                op0=ALU.mult, op1=ALU.add), "VS")
            S.run("vector", lambda e, lgsl=lgsl, tl=tl: e.scalar_tensor_tensor(
                out=spost[:, tl * 4:(tl + 1) * 4], in0=lgsl[:, 20:24],
                scalar=alphac[:, tl:tl + 1], in1=hobrep[:, :],
                op0=ALU.mult, op1=ALU.add), "VS")
            S.wait("vector", "VS", S.cnt["VS"])
        vs_zbuild = S.cnt["VS"]

        S.wait("vector", "VS", vs_zbuild)
        S.run("vector", lambda e: e.tensor_reduce(
            out=zm[:, :], in_=zt.rearrange("p (t k) -> p t k", k=16),
            axis=mybir.AxisListType.X, op=ALU.max), "VS")
        S.wait("vector", "VS", S.cnt["VS"])
        S.run("vector", lambda e: e.tensor_tensor(
            out=zt.rearrange("p (t k) -> p t k", k=16),
            in0=zt.rearrange("p (t k) -> p t k", k=16),
            in1=zm.rearrange("p (t o) -> p t o", o=1).broadcast_to([128, 8, 16]),
            op=ALU.subtract), "VS")
        vs_zshift = S.cnt["VS"]
        S.wait("vector", "VS", vs_zshift)

        zt_ij = zt.rearrange("p (t i j) -> p t i j", i=4, j=4)
        pe_ij = pe_.rearrange("p (t i j) -> p t i j", i=4, j=4)
        nu = eu
        nv = ev
        mx = rs
        sm_ = sb("sm_", [128, 32])
        S.run("vector", lambda e: e.memset(nv[:, :], 0.0), "VS")
        S.wait("vector", "VS", S.cnt["VS"])

        def half_step(bc_in, bc_dims, red_view, out_col):
            S.run("vector", lambda e: e.tensor_tensor(
                out=pe_ij, in0=zt_ij, in1=bc_in, op=ALU.subtract), "VS")
            S.wait("vector", "VS", S.cnt["VS"])
            S.run("vector", lambda e: e.tensor_reduce(
                out=mx[:, :], in_=red_view(pe_), axis=mybir.AxisListType.X,
                op=ALU.max), "VS")
            S.wait("vector", "VS", S.cnt["VS"])
            S.run("vector", lambda e: e.tensor_tensor(
                out=pe_ij, in0=pe_ij, in1=bc_dims(mx), op=ALU.subtract), "VS")
            vs_sub = S.cnt["VS"]
            S.wait("vector", "VS", vs_sub)
            S.wait("scalar", "VS", vs_sub)
            S.run("scalar", lambda e: e.activation(
                out=pe_[:, :], in_=pe_[:, :], func=AF.Exp), "SS")
            ss_e = S.cnt["SS"]
            S.wait("scalar", "SS", ss_e)
            S.wait("vector", "SS", ss_e)
            S.run("vector", lambda e: e.tensor_reduce(
                out=sm_[:, :], in_=red_view(pe_), axis=mybir.AxisListType.X,
                op=ALU.add), "VS")
            vs_s = S.cnt["VS"]
            S.wait("vector", "VS", vs_s)
            S.wait("scalar", "VS", vs_s)
            S.run("scalar", lambda e: e.activation(
                out=sm_[:, :], in_=sm_[:, :], func=AF.Ln), "SS")
            ss_l = S.cnt["SS"]
            S.wait("vector", "SS", ss_l)
            S.run("vector", lambda e: e.tensor_tensor(
                out=out_col[:, :], in0=mx[:, :], in1=sm_[:, :], op=ALU.add), "VS")
            S.wait("vector", "VS", S.cnt["VS"])

        bc_j_of = lambda col: col.rearrange(
            "p (t o j) -> p t o j", o=1, j=4).broadcast_to([128, 8, 4, 4])
        bc_i_of = lambda col: col.rearrange(
            "p (t i o) -> p t i o", o=1, i=4).broadcast_to([128, 8, 4, 4])
        bc_ti = lambda col: col.rearrange(
            "p (t i o) -> p t i o", o=1, i=4).broadcast_to([128, 8, 4, 4])
        def bc_tj(col):
            return col.rearrange("p (t j) -> p t j", j=4) \
                .rearrange("p (t o) j -> p t o j", o=1) \
                .broadcast_to([128, 8, 4, 4])
        red_j = lambda buf: buf.rearrange("p (ti j) -> p ti j", j=4)
        red_i = lambda buf: buf.rearrange("p (t i j) -> p t j i", i=4, j=4)

        for it in range(50):
            half_step(bc_tj(nv), bc_ti, red_j, nu)
            half_step(bc_ti(nu), bc_tj, red_i, nv)
        S.run("vector", lambda e: e.tensor_tensor(
            out=pe_ij, in0=zt_ij, in1=bc_ti(nu), op=ALU.subtract), "VS")
        S.wait("vector", "VS", S.cnt["VS"])
        S.run("vector", lambda e: e.tensor_tensor(
            out=pe_ij, in0=pe_ij, in1=bc_tj(nv), op=ALU.subtract), "VS")
        vs_parg = S.cnt["VS"]
        S.wait("scalar", "VS", vs_parg)
        S.run("scalar", lambda e: e.activation(
            out=pe_[:, :], in_=pe_[:, :], func=AF.Exp), "SS")
        ss_P = S.cnt["SS"]
        S.wait("vector", "SS", ss_P)
        vs_P = S.cnt["VS"]
        S.wait("scalar", "GD", gd_once)
        S.run("scalar", lambda e: e.activation(
            out=gcol[:, :], in_=gcol[:, :], func=AF.Sigmoid), "SS")
        ss_g = S.cnt["SS"]
        S.wait("vector", "SS", ss_g)
        S.run("vector", lambda e: e.tensor_scalar(
            out=omgc[:, :], in0=gcol[:, :], scalar1=-1.0, scalar2=1.0,
            op0=ALU.mult, op1=ALU.add), "VS")
        S.wait("vector", "VS", S.cnt["VS"])
        S.run("vector", lambda e: e.tensor_scalar(
            out=igrep[:, :], in0=eyerep[:, :], scalar1=omgc[:, 0:1], scalar2=None,
            op0=ALU.mult), "VS")
        S.wait("vector", "VS", S.cnt["VS"])
        S.run("vector", lambda e: e.scalar_tensor_tensor(
            out=ht.rearrange("p (t k) -> p t k", k=16),
            in0=pe_.rearrange("p (t k) -> p t k", k=16),
            scalar=gcol[:, 0:1],
            in1=igrep.rearrange("p (o k) -> p o k", o=1).broadcast_to([128, 8, 16]),
            op0=ALU.mult, op1=ALU.add), "VS")
        vs_H = S.cnt["VS"]
        S.wait("vector", "VS", vs_H)

        for (src, dst) in ((spre, hpre), (spost, hpost)):
            S.run("vector", lambda e, src=src: e.tensor_reduce(
                out=zm[:, :], in_=src.rearrange("p (t k) -> p t k", k=4),
                axis=mybir.AxisListType.X, op=ALU.max), "VS")
            S.wait("vector", "VS", S.cnt["VS"])
            S.run("vector", lambda e, src=src: e.tensor_tensor(
                out=src.rearrange("p (t k) -> p t k", k=4),
                in0=src.rearrange("p (t k) -> p t k", k=4),
                in1=zm.rearrange("p (t o) -> p t o", o=1).broadcast_to([128, 8, 4]),
                op=ALU.subtract), "VS")
            vs_sm = S.cnt["VS"]
            S.wait("scalar", "VS", vs_sm)
            S.run("scalar", lambda e, src=src: e.activation(
                out=src[:, :], in_=src[:, :], func=AF.Exp), "SS")
            ss_sm = S.cnt["SS"]
            S.wait("vector", "SS", ss_sm)
            S.run("vector", lambda e, src=src: e.tensor_reduce(
                out=zm[:, :], in_=src.rearrange("p (t k) -> p t k", k=4),
                axis=mybir.AxisListType.X, op=ALU.add), "VS")
            S.wait("vector", "VS", S.cnt["VS"])
            S.run("vector", lambda e: e.reciprocal(out=zm[:, :], in_=zm[:, :]), "VS")
            S.wait("vector", "VS", S.cnt["VS"])
            S.run("vector", lambda e, src=src, dst=dst: e.tensor_tensor(
                out=dst.rearrange("p (t k) -> p t k", k=4),
                in0=src.rearrange("p (t k) -> p t k", k=4),
                in1=zm.rearrange("p (t o) -> p t o", o=1).broadcast_to([128, 8, 4]),
                op=ALU.mult), "VS")
            S.wait("vector", "VS", S.cnt["VS"])
        vs_hs = S.cnt["VS"]

        S.wait("gpsimd", "VS", vs_hs)
        S.run("gpsimd", lambda e: e.dma_start(out=o_dbg[:, 0:192], in_=lgt[:, :]), "GD", 16)
        S.run("gpsimd", lambda e: e.dma_start(out=o_dbg[:, 192:320], in_=ht[:, :]), "GD", 16)
        S.run("gpsimd", lambda e: e.dma_start(out=o_dbg[:, 320:352], in_=hpre[:, :]), "GD", 16)
        S.run("gpsimd", lambda e: e.dma_start(out=o_dbg[:, 352:384], in_=hpost[:, :]), "GD", 16)
        S.run("gpsimd", lambda e: e.dma_start(out=o_dbg[:, 384:392], in_=mupc[:, :]), "GD", 16)
        S.run("gpsimd", lambda e: e.dma_start(out=o_dbg[:, 392:400], in_=varpc[:, :]), "GD", 16)
        S.run("gpsimd", lambda e: e.dma_start(out=o_dbg[:, 400:408], in_=alphac[:, :]), "GD", 16)
        S.run("gpsimd", lambda e: e.dma_start(out=o_dbg[:, 408:416], in_=r2c[:, :]), "GD", 16)
        S.run("gpsimd", lambda e: e.dma_start(out=o_dbg[:, 416:424], in_=mu2c[:, :]), "GD", 16)

        ldB = [0] * NTT
        vsB = [0] * NTT

        def loadB(tl, wait_vs):
            S.wait("sync", "VS", wait_vs)
            S.run("sync", lambda e, tl=tl: e.dma_start(
                out=xt[tl], in_=xtok[tl * 128:(tl + 1) * 128, :]), "LD", 16)
            S.run("sync", lambda e, tl=tl: e.dma_start(
                out=yt[tl], in_=y2tok[tl * 128:(tl + 1) * 128, :]), "LD", 16)
            ldB[tl] = S.cnt["LD"]

        loadB(0, vsA[NTT - 1])
        loadB(1, vsA[NTT - 1])
        ts_tr_prev = 0
        vs_xpT_evac = 0
        vs_yb_evac = 0
        gd_x3 = 0
        for tl in range(NTT):
            S.wait("vector", "LD", ldB[tl])
            S.wait("vector", "VS", vs_hs)
            S.wait("vector", "GD", gd_x3)
            S.run("vector", lambda e, tl=tl: e.tensor_scalar(
                out=x2t[:, :], in0=yt[tl], scalar1=r2c[:, tl:tl + 1],
                scalar2=nmu2r2c[:, tl:tl + 1], op0=ALU.mult, op1=ALU.add), "VS")
            S.wait("vector", "VS", S.cnt["VS"])
            S.run("vector", lambda e, tl=tl: e.tensor_tensor(
                out=x2t[:, :], in0=x2t[:, :], in1=xt[tl], op=ALU.add), "VS")
            vsB[tl] = S.cnt["VS"]
            S.wait("vector", "VS", vsB[tl])
            if tl + 2 < NTT:
                loadB(tl + 2, vsB[tl])
            S.run("vector", lambda e, tl=tl: e.tensor_scalar(
                out=xint[:, :], in0=x2t[:, :], scalar1=nmupc[:, tl:tl + 1],
                scalar2=rpc[:, tl:tl + 1], op0=ALU.add, op1=ALU.mult), "VS")
            S.wait("vector", "VS", S.cnt["VS"])
            for i in range(4):
                S.run("vector", lambda e, tl=tl, i=i: e.tensor_scalar(
                    out=xmix[:, i * 1024:(i + 1) * 1024],
                    in0=xint[:, 0:1024],
                    scalar1=ht[:, tl * 16 + i * 4:tl * 16 + i * 4 + 1],
                    scalar2=None, op0=ALU.mult), "VS")
            S.wait("vector", "VS", S.cnt["VS"])
            for j in range(1, 4):
                for i in range(4):
                    S.run("vector", lambda e, tl=tl, i=i, j=j: e.scalar_tensor_tensor(
                        out=xmix[:, i * 1024:(i + 1) * 1024],
                        in0=xint[:, j * 1024:(j + 1) * 1024],
                        scalar=ht[:, tl * 16 + i * 4 + j:tl * 16 + i * 4 + j + 1],
                        in1=xmix[:, i * 1024:(i + 1) * 1024],
                        op0=ALU.mult, op1=ALU.add), "VS")
                S.wait("vector", "VS", S.cnt["VS"])
            S.run("vector", lambda e, tl=tl: e.tensor_scalar(
                out=xpre32[:, :], in0=xint[:, 0:1024],
                scalar1=hpre[:, tl * 4:tl * 4 + 1], scalar2=None,
                op0=ALU.mult), "VS")
            S.wait("vector", "VS", S.cnt["VS"])
            for n in range(1, 4):
                S.run("vector", lambda e, tl=tl, n=n: e.scalar_tensor_tensor(
                    out=xpre32[:, :], in0=xint[:, n * 1024:(n + 1) * 1024],
                    scalar=hpre[:, tl * 4 + n:tl * 4 + n + 1], in1=xpre32[:, :],
                    op0=ALU.mult, op1=ALU.add), "VS")
                S.wait("vector", "VS", S.cnt["VS"])
            S.wait("vector", "TS", ts_tr_prev)
            S.run("vector", lambda e: e.tensor_copy(
                out=xpreb[:, :], in_=xpre32[:, :]), "VS")
            vs_xpreb = S.cnt["VS"]
            S.wait("vector", "VS", vs_xpreb)
            vs_ev = vs_xpT_evac
            for k in range(8):
                S.wait("tensor", "VS", vs_xpreb if k == 0 else vs_ev)
                S.run("tensor", lambda e, k=k: e.transpose(
                    ptb[:, :], xpreb[:, k * 128:(k + 1) * 128], idn[:, :]), "TS")
                ts_tr = S.cnt["TS"]
                S.wait("vector", "TS", ts_tr)
                S.run("vector", lambda e, k=k: e.tensor_copy(
                    out=xpT[:, k * 128:(k + 1) * 128], in_=ptb[:, :]), "VS")
                vs_ev = S.cnt["VS"]
                S.wait("vector", "VS", vs_ev)
            ts_tr_prev = S.cnt["TS"]
            vs_xpT_evac = vs_ev
            S.wait("tensor", "VS", vs_xpT_evac)
            S.wait("tensor", "GD", gd_once)
            S.wait("tensor", "VS", vs_yb_evac)
            for h in range(2):
                for k in range(8):
                    S.run("tensor", lambda e, k=k, h=h: e.matmul(
                        py[h][:, :], xpT[:, k * 128:(k + 1) * 128],
                        wbb[k][:, h * 512:(h + 1) * 512],
                        start=(k == 0), stop=(k == 7)),
                        "TS" if k == 7 else None)
            ts_mm = S.cnt["TS"]
            S.wait("vector", "TS", ts_mm)
            for h in range(2):
                S.run("vector", lambda e, h=h: e.tensor_copy(
                    out=yb[:, h * 512:(h + 1) * 512], in_=py[h][:, :]), "VS")
            S.wait("vector", "VS", S.cnt["VS"])
            vs_yb_evac = S.cnt["VS"]
            for n in range(4):
                S.run("vector", lambda e, tl=tl, n=n: e.scalar_tensor_tensor(
                    out=xmix[:, n * 1024:(n + 1) * 1024], in0=yb[:, :],
                    scalar=hpost[:, tl * 4 + n:tl * 4 + n + 1],
                    in1=xmix[:, n * 1024:(n + 1) * 1024],
                    op0=ALU.mult, op1=ALU.add), "VS")
            vs_ffn = S.cnt["VS"]
            S.wait("vector", "VS", vs_ffn)
            S.wait("scalar", "VS", vs_ffn)
            S.run("scalar", lambda e: e.activation(
                out=scrb[:, :], in_=xmix[:, :], func=AF.Copy, bias=0.0,
                accum_out=s3c[:, 0:1]), "SS")
            S.wait("scalar", "SS", S.cnt["SS"])
            S.run("scalar", lambda e: e.activation(
                out=scrb[:, :], in_=xmix[:, :], func=AF.Square,
                accum_out=q3c[:, 0:1]), "SS")
            ss_ln3 = S.cnt["SS"]
            S.wait("vector", "SS", ss_ln3)
            S.run("vector", lambda e: e.tensor_scalar(
                out=mu3c[:, :], in0=s3c[:, :], scalar1=1.0 / D, scalar2=None,
                op0=ALU.mult), "VS")
            S.run("vector", lambda e: e.tensor_scalar(
                out=t1c[:, :], in0=q3c[:, :], scalar1=1.0 / D, scalar2=None,
                op0=ALU.mult), "VS")
            S.wait("vector", "VS", S.cnt["VS"])
            S.run("vector", lambda e: e.tensor_tensor(
                out=r3c[:, :], in0=mu3c[:, :], in1=mu3c[:, :], op=ALU.mult), "VS")
            S.wait("vector", "VS", S.cnt["VS"])
            S.run("vector", lambda e: e.tensor_tensor(
                out=t1c[:, :], in0=t1c[:, :], in1=r3c[:, :], op=ALU.subtract), "VS")
            vs_v3 = S.cnt["VS"]
            S.wait("scalar", "VS", vs_v3)
            S.run("scalar", lambda e: e.activation(
                out=t1c[:, :], in_=t1c[:, :], func=AF.Sqrt, bias=LN_EPS), "SS")
            ss_sd3 = S.cnt["SS"]
            S.wait("vector", "SS", ss_sd3)
            S.run("vector", lambda e: e.reciprocal(out=r3c[:, :], in_=t1c[:, :]), "VS")
            S.wait("vector", "VS", S.cnt["VS"])
            S.run("vector", lambda e: e.tensor_tensor(
                out=nmu3r3c[:, :], in0=mu3c[:, :], in1=r3c[:, :], op=ALU.mult), "VS")
            S.wait("vector", "VS", S.cnt["VS"])
            S.run("vector", lambda e: e.tensor_scalar(
                out=nmu3r3c[:, :], in0=nmu3r3c[:, :], scalar1=-1.0, scalar2=None,
                op0=ALU.mult), "VS")
            S.wait("vector", "VS", S.cnt["VS"])
            S.wait("vector", "GD", gd_x3)
            S.run("vector", lambda e: e.tensor_scalar(
                out=xmix[:, :], in0=xmix[:, :], scalar1=r3c[:, 0:1],
                scalar2=nmu3r3c[:, 0:1], op0=ALU.mult, op1=ALU.add), "VS")
            S.wait("vector", "VS", S.cnt["VS"])
            S.run("vector", lambda e: e.tensor_tensor(
                out=x2t[:, :], in0=x2t[:, :], in1=xmix[:, :], op=ALU.add), "VS")
            vs_x3 = S.cnt["VS"]
            S.wait("gpsimd", "VS", vs_x3)
            S.run("gpsimd", lambda e, tl=tl: e.dma_start(
                out=o_x3[tl * 128:(tl + 1) * 128, :], in_=x2t[:, :]), "GD", 16)
            gd_x3 = S.cnt["GD"]
        S.wait("sync", "GD", S.cnt["GD"])

        S.emit(block, sems)
    return nc


_L3 = None


def _get_l3():
    global _L3
    if _L3 is None:
        _L3 = build_l3()
    return _L3


def run_l3(x2d, y2_full, lnp_parts, ly_parts, lx_parts, wsumf, hres_base,
           hpre_base, hpost_base, gate, wbr):
    nc = _get_l3()
    eye = np.eye(4, dtype=np.float32).reshape(1, 16)
    idn = np.eye(128, dtype=np.float32)
    lnp_st = np.stack(lnp_parts, axis=-1)
    ly_st = np.stack(ly_parts, axis=-1)
    in_maps = []
    for c in range(NCORES):
        tsl = slice(c * TOK, (c + 1) * TOK)
        lp = lnp_st[:, tsl, :].reshape(2, 8, 128, 8).transpose(2, 1, 0, 3) \
            .reshape(128, 16 * 8)
        lyp = ly_st[:, tsl, :].reshape(24, 8, 128, 8).transpose(2, 1, 0, 3) \
            .reshape(128, 192 * 8)
        lxp = lx_parts[c].reshape(24, 8, 128).transpose(2, 1, 0).reshape(128, 192)
        in_maps.append(dict(
            xtok=np.ascontiguousarray(x2d[tsl]),
            y2tok=np.ascontiguousarray(y2_full[:, tsl].T),
            lnps=np.ascontiguousarray(lp),
            lyt=np.ascontiguousarray(lyp),
            lxt=np.ascontiguousarray(lxp),
            wsumf=wsumf.reshape(1, 24),
            hrb=hres_base.reshape(1, 16),
            hpb=hpre_base.reshape(1, 4),
            hob=hpost_base.reshape(1, 4),
            gatei=gate.reshape(1, 1),
            eye16=eye,
            ident=idn,
            wb=wbr,
        ))
    res = run_bass_kernel_spmd(nc, in_maps, core_ids=list(range(NCORES)))
    return res.results


def kernel(L0_attn_x, L0_attn_A_log, L0_attn_D, L0_attn_x_proj_w,
           L0_attn_dt_proj_w, L0_attn_dt_proj_b, L0_ffn_W_res, L0_ffn_W_pre,
           L0_ffn_W_post, L0_ffn_H_res_base, L0_ffn_H_pre_base,
           L0_ffn_H_post_base, L0_ffn_gate, L0_ffn_W_branch):
    f32 = np.float32
    x2d = np.ascontiguousarray(np.asarray(L0_attn_x, f32).reshape(TT, D))
    xpw = np.asarray(L0_attn_x_proj_w, f32)
    wcat = np.ascontiguousarray(np.concatenate(
        [np.asarray(L0_ffn_W_res, f32), np.asarray(L0_ffn_W_pre, f32),
         np.asarray(L0_ffn_W_post, f32)], axis=0))

    r1 = run_l1(x2d, xpw, wcat)
    xp_full = np.ascontiguousarray(
        np.concatenate([r["xpraw"] for r in r1], axis=1))
    mean = np.concatenate([r["mcol"].T.reshape(-1) for r in r1])
    rstd = np.concatenate([r["rcol"].T.reshape(-1) for r in r1])
    mur = np.concatenate([r["mrcol"].T.reshape(-1) for r in r1])
    st1 = np.ascontiguousarray(np.stack([mean, rstd, mur]).astype(f32))
    wsum = r1[0]["wsums"][0, :R].astype(f32)
    wsumf = r1[0]["wsums"][0, R:].astype(f32)
    lx_parts = [r["lx"] for r in r1]

    r2 = run_l2(x2d, xp_full, st1, np.asarray(L0_attn_A_log, f32),
                np.asarray(L0_attn_D, f32), np.asarray(L0_attn_dt_proj_w, f32),
                np.asarray(L0_attn_dt_proj_b, f32), wcat, wsum)
    y2_full = np.concatenate([r["y2"] for r in r2], axis=0)
    lnp_parts = [r["lnp"] for r in r2]
    ly_parts = [r["ly"] for r in r2]

    r3 = run_l3(x2d, y2_full, lnp_parts, ly_parts, lx_parts, wsumf,
                np.asarray(L0_ffn_H_res_base, f32),
                np.asarray(L0_ffn_H_pre_base, f32),
                np.asarray(L0_ffn_H_post_base, f32),
                np.asarray(L0_ffn_gate, f32),
                np.ascontiguousarray(np.asarray(L0_ffn_W_branch, f32)))
    out = np.concatenate([r["x3"] for r in r3], axis=0)
    return out.reshape(B, L, D)
